# version 05
# baseline (speedup 1.0000x reference)
import math
import numpy as np
import concourse.bass as bass
import concourse.mybir as mybir
from concourse.alu_op_type import AluOpType
from concourse.bass_utils import run_bass_kernel_spmd

F32 = mybir.dt.float32
BF16 = mybir.dt.bfloat16
AF = mybir.ActivationFunctionType
AX = mybir.AxisListType.X
OP = AluOpType

RNE = 1.5 * 2 ** 23
D = 1024
NCORES = 8
LAYERS = 4
SKIP = set()
DBG = None

ENGS = ("tensor", "vector", "scalar", "gpsimd", "sync")


class MSem:
    def __init__(self, handle, name):
        self.h = handle
        self.name = name
        self.n = 0


class MK:
    def __init__(self, nc, T, layers, stack):
        self.nc = nc
        self.T = T
        self.layers = layers
        self.stack = stack
        self.NT = T // 512
        self.NT128 = T // 128
        self.FC = D // 128
        self.q = {e: [] for e in ENGS}
        self.dram = {}
        self.cc_idx = 0
        self._flushed = {"sync": -1, "gpsimd": -1}

    def sem(self, name):
        h = self.stack.enter_context(self.nc.semaphore(name))
        return MSem(h, name)

    def wait(self, eng, ev):
        if ev is None:
            return
        s, v = ev
        self.q[eng].append(("w", s.h, v))

    def op(self, eng, fn, inc=None, inck=1):
        incs = []
        ev = None
        if inc is not None:
            inc.n += inck
            incs.append((inc.h, inck))
            ev = (inc, inc.n)
        self.q[eng].append(("i", fn, incs))
        return ev

    def drain(self, eng):
        self.q[eng].append(("i", lambda e: e.drain(), []))

    def dma(self, eng, out, in_):
        sem = self.sD if eng == "sync" else self.sG
        return self.op(eng, lambda e, o=out, i=in_: e.dma_start(out=o, in_=i),
                       inc=sem, inck=16)

    def dump(self, name, region, cols):
        if DBG == name:
            self.barrier()
            self.dma("sync", self.dbg[:, 0:cols], region[:, 0:cols])
            self.barrier()

    def dump_x(self, name):
        if DBG == name:
            self.barrier()
            T = self.T
            for c in range(self.FC):
                self.op("vector", lambda v, c=c: v.tensor_copy(
                    self.ARB[:, 0:T] if False else self.dbgbf[:, c * T:(c + 1) * T],
                    self.X[:, c * T:(c + 1) * T]))
                self.drain("vector")
            self.barrier()
            self.dma("sync", self.dbg[:, 0:self.FC * T], self.dbgbf[:, 0:self.FC * T])
            self.barrier()

    def dma_wait(self, waiter, queue_eng):
        sem = self.sD if queue_eng == "sync" else self.sG
        if self._flushed[queue_eng] != sem.n:
            self.wait(queue_eng, (sem, sem.n))
            self._flushed[queue_eng] = sem.n
        if waiter != queue_eng:
            self.wait(waiter, (sem, sem.n))

    def fence(self, waiter, producer):
        ev = self.op(producer, lambda e: e.drain(), inc=self.esem[producer])
        self.wait(waiter, ev)

    def barrier(self, engs=ENGS):
        if "sync" in engs:
            self.dma_wait("sync", "sync")
        if "gpsimd" in engs:
            self.dma_wait("gpsimd", "gpsimd")
        evs = []
        for e in engs:
            evs.append(self.op(e, lambda en: en.drain(), inc=self.esem[e]))
        for e in engs:
            for ev in evs:
                if ev[0] is not self.esem[e]:
                    self.wait(e, ev)
            for ds in (self.sD, self.sG):
                self.wait(e, (ds, ds.n))

    def replay(self, block):
        for e in ENGS:
            items = self.q[e]

            def body(eng, items=items):
                for it in items:
                    if it[0] == "w":
                        eng.wait_ge(it[1], it[2])
                    else:
                        ins = it[1](eng)
                        for (h, k) in it[2]:
                            ins.then_inc(h, k)

            getattr(block, e)(body)

    def din(self, name, shape, dtype=F32):
        self.dram[name] = self.nc.dram_tensor(name, shape, dtype, kind="ExternalInput").ap()
        return self.dram[name]

    def declare(self):
        nc, T = self.nc, self.T
        self.xT = self.din("xT", [D, T])
        self.yT = nc.dram_tensor("yT", [D, T], F32, kind="ExternalOutput").ap()
        self.dbg = nc.dram_tensor("dbg", [128, 40960], BF16, kind="ExternalOutput").ap() if DBG else None

        self.din("L0WqT", [D, D]); self.din("L0WkT", [D, D])
        self.din("L0WvT", [D, D]); self.din("L0WoT", [D, D])
        self.din("L0nrm", [1, 16])
        self.din("L0wg", [3072, D]); self.din("L0wu", [3072, D]); self.din("L0wd", [D, 3072])
        self.din("L1Wq", [D, D]); self.din("L1Wk", [D, 512])
        self.din("L1Wv", [D, 512]); self.din("L1Wo", [D, D]); self.din("L1bo", [128, 8])
        self.din("L1wg", [2816, D]); self.din("L1wu", [2816, D]); self.din("L1wd", [D, 2816])
        self.din("L2qkvT", [D, 3072]); self.din("L2projT", [D, D]); self.din("L2pb", [128, 8])
        self.din("L2ker", [1, 12]); self.din("L2gam", [128, 8]); self.din("L2phs", [128, 8])
        self.din("L2thf", [1, 1])
        self.din("L2th", [D, D]); self.din("L2ph", [D, D])
        self.din("L2thn", [D, D]); self.din("L2phn", [D, D])
        self.din("L3td", [128, 8]); self.din("L3tf", [128, 8])
        self.din("L3tmk", [128, 8]); self.din("L3tmv", [128, 8]); self.din("L3tmr", [128, 8])
        self.din("L3WkT", [D, D]); self.din("L3WvT", [D, D])
        self.din("L3WrT", [D, D]); self.din("L3WoT", [D, D])
        self.din("L3wg", [3072, D]); self.din("L3wu", [3072, D]); self.din("L3wd", [D, 3072])
        self.din("ident", [128, 128], BF16)
        self.din("identf", [128, 128], F32)
        self.din("ones_bf", [128, 128], BF16)
        self.din("ones_f", [1, 128], F32)
        self.din("triadd", [128, 128], F32)
        self.din("l1rope", [2, 128, T], BF16)
        self.din("l2rope", [2, 128, T], BF16)
        self.din("trampf", [128, T], F32)

        WSP = max(T, 1024)
        self.sp_gate = nc.dram_tensor("sp_gate", [3072, WSP], BF16).ap()
        self.sp_up = nc.dram_tensor("sp_up", [3072, WSP], BF16).ap()
        self.sp_g2 = nc.dram_tensor("sp_g2", [3072, WSP], BF16).ap()
        self.sp_o = nc.dram_tensor("sp_o", [3072, WSP], BF16).ap()
        self.sp_attn = nc.dram_tensor("sp_attn", [D, T], BF16).ap()
        self.cc_in = [nc.dram_tensor(f"cc_in{i}", [1, 2], F32).ap() for i in range(8)]
        self.cc_out = [nc.dram_tensor(f"cc_out{i}", [1, 2], F32).ap() for i in range(8)]

        sb = lambda n, s, d=BF16: self.stack.enter_context(nc.sbuf_tensor(n, s, d))
        ps = lambda n, s, d=F32: self.stack.enter_context(nc.psum_tensor(n, s, d))

        self.X = sb("X", [128, self.FC * T], F32)
        self.XN = sb("XN", [128, self.FC * T], BF16)
        self.ARB = sb("ARB", [128, 40960], BF16)
        self.ARF = sb("ARF", [128, 6144], F32)
        self.CONST = sb("CONST", [128, 256], BF16)
        self.ONESF = sb("ONESF", [1, 128], F32)
        self.TRIADD = sb("TRIADD", [128, 128], F32)
        self.IDENTF = sb("IDENTF", [128, 128], F32)
        self.STATS = sb("STATS", [128, 256], F32)
        self.GMAX = sb("GMAX", [128, 32], F32)
        self.CCOL = sb("CCOL", [128, 8], F32)

        self.dbgbf = self.ARB[:, 0:self.FC * T]
        self.ident = self.CONST[:, 0:128]
        self.ones_bf = self.CONST[:, 128:256]

        self.pb = [ps(f"pb{i}", [128, 512], F32) for i in range(8)]
        self.pt0 = self.pb[6].bitcast(BF16)
        self.pt1 = self.pb[7].bitcast(BF16)

        self.esem = {e: self.sem(f"S_{e}") for e in ENGS}
        self.sD = self.sem("S_dma_sync")
        self.sG = self.sem("S_dma_gps")
        self.sT = self.esem["tensor"]
        self.sV = self.esem["vector"]
        self.sS = self.esem["scalar"]
        self.sP = self.esem["gpsimd"]
        self.sCC = self.sem("S_cc")

    def init_loads(self):
        T = self.T
        for c in range(self.FC):
            self.dma("sync", self.X[:, c * T:(c + 1) * T], self.xT[c * 128:(c + 1) * 128, :])
        self.dma("sync", self.ident, self.dram["ident"][:, :])
        self.dma("sync", self.ones_bf, self.dram["ones_bf"][:, :])
        self.dma("sync", self.ONESF[:, :], self.dram["ones_f"][:, :])
        self.dma("sync", self.TRIADD[:, :], self.dram["triadd"][:, :])
        self.dma("sync", self.IDENTF[:, :], self.dram["identf"][:, :])
        self.op("vector", lambda v: v.memset(self.CCOL[:, 0:1], 1e-5))
        self.op("vector", lambda v: v.memset(self.CCOL[:, 1:2], math.pi / 2))
        self.op("vector", lambda v: v.memset(self.CCOL[:, 2:3], 0.0))
        self.op("vector", lambda v: v.memset(self.CCOL[:, 3:4], -math.pi))
        self.drain("vector")
        self.barrier()

    def replicate_cols(self, src_row_ap, n, dst_col):
        pbank = self.pb[5]
        self.fence("tensor", "vector")
        evm = self.op("tensor", lambda t, s=src_row_ap, n=n, p=pbank: t.matmul(
            p[:, 0:n], self.ONESF[:, :], s, start=True, stop=True), inc=self.sT)
        self.wait("vector", evm)
        evv = self.op("vector", lambda v, n=n, d=dst_col, p=pbank: v.tensor_copy(
            self.GMAX[:, d:d + n], p[:, 0:n]), inc=self.sV)
        self.wait("tensor", evv)
        return evv

    def col_max_to_scalar(self, col_ap, dst_col):
        pbank = self.pb[5]
        self.fence("tensor", "vector")
        ev = self.op("tensor", lambda t, c=col_ap, p=pbank: t.transpose(
            p[0:1, 0:128], c, self.IDENTF[:, :]), inc=self.sT)
        self.wait("vector", ev)
        evv = self.op("vector", lambda v, d=dst_col, p=pbank: v.tensor_reduce(
            out=self.STATS[0:1, d:d + 1], in_=p[0:1, 0:128], axis=AX, op=OP.max),
            inc=self.sV)
        self.wait("tensor", evv)
        return evv

    def part_tree_max(self, cols, dst_col):
        c0 = cols[0]
        for c in cols[1:]:
            self.op("vector", lambda v, a=c0, b=c: v.tensor_tensor(
                out=self.STATS[:, a:a + 1], in0=self.STATS[:, a:a + 1],
                in1=self.STATS[:, b:b + 1], op=OP.max))
            self.drain("vector")
        return self.col_max_to_scalar(self.STATS[:, c0:c0 + 1], dst_col)

    def collective_max(self, src_col, n_vals, dst_gmax_col):
        i = self.cc_idx
        self.cc_idx += 1
        ci, co = self.cc_in[i], self.cc_out[i]
        self.fence("gpsimd", "vector")
        self.dma("gpsimd", ci[0:1, 0:n_vals], self.STATS[0:1, src_col:src_col + n_vals])
        self.dma_wait("gpsimd", "gpsimd")
        evc = self.op("gpsimd", lambda g, ci=ci, co=co, n=n_vals: g.collective_compute(
            "AllReduce", OP.max, replica_groups=[list(range(NCORES))],
            ins=[ci[0:1, 0:n].opt()], outs=[co[0:1, 0:n].opt()]), inc=self.sCC)
        self.wait("gpsimd", evc)
        self.dma("gpsimd", self.STATS[0:1, 200:200 + n_vals], co[0:1, 0:n_vals])
        self.dma_wait("vector", "gpsimd")
        self.dma_wait("tensor", "gpsimd")
        return self.replicate_cols(self.STATS[0:1, 200:200 + n_vals], n_vals, dst_gmax_col)

    def make_inv_scale(self, raw_col, inv_col, scale_col, qmax):
        g = self.GMAX
        self.drain("vector")
        self.op("vector", lambda v: v.tensor_scalar(
            out=g[:, scale_col:scale_col + 1], in0=g[:, raw_col:raw_col + 1],
            scalar1=1.0 / qmax, scalar2=1e-12, op0=OP.mult, op1=OP.max))
        self.drain("vector")
        ev = self.op("vector", lambda v: v.reciprocal(
            out=g[:, inv_col:inv_col + 1], in_=g[:, scale_col:scale_col + 1]),
            inc=self.sV)
        self.drain("vector")
        return ev

    def layernorm(self):
        T, NT, FC = self.T, self.NT, self.FC
        xb = self.ARB[:, 0:512]
        xsq = self.ARB[:, 512:1024]
        mean = self.ARF[:, 0:512]
        var = self.ARF[:, 512:1024]
        rstd = self.ARF[:, 1024:1536]
        self.barrier()
        for tc in range(NT):
            pm, pq = self.pb[0], self.pb[1]
            self.fence("tensor", "vector")
            last_mm = None
            for c in range(FC):
                xsl = slice(c * T + tc * 512, c * T + (tc + 1) * 512)
                e1 = self.op("vector", lambda v, a=xsl: v.tensor_copy(xb[:], self.X[:, a]), inc=self.sV)
                e2 = self.op("scalar", lambda s, a=xsl: s.activation(
                    out=xsq[:], in_=self.X[:, a], func=AF.Square), inc=self.sS)
                self.wait("tensor", e1)
                self.wait("tensor", e2)
                self.op("tensor", lambda t, c0=c: t.matmul(
                    pm[:, 0:512], self.ones_bf, xb[:], start=(c0 == 0), stop=(c0 == FC - 1)))
                last_mm = self.op("tensor", lambda t, c0=c: t.matmul(
                    pq[:, 0:512], self.ones_bf, xsq[:], start=(c0 == 0), stop=(c0 == FC - 1)),
                    inc=self.sT)
                self.wait("vector", last_mm)
                self.wait("scalar", last_mm)
            self.wait("vector", last_mm)
            self.op("vector", lambda v: v.tensor_scalar(
                out=mean, in0=pm[:, 0:512], scalar1=1.0 / D, scalar2=None, op0=OP.mult))
            self.drain("vector")
            self.op("vector", lambda v: v.tensor_tensor(out=var, in0=mean, in1=mean, op=OP.mult))
            self.drain("vector")
            ev = self.op("vector", lambda v: v.scalar_tensor_tensor(
                out=var, in0=pq[:, 0:512], scalar=1.0 / D, in1=var,
                op0=OP.mult, op1=OP.subtract), inc=self.sV)
            self.wait("scalar", ev)
            ev = self.op("scalar", lambda s: s.activation(
                out=rstd, in_=var, func=AF.Sqrt, bias=self.CCOL[:, 0:1]), inc=self.sS)
            self.wait("vector", ev)
            self.op("vector", lambda v: v.reciprocal(out=rstd, in_=rstd))
            self.drain("vector")
            for c in range(FC):
                xsl = slice(c * T + tc * 512, c * T + (tc + 1) * 512)
                self.op("vector", lambda v, a=xsl: v.tensor_tensor(
                    out=var, in0=self.X[:, a], in1=mean, op=OP.subtract))
                self.drain("vector")
                self.op("vector", lambda v, a=xsl: v.tensor_tensor(
                    out=self.XN[:, a], in0=var, in1=rstd, op=OP.mult))
                self.drain("vector")
        self.barrier()

    def load_cast_w(self, name, wslot, wstg, width=1024, scale=None):
        last = None
        for c in range(self.FC):
            self.dma("sync", wstg[:, 0:width], self.dram[name][c * 128:(c + 1) * 128, :])
            self.dma_wait("vector", "sync")
            last = self.op("vector", lambda v, c=c: v.tensor_scalar(
                out=wslot[:, c * width:(c + 1) * width], in0=wstg[:, 0:width],
                scalar1=(1.0 if scale is None else scale), scalar2=None,
                op0=OP.mult), inc=self.sV)
            self.wait("sync", last)
        return last

    def l0_attn(self):
        T, NT, NT128, FC = self.T, self.NT, self.NT128, self.FC
        A = self.ARB[:, 0:8 * T]
        B = self.ARB[:, 8 * T:16 * T]
        WSLOT = self.ARB[:, 16 * T:16 * T + 8192]
        KV = self.ARF.bitcast(BF16)[:, 8192:8704]
        WSTG = self.ARF[:, 0:1024]
        ROWF = self.ARF[:, 1024:2048]
        VTS = self.ARF.bitcast(BF16)

        def project_tm(dst, wname, l2norm):
            ev = self.load_cast_w(wname, WSLOT, WSTG)
            self.wait("tensor", ev)
            for tc in range(NT128):
                for dc in range(2):
                    pbk = self.pb[(tc * 2 + dc) % 2]
                    self.fence("tensor", "vector")
                    last = None
                    for i in range(FC):
                        last = self.op("tensor", lambda t, k=i, i0=i, tc=tc, dc=dc, pbk=pbk: t.matmul(
                            pbk[:, 0:512],
                            self.XN[:, k * T + tc * 128:k * T + tc * 128 + 128],
                            WSLOT[:, k * 1024 + dc * 512:k * 1024 + dc * 512 + 512],
                            start=(i0 == 0), stop=(i0 == FC - 1)),
                            inc=self.sT if i == FC - 1 else None)
                    self.wait("vector", last)
                    self.op("vector", lambda v, tc=tc, dc=dc, pbk=pbk: v.tensor_copy(
                        dst[:, tc * 1024 + dc * 512: tc * 1024 + dc * 512 + 512],
                        pbk[:, 0:512]), inc=self.sV)
                    self.drain("vector")
                if l2norm:
                    sl = slice(tc * 1024, (tc + 1) * 1024)
                    self.fence("scalar", "vector")
                    ev = self.op("scalar", lambda s, a=sl: s.activation(
                        out=ROWF[:, 0:1024], in_=dst[:, a], func=AF.Square), inc=self.sS)
                    self.wait("vector", ev)
                    self.op("vector", lambda v: v.tensor_reduce(
                        out=self.STATS[:, 64:80],
                        in_=ROWF[:, 0:1024].rearrange("p (h e) -> p h e", e=64),
                        axis=AX, op=OP.add))
                    self.drain("vector")
                    self.fence("scalar", "vector")
                    ev = self.op("scalar", lambda s: s.activation(
                        out=self.STATS[:, 80:96], in_=self.STATS[:, 64:80],
                        func=AF.Sqrt), inc=self.sS)
                    self.wait("vector", ev)
                    self.op("vector", lambda v: v.tensor_scalar(
                        out=self.STATS[:, 80:96], in0=self.STATS[:, 80:96],
                        scalar1=1e-12, scalar2=None, op0=OP.max))
                    self.drain("vector")
                    self.op("vector", lambda v: v.reciprocal(
                        out=self.STATS[:, 80:96], in_=self.STATS[:, 80:96]))
                    self.drain("vector")
                    self.op("vector", lambda v, a=sl: v.tensor_tensor(
                        out=dst[:, a].rearrange("p (h e) -> p h e", e=64),
                        in0=dst[:, a].rearrange("p (h e) -> p h e", e=64),
                        in1=self.STATS[:, 80:96].unsqueeze(2).to_broadcast([128, 16, 64]),
                        op=OP.mult), inc=self.sV)
                    self.drain("vector")

        self.dma("sync", self.STATS[0:1, 0:16], self.dram["L0nrm"][:, :])
        self.dma_wait("scalar", "sync")
        self.op("scalar", lambda s: s.activation(
            out=self.STATS[0:1, 16:32], in_=self.STATS[0:1, 0:16], func=AF.Sigmoid))
        self.drain("scalar")
        ev = self.op("scalar", lambda s: s.activation(
            out=self.STATS[0:1, 16:32], in_=self.STATS[0:1, 16:32], func=AF.Exp,
            scale=-math.log(float(self.T))), inc=self.sS)
        self.wait("vector", ev)
        self.wait("tensor", ev)
        self.replicate_cols(self.STATS[0:1, 16:32], 16, 0)

        project_tm(A, "L0WqT", l2norm=True)
        self.barrier(("tensor", "vector", "scalar", "sync"))
        for tc in range(NT128):
            for dc in range(FC):
                pbt = self.pt0 if (tc * FC + dc) % 2 == 0 else self.pt1
                self.fence("tensor", "vector")
                ev = self.op("tensor", lambda t, tc=tc, dc=dc, pbt=pbt: t.transpose(
                    pbt[:, 0:128], A[:, tc * 1024 + dc * 128: tc * 1024 + dc * 128 + 128],
                    self.ident), inc=self.sT)
                self.wait("vector", ev)
                self.op("vector", lambda v, tc=tc, dc=dc, pbt=pbt: v.tensor_copy(
                    B[:, dc * T + tc * 128: dc * T + tc * 128 + 128], pbt[:, 0:128]),
                    inc=self.sV)
        self.barrier(("tensor", "vector", "sync"))
        project_tm(A, "L0WkT", l2norm=True)
        self.barrier(("tensor", "vector", "scalar", "sync"))

        ev = self.load_cast_w("L0WvT", WSLOT, WSTG)
        self.wait("tensor", ev)
        pkv = self.pb[2]
        kv_read_ev = None
        for tc in range(NT128):
            vt = VTS[:, 4096 + (tc % 2) * 2048: 4096 + (tc % 2) * 2048 + 1024]
            for dc in range(2):
                pbk = self.pb[3 + dc]
                self.fence("tensor", "vector")
                last = None
                for i in range(FC):
                    last = self.op("tensor", lambda t, k=i, i0=i, tc=tc, dc=dc, pbk=pbk: t.matmul(
                        pbk[:, 0:512],
                        self.XN[:, k * T + tc * 128:k * T + tc * 128 + 128],
                        WSLOT[:, k * 1024 + dc * 512:k * 1024 + dc * 512 + 512],
                        start=(i0 == 0), stop=(i0 == FC - 1)),
                        inc=self.sT if i == FC - 1 else None)
                self.wait("vector", last)
                if kv_read_ev is not None:
                    self.wait("vector", kv_read_ev)
                self.op("vector", lambda v, vt=vt, dc=dc, pbk=pbk: v.tensor_copy(
                    vt[:, dc * 512:dc * 512 + 512], pbk[:, 0:512]), inc=self.sV)
                self.drain("vector")
            evv = self.op("vector", lambda v: v.nop(), inc=self.sV)
            self.wait("tensor", evv)
            last = None
            for h in range(16):
                pbase = 64 * (h % 2)
                cbase = 64 * (h // 2)
                last = self.op("tensor", lambda t, h=h, tc=tc, vt=vt, pbase=pbase, cbase=cbase: t.matmul(
                    pkv[pbase:pbase + 64, cbase:cbase + 64],
                    A[:, tc * 1024 + h * 64: tc * 1024 + h * 64 + 64],
                    vt[:, h * 64:h * 64 + 64],
                    start=(tc == 0 and h < 2), stop=(tc == NT128 - 1 and h >= 14),
                    skip_group_check=True),
                    inc=self.sT if h == 15 else None)
            kv_read_ev = last
        self.wait("vector", kv_read_ev)
        for h in range(16):
            pbase = 64 * (h % 2)
            cbase = 64 * (h // 2)
            self.op("vector", lambda v, h=h, pbase=pbase, cbase=cbase: v.tensor_scalar(
                out=KV[pbase:pbase + 64, cbase:cbase + 64],
                in0=pkv[pbase:pbase + 64, cbase:cbase + 64],
                scalar1=self.GMAX[pbase:pbase + 64, h:h + 1], scalar2=None, op0=OP.mult))
            self.drain("vector")
        self.barrier(("tensor", "vector", "sync"))

        for h in range(16):
            pbase = 64 * (h % 2)
            cb = (h // 2) * T
            for tc in range(NT):
                pbk = self.pb[(h * NT + tc) % 2]
                self.fence("tensor", "vector")
                ev = self.op("tensor", lambda t, h=h, tc=tc, pbase=pbase, cb=cb, pbk=pbk: t.matmul(
                    pbk[0:64, 0:512],
                    KV[pbase:pbase + 64, (h // 2) * 64:(h // 2) * 64 + 64],
                    B[pbase:pbase + 64, cb + tc * 512:cb + tc * 512 + 512],
                    start=True, stop=True), inc=self.sT)
                self.wait("vector", ev)
                self.op("vector", lambda v, h=h, tc=tc, pbase=pbase, cb=cb, pbk=pbk: v.tensor_copy(
                    A[pbase:pbase + 64, cb + tc * 512:cb + tc * 512 + 512], pbk[0:64, 0:512]),
                    inc=self.sV)
        self.barrier(("tensor", "vector", "sync"))

        ev = self.load_cast_w("L0WoT", WSLOT, WSTG)
        self.wait("tensor", ev)
        for cc in range(FC):
            for tc in range(NT):
                pbk = self.pb[(cc * NT + tc) % 2]
                self.fence("tensor", "vector")
                last = None
                for i in range(FC):
                    last = self.op("tensor", lambda t, k=i, i0=i, cc=cc, tc=tc, pbk=pbk: t.matmul(
                        pbk[:, 0:512],
                        WSLOT[:, k * 1024 + cc * 128:k * 1024 + cc * 128 + 128],
                        A[:, k * T + tc * 512:k * T + tc * 512 + 512],
                        start=(i0 == 0), stop=(i0 == FC - 1)),
                        inc=self.sT if i == FC - 1 else None)
                self.wait("vector", last)
                self.op("vector", lambda v, cc=cc, tc=tc, pbk=pbk: v.tensor_tensor(
                    out=self.X[:, cc * T + tc * 512: cc * T + tc * 512 + 512],
                    in0=self.X[:, cc * T + tc * 512: cc * T + tc * 512 + 512],
                    in1=pbk[:, 0:512], op=OP.add), inc=self.sV)
                self.drain("vector")
        self.barrier()

    def qdq_xn(self):
        T, FC = self.T, self.FC
        for c in range(FC):
            self.op("vector", lambda v, c=c: v.tensor_reduce(
                out=self.STATS[:, 32 + c:33 + c], in_=self.XN[:, c * T:(c + 1) * T],
                axis=AX, op=OP.max, apply_absolute_value=True))
            self.drain("vector")
        self.part_tree_max(list(range(32, 32 + FC)), dst_col=0)
        self.collective_max(0, 1, 18)
        self.make_inv_scale(18, 16, 17, 32767.0)
        g = self.GMAX
        tmp = self.ARF[:, 0:512]
        for c in range(FC):
            for tc in range(self.NT):
                sl = slice(c * T + tc * 512, c * T + (tc + 1) * 512)
                self.op("vector", lambda v, a=sl: v.tensor_scalar(
                    out=tmp, in0=self.XN[:, a], scalar1=g[:, 16:17],
                    scalar2=RNE, op0=OP.mult, op1=OP.add))
                self.drain("vector")
                self.op("vector", lambda v, a=sl: v.tensor_scalar(
                    out=self.XN[:, a], in0=tmp, scalar1=RNE, scalar2=g[:, 17:18],
                    op0=OP.subtract, op1=OP.mult))
                self.drain("vector")
        self.barrier(("vector", "tensor", "sync"))

    def lpbq_chunk(self, wname, oc, dst, I):
        WSTG = self.ARF[:, 0:1024] if I == 1024 else self.ARF[:, 0:3072]
        scr = self.ARF[:, 3072:3072 + 1024] if I == 1024 else self.ARF[:, 3072:3072 + 3072]
        QN = self.ARB[:, 0:I]
        nb = I // 32
        self.fence("sync", "vector")
        self.dma("sync", WSTG[:, 0:I], self.dram[wname][oc * 128:(oc + 1) * 128, :])
        self.dma_wait("vector", "sync")
        self.op("vector", lambda v: v.tensor_reduce(
            out=self.STATS[:, 0:nb], in_=WSTG[:, 0:I].rearrange("p (b w) -> p b w", w=32),
            axis=AX, op=OP.max, apply_absolute_value=True))
        self.drain("vector")
        self.op("vector", lambda v: v.tensor_scalar(
            out=self.STATS[:, 0:nb], in0=self.STATS[:, 0:nb],
            scalar1=1.0 / 7.0, scalar2=1e-12, op0=OP.mult, op1=OP.max))
        self.drain("vector")
        self.op("vector", lambda v: v.reciprocal(
            out=self.STATS[:, 96:96 + nb], in_=self.STATS[:, 0:nb]))
        self.drain("vector")
        self.op("vector", lambda v: v.tensor_tensor(
            out=scr[:, 0:I].rearrange("p (b w) -> p b w", w=32),
            in0=WSTG[:, 0:I].rearrange("p (b w) -> p b w", w=32),
            in1=self.STATS[:, 96:96 + nb].unsqueeze(2).to_broadcast([128, nb, 32]),
            op=OP.mult))
        self.drain("vector")
        self.op("vector", lambda v: v.tensor_scalar(
            out=scr[:, 0:I], in0=scr[:, 0:I], scalar1=RNE, scalar2=RNE,
            op0=OP.add, op1=OP.subtract))
        self.drain("vector")
        self.op("vector", lambda v: v.tensor_tensor(
            out=QN[:, 0:I].rearrange("p (b w) -> p b w", w=32),
            in0=scr[:, 0:I].rearrange("p (b w) -> p b w", w=32),
            in1=self.STATS[:, 0:nb].unsqueeze(2).to_broadcast([128, nb, 32]),
            op=OP.mult))
        self.drain("vector")
        ev = self.op("vector", lambda v: v.nop(), inc=self.sV)
        self.wait("tensor", ev)
        evv = None
        for ic in range(I // 128):
            pbt = self.pt0 if ic % 2 == 0 else self.pt1
            if evv is not None:
                self.wait("tensor", evv)
            ev = self.op("tensor", lambda t, ic=ic, pbt=pbt: t.transpose(
                pbt[:, 0:128], QN[:, ic * 128:ic * 128 + 128], self.ident), inc=self.sT)
            self.wait("vector", ev)
            evv = self.op("vector", lambda v, ic=ic, pbt=pbt: v.tensor_copy(
                dst[:, ic * 128:ic * 128 + 128], pbt[:, 0:128]), inc=self.sV)
        return evv

    def qdq_ffn(self, wg, wu, wd, spills):
        T, NT, FC = self.T, self.NT, self.FC
        sp_gate, sp_up, sp_g2, sp_o = spills
        WS = self.ARB[:, 4096:4096 + 4096]
        SPT = self.ARB[:, 8192:8192 + 2048]
        SPT2 = self.ARB[:, 10240:10240 + 2048]
        TIN = self.ARB[:, 12288:12288 + 2048]
        TIN2 = self.ARB[:, 14336:14336 + 2048]
        g = self.GMAX
        sA = self.ARF[:, 2048 + 0:2048 + 512]
        sB = self.ARF[:, 2560:3072]
        sC = self.ARF[:, 2048 + 2048:2048 + 2560] if False else None

        self.op("vector", lambda v: v.memset(self.STATS[:, 100:102], 0.0))
        self.drain("vector")
        for oc in range(24):
            for j, (wname, spill, spt) in enumerate(((wg, sp_gate, SPT), (wu, sp_up, SPT2))):
                wslc = WS[:, j * 2048:(j + 1) * 2048]
                evq = self.lpbq_chunk(wname, oc, wslc, 1024)
                self.wait("tensor", evq)
                self.dma_wait("vector", "gpsimd")
                for tc in range(NT):
                    pbk = self.pb[tc % 2]
                    self.fence("tensor", "vector")
                    self.fence("tensor", "gpsimd")
                    last = None
                    for i in range(FC):
                        last = self.op("tensor", lambda t, k=i, i0=i, tc=tc, pbk=pbk, wslc=wslc: t.matmul(
                            pbk[:, 0:512],
                            wslc[:, k * 128:k * 128 + 128],
                            self.XN[:, k * T + tc * 512:k * T + tc * 512 + 512],
                            start=(i0 == 0), stop=(i0 == FC - 1)),
                            inc=self.sT if i == FC - 1 else None)
                    self.wait("vector", last)
                    st = spt[:, (tc % 4) * 512:(tc % 4) * 512 + 512]
                    self.op("vector", lambda v, st=st, pbk=pbk: v.tensor_copy(st, pbk[:, 0:512]))
                    self.drain("vector")
                    self.op("vector", lambda v, st=st: v.tensor_reduce(
                        out=self.STATS[:, 102:103], in_=st, axis=AX, op=OP.max,
                        apply_absolute_value=True))
                    self.drain("vector")
                    self.op("vector", lambda v, j=j: v.tensor_tensor(
                        out=self.STATS[:, 100 + j:101 + j], in0=self.STATS[:, 100 + j:101 + j],
                        in1=self.STATS[:, 102:103], op=OP.max))
                    self.drain("vector")
                    evv = self.op("vector", lambda v: v.nop(), inc=self.sV)
                    self.wait("gpsimd", evv)
                    self.dma("gpsimd", spill[oc * 128:(oc + 1) * 128, tc * 512:(tc + 1) * 512], st)
        self.barrier(("vector", "tensor", "gpsimd", "sync"))
        self.part_tree_max([100], dst_col=0)
        self.part_tree_max([101], dst_col=1)
        self.collective_max(0, 2, 24)
        self.make_inv_scale(24, 4, 5, 32767.0)
        self.make_inv_scale(25, 6, 7, 32767.0)
        self.barrier(("vector", "tensor", "gpsimd", "sync"))

        self.op("vector", lambda v: v.memset(self.STATS[:, 100:101], 0.0))
        self.drain("vector")
        for oc in range(24):
            for tc in range(NT):
                ti = TIN[:, (tc % 4) * 512:(tc % 4) * 512 + 512]
                self.fence("sync", "vector")
                self.dma_wait("sync", "gpsimd")
                self.dma("sync", ti, sp_gate[oc * 128:(oc + 1) * 128, tc * 512:(tc + 1) * 512])
                self.dma_wait("vector", "sync")
                self.op("vector", lambda v, ti=ti: v.tensor_scalar(
                    out=sA, in0=ti, scalar1=g[:, 4:5], scalar2=RNE, op0=OP.mult, op1=OP.add))
                self.drain("vector")
                self.op("vector", lambda v: v.tensor_scalar(
                    out=sA, in0=sA, scalar1=RNE, scalar2=g[:, 5:6], op0=OP.subtract, op1=OP.mult))
                self.drain("vector")
                ev = self.op("vector", lambda v: v.nop(), inc=self.sV)
                self.wait("scalar", ev)
                ev = self.op("scalar", lambda s: s.activation(
                    out=sB, in_=sA, func=AF.Sigmoid), inc=self.sS)
                self.wait("vector", ev)
                self.op("vector", lambda v: v.tensor_scalar(
                    out=sB, in0=sB, scalar1=65536.0, scalar2=RNE, op0=OP.mult, op1=OP.add))
                self.drain("vector")
                self.op("vector", lambda v: v.tensor_scalar(
                    out=sB, in0=sB, scalar1=RNE, scalar2=65535.0, op0=OP.subtract, op1=OP.min))
                self.drain("vector")
                self.op("vector", lambda v: v.tensor_scalar(
                    out=sB, in0=sB, scalar1=2.0 ** -16, scalar2=None, op0=OP.mult))
                self.drain("vector")
                self.op("vector", lambda v, ti=ti: v.tensor_tensor(
                    out=ti, in0=sA, in1=sB, op=OP.mult))
                self.drain("vector")
                self.op("vector", lambda v, ti=ti: v.tensor_reduce(
                    out=self.STATS[:, 102:103], in_=ti, axis=AX, op=OP.max,
                    apply_absolute_value=True))
                self.drain("vector")
                self.op("vector", lambda v: v.tensor_tensor(
                    out=self.STATS[:, 100:101], in0=self.STATS[:, 100:101],
                    in1=self.STATS[:, 102:103], op=OP.max))
                self.drain("vector")
                evv = self.op("vector", lambda v: v.nop(), inc=self.sV)
                self.wait("gpsimd", evv)
                self.dma("gpsimd", sp_g2[oc * 128:(oc + 1) * 128, tc * 512:(tc + 1) * 512], ti)
        self.barrier(("vector", "tensor", "gpsimd", "scalar", "sync"))
        self.part_tree_max([100], dst_col=0)
        self.collective_max(0, 1, 26)
        self.make_inv_scale(26, 8, 9, 32767.0)
        self.barrier(("vector", "tensor", "gpsimd", "sync"))

        self.op("vector", lambda v: v.memset(self.STATS[:, 100:101], 0.0))
        self.drain("vector")
        for oc in range(24):
            for tc in range(NT):
                ti = TIN[:, (tc % 4) * 512:(tc % 4) * 512 + 512]
                ti2 = TIN2[:, (tc % 4) * 512:(tc % 4) * 512 + 512]
                self.fence("sync", "vector")
                self.dma_wait("sync", "gpsimd")
                self.dma("sync", ti, sp_g2[oc * 128:(oc + 1) * 128, tc * 512:(tc + 1) * 512])
                self.dma("sync", ti2, sp_up[oc * 128:(oc + 1) * 128, tc * 512:(tc + 1) * 512])
                self.dma_wait("vector", "sync")
                self.op("vector", lambda v, ti=ti: v.tensor_scalar(
                    out=sA, in0=ti, scalar1=g[:, 8:9], scalar2=RNE, op0=OP.mult, op1=OP.add))
                self.drain("vector")
                self.op("vector", lambda v: v.tensor_scalar(
                    out=sA, in0=sA, scalar1=RNE, scalar2=g[:, 9:10], op0=OP.subtract, op1=OP.mult))
                self.drain("vector")
                self.op("vector", lambda v, ti2=ti2: v.tensor_scalar(
                    out=sB, in0=ti2, scalar1=g[:, 6:7], scalar2=RNE, op0=OP.mult, op1=OP.add))
                self.drain("vector")
                self.op("vector", lambda v: v.tensor_scalar(
                    out=sB, in0=sB, scalar1=RNE, scalar2=g[:, 7:8], op0=OP.subtract, op1=OP.mult))
                self.drain("vector")
                self.op("vector", lambda v, ti=ti: v.tensor_tensor(
                    out=ti, in0=sA, in1=sB, op=OP.mult))
                self.drain("vector")
                self.op("vector", lambda v, ti=ti: v.tensor_reduce(
                    out=self.STATS[:, 102:103], in_=ti, axis=AX, op=OP.max,
                    apply_absolute_value=True))
                self.drain("vector")
                self.op("vector", lambda v: v.tensor_tensor(
                    out=self.STATS[:, 100:101], in0=self.STATS[:, 100:101],
                    in1=self.STATS[:, 102:103], op=OP.max))
                self.drain("vector")
                evv = self.op("vector", lambda v: v.nop(), inc=self.sV)
                self.wait("gpsimd", evv)
                self.dma("gpsimd", sp_o[oc * 128:(oc + 1) * 128, tc * 512:(tc + 1) * 512], ti)
        self.barrier(("vector", "tensor", "gpsimd", "sync"))
        self.part_tree_max([100], dst_col=0)
        self.collective_max(0, 1, 27)
        self.make_inv_scale(27, 10, 11, 32767.0)
        self.barrier(("vector", "tensor", "gpsimd", "sync"))

        WDQ = self.ARB[:, 4096 + 12288:4096 + 12288 + 24576]
        for oc2 in range(8):
            evv = self.lpbq_chunk(wd, oc2, None, 3072) if False else None
            self._lpbq_into_wdq(wd, oc2, WDQ)
        self.barrier(("vector", "tensor", "sync"))
        for tc in range(NT):
            for half, ccs in enumerate((range(0, 4), range(4, 8))):
                self.fence("tensor", "vector")
                lasts = {}
                for ic in range(24):
                    ti = TIN[:, (ic % 4) * 512:(ic % 4) * 512 + 512]
                    self.fence("sync", "tensor")
                    self.dma("sync", ti, sp_o[ic * 128:(ic + 1) * 128, tc * 512:(tc + 1) * 512])
                    self.dma_wait("vector", "sync")
                    self.op("vector", lambda v, ti=ti: v.tensor_scalar(
                        out=sA, in0=ti, scalar1=g[:, 10:11], scalar2=RNE, op0=OP.mult, op1=OP.add))
                    self.drain("vector")
                    ev = self.op("vector", lambda v, ti=ti: v.tensor_scalar(
                        out=ti, in0=sA, scalar1=RNE, scalar2=g[:, 11:12],
                        op0=OP.subtract, op1=OP.mult), inc=self.sV)
                    self.drain("vector")
                    self.wait("tensor", ev)
                    for ci, cc in enumerate(ccs):
                        lasts[cc] = self.op("tensor", lambda t, ic=ic, cc=cc, ti=ti: t.matmul(
                            self.pb[cc % 4][:, 0:512],
                            WDQ[:, ic * 1024 + cc * 128:ic * 1024 + cc * 128 + 128],
                            ti, start=(ic == 0), stop=(ic == 23)),
                            inc=self.sT if (ic == 23) else None)
                for ci, cc in enumerate(ccs):
                    self.wait("vector", lasts[cc])
                    self.op("vector", lambda v, cc=cc, tc=tc: v.tensor_tensor(
                        out=self.X[:, cc * T + tc * 512: cc * T + tc * 512 + 512],
                        in0=self.X[:, cc * T + tc * 512: cc * T + tc * 512 + 512],
                        in1=self.pb[cc % 4][:, 0:512], op=OP.add), inc=self.sV)
                    self.drain("vector")
        self.barrier()

    def _lpbq_into_wdq(self, wd, oc2, WDQ):
        I = 3072
        WSTG = self.ARF[:, 0:3072]
        scr = self.ARF[:, 3072:3072 + 3072]
        QN = self.ARB[:, 0:I]
        nb = I // 32
        self.fence("sync", "vector")
        self.dma("sync", WSTG[:, 0:I], self.dram[wd][oc2 * 128:(oc2 + 1) * 128, :])
        self.dma_wait("vector", "sync")
        self.op("vector", lambda v: v.tensor_reduce(
            out=self.STATS[:, 0:nb], in_=WSTG[:, 0:I].rearrange("p (b w) -> p b w", w=32),
            axis=AX, op=OP.max, apply_absolute_value=True))
        self.drain("vector")
        self.op("vector", lambda v: v.tensor_scalar(
            out=self.STATS[:, 0:nb], in0=self.STATS[:, 0:nb],
            scalar1=1.0 / 7.0, scalar2=1e-12, op0=OP.mult, op1=OP.max))
        self.drain("vector")
        self.op("vector", lambda v: v.reciprocal(
            out=self.STATS[:, 96:96 + nb], in_=self.STATS[:, 0:nb]))
        self.drain("vector")
        self.op("vector", lambda v: v.tensor_tensor(
            out=scr[:, 0:I].rearrange("p (b w) -> p b w", w=32),
            in0=WSTG[:, 0:I].rearrange("p (b w) -> p b w", w=32),
            in1=self.STATS[:, 96:96 + nb].unsqueeze(2).to_broadcast([128, nb, 32]),
            op=OP.mult))
        self.drain("vector")
        self.op("vector", lambda v: v.tensor_scalar(
            out=scr[:, 0:I], in0=scr[:, 0:I], scalar1=RNE, scalar2=RNE,
            op0=OP.add, op1=OP.subtract))
        self.drain("vector")
        self.op("vector", lambda v: v.tensor_tensor(
            out=QN[:, 0:I].rearrange("p (b w) -> p b w", w=32),
            in0=scr[:, 0:I].rearrange("p (b w) -> p b w", w=32),
            in1=self.STATS[:, 0:nb].unsqueeze(2).to_broadcast([128, nb, 32]),
            op=OP.mult))
        self.drain("vector")
        ev = self.op("vector", lambda v: v.nop(), inc=self.sV)
        self.wait("tensor", ev)
        evv = None
        for ic in range(24):
            pbt = self.pt0 if ic % 2 == 0 else self.pt1
            if evv is not None:
                self.wait("tensor", evv)
            ev = self.op("tensor", lambda t, ic=ic, pbt=pbt: t.transpose(
                pbt[:, 0:128], QN[:, ic * 128:ic * 128 + 128], self.ident), inc=self.sT)
            self.wait("vector", ev)
            evv = self.op("vector", lambda v, ic=ic, pbt=pbt, oc2=oc2: v.tensor_copy(
                WDQ[:, ic * 1024 + oc2 * 128:ic * 1024 + oc2 * 128 + 128], pbt[:, 0:128]),
                inc=self.sV)
        self.wait("tensor", evv)

    def proj_fm(self, wname, width, dst, wslot, wstg, scale=None, rhs_region=None):
        rhs = rhs_region if rhs_region is not None else self.XN
        T, NT, FC = self.T, self.NT, self.FC
        ev = self.load_cast_w(wname, wslot, wstg, width=width, scale=scale)
        self.wait("tensor", ev)
        for dc in range(width // 128):
            for tc in range(NT):
                pbk = self.pb[(dc * NT + tc) % 2]
                self.fence("tensor", "vector")
                last = None
                for i in range(FC):
                    last = self.op("tensor", lambda t, k=i, i0=i, dc=dc, tc=tc, pbk=pbk: t.matmul(
                        pbk[:, 0:512],
                        wslot[:, k * width + dc * 128:k * width + dc * 128 + 128],
                        rhs[:, k * T + tc * 512:k * T + tc * 512 + 512],
                        start=(i0 == 0), stop=(i0 == FC - 1)),
                        inc=self.sT if i == FC - 1 else None)
                self.wait("vector", last)
                self.op("vector", lambda v, dc=dc, tc=tc, pbk=pbk: v.tensor_copy(
                    dst[:, dc * T + tc * 512: dc * T + tc * 512 + 512], pbk[:, 0:512]),
                    inc=self.sV)
                self.drain("vector")

    def l1_attn(self):
        T, NT, NT128, FC = self.T, self.NT, self.NT128, self.FC
        Q = self.ARB[:, 0:8 * T]
        K = self.ARB[:, 8 * T:12 * T]
        CTAB = self.ARB[:, 12 * T:13 * T]
        STAB = self.ARB[:, 13 * T:14 * T]
        WSLOT = self.ARB[:, 14 * T:14 * T + 8192]
        SHUF = self.ARB[:, 14 * T + 8192:15 * T + 8192]
        PT = WSLOT[:, 0:2048]
        KSTG = WSLOT[:, 2048:2048 + T]
        QSTG = WSLOT[:, 2048 + T:2048 + 2 * T]
        OUTTM = SHUF[:, 0:64]
        WSTG = self.ARF[:, 0:1024]
        BO = self.ARF[:, 5200:5208]
        VTS = self.ARF.bitcast(BF16)
        VA = VTS[:, 2048:2048 + self.NT128 * 520]

        self.dma("sync", CTAB, self.dram["l1rope"][0, :, :])
        self.dma("sync", STAB, self.dram["l1rope"][1, :, :])
        self.dma("sync", BO, self.dram["L1bo"][:, :])
        self.dma_wait("vector", "sync")
        self.dma_wait("tensor", "sync")

        mask = []
        for i in range(16):
            mask += [2 * i + 1, 2 * i]

        def rope_apply(region, nchunks):
            for c in range(nchunks):
                tile = region[:, c * T:(c + 1) * T]
                self.op("vector", lambda v, tile=tile: v.stream_shuffle(
                    out=SHUF, in_=tile, mask=mask))
                self.drain("vector")
                self.op("vector", lambda v, tile=tile: v.tensor_tensor(
                    out=tile, in0=tile, in1=CTAB, op=OP.mult))
                self.drain("vector")
                self.op("vector", lambda v: v.tensor_tensor(
                    out=SHUF, in0=SHUF, in1=STAB, op=OP.mult))
                self.drain("vector")
                self.op("vector", lambda v, tile=tile: v.tensor_tensor(
                    out=tile, in0=tile, in1=SHUF, op=OP.add))
                self.drain("vector")

        self.proj_fm("L1Wq", 1024, Q, WSLOT, WSTG, scale=64.0 ** -0.5)
        self.dump("l1_q", Q, 8 * T)
        rope_apply(Q, FC)
        self.dump("l1_qrope", Q, 8 * T)
        self.proj_fm("L1Wk", 512, K, WSLOT, WSTG)
        rope_apply(K, 4)
        self.dump("l1_krope", K, 4 * T)
        self.barrier(("tensor", "vector", "sync"))

        ev = self.load_cast_w("L1Wv", WSLOT, WSTG, width=512)
        self.wait("tensor", ev)
        for tc in range(NT128):
            vt = VA[:, tc * 520:(tc + 1) * 520]
            pbk = self.pb[tc % 2]
            self.fence("tensor", "vector")
            last = None
            for i in range(FC):
                last = self.op("tensor", lambda t, k=i, i0=i, tc=tc, pbk=pbk: t.matmul(
                    pbk[:, 0:512],
                    self.XN[:, k * T + tc * 128:k * T + tc * 128 + 128],
                    WSLOT[:, k * 512:k * 512 + 512],
                    start=(i0 == 0), stop=(i0 == FC - 1)),
                    inc=self.sT if i == FC - 1 else None)
            self.wait("vector", last)
            self.op("vector", lambda v, vt=vt, pbk=pbk: v.tensor_copy(
                vt.rearrange("p (h w) -> p h w", w=65)[:, :, 0:64],
                pbk[:, 0:512].rearrange("p (h w) -> p h w", w=64)))
            self.drain("vector")
            self.op("vector", lambda v, vt=vt: v.memset(
                vt.rearrange("p (h w) -> p h w", w=65)[:, :, 64:65], 1.0), inc=self.sV)
            self.drain("vector")
        self.barrier(("tensor", "vector", "scalar", "sync"))

        for h in range(16):
            kvh = h // 2
            kbase = 64 * (kvh % 2)
            kchunk = kvh // 2
            qbase = 64 * (h % 2)
            qchunk = h // 2
            self.fence("vector", "tensor")
            if h % 2 == 0:
                self.op("vector", lambda v, kbase=kbase, kchunk=kchunk: v.tensor_copy(
                    KSTG[0:64, :], K[kbase:kbase + 64, kchunk * T:(kchunk + 1) * T]))
                self.drain("vector")
            ev = self.op("vector", lambda v, qbase=qbase, qchunk=qchunk: v.tensor_copy(
                QSTG[0:64, :], Q[qbase:qbase + 64, qchunk * T:(qchunk + 1) * T]),
                inc=self.sV)
            self.drain("vector")
            self.wait("tensor", ev)
            for qb in range(NT):
                ppv = self.pb[2]
                self.fence("tensor", "vector")
                for kc in range(NT128):
                    pbk = self.pb[3 + kc % 2]
                    pt = PT[:, (kc % 4) * 512:(kc % 4) * 512 + 512]
                    self.op("tensor", lambda t, kc=kc, pbk=pbk, qb=qb: t.matmul(
                        pbk[:, 0:512],
                        KSTG[0:64, kc * 128:kc * 128 + 128],
                        QSTG[0:64, qb * 512:qb * 512 + 512],
                        start=True, stop=True), inc=self.sT)
                    self.wait("scalar", (self.sT, self.sT.n))
                    evx = self.op("scalar", lambda s, pt=pt, pbk=pbk: s.activation(
                        out=pt, in_=pbk[:, 0:512], func=AF.Exp), inc=self.sS)
                    self.wait("tensor", evx)
                    for qs in range(4):
                        self.op("tensor", lambda t, kc=kc, qs=qs, pt=pt, ppv=ppv, tcv=kc,
                                kvh=kvh: t.matmul(
                            ppv[:, 65 * qs:65 * qs + 65],
                            pt[:, qs * 128:qs * 128 + 128],
                            VA[:, tcv * 520 + 65 * kvh: tcv * 520 + 65 * kvh + 65],
                            start=(kc == 0 and qs == 0), stop=(kc == NT128 - 1 and qs == 3),
                            skip_group_check=True),
                            inc=self.sT if (kc == NT128 - 1 and qs == 3) else None)
                self.wait("vector", (self.sT, self.sT.n))
                for qs in range(4):
                    self.op("vector", lambda v, qs=qs, ppv=ppv: v.reciprocal(
                        out=self.STATS[:, 120:121], in_=ppv[:, 65 * qs + 64:65 * qs + 65]))
                    self.drain("vector")
                    ev = self.op("vector", lambda v, qs=qs, ppv=ppv: v.tensor_scalar(
                        out=OUTTM, in0=ppv[:, 65 * qs:65 * qs + 64],
                        scalar1=self.STATS[:, 120:121], scalar2=None, op0=OP.mult),
                        inc=self.sV)
                    self.drain("vector")
                    self.wait("tensor", ev)
                    pbt = self.pt0
                    evt = self.op("tensor", lambda t, pbt=pbt: t.transpose(
                        pbt[0:64, 0:128], OUTTM, self.ident), inc=self.sT)
                    self.wait("vector", evt)
                    evv = self.op("vector", lambda v, qb=qb, qs=qs, pbt=pbt, qbase=qbase,
                                  qchunk=qchunk: v.tensor_copy(
                        Q[qbase:qbase + 64,
                          qchunk * T + qb * 512 + qs * 128: qchunk * T + qb * 512 + qs * 128 + 128],
                        pbt[0:64, 0:128]), inc=self.sV)
                    self.wait("tensor", evv)
        self.barrier(("tensor", "vector", "scalar", "sync"))
        self.dump("l1_attnT", Q, 8 * T)

        ev = self.load_cast_w("L1Wo", WSLOT, WSTG)
        self.wait("tensor", ev)
        for cc in range(FC):
            for tc in range(NT):
                pbk = self.pb[(cc * NT + tc) % 2]
                self.fence("tensor", "vector")
                last = None
                for i in range(FC):
                    last = self.op("tensor", lambda t, k=i, i0=i, cc=cc, tc=tc, pbk=pbk: t.matmul(
                        pbk[:, 0:512],
                        WSLOT[:, k * 1024 + cc * 128:k * 1024 + cc * 128 + 128],
                        Q[:, k * T + tc * 512:k * T + tc * 512 + 512],
                        start=(i0 == 0), stop=(i0 == FC - 1)),
                        inc=self.sT if i == FC - 1 else None)
                self.wait("vector", last)
                self.op("vector", lambda v, cc=cc, tc=tc, pbk=pbk: v.scalar_tensor_tensor(
                    out=self.X[:, cc * T + tc * 512: cc * T + tc * 512 + 512],
                    in0=pbk[:, 0:512], scalar=BO[:, cc:cc + 1],
                    in1=self.X[:, cc * T + tc * 512: cc * T + tc * 512 + 512],
                    op0=OP.add, op1=OP.add), inc=self.sV)
                self.drain("vector")
        self.barrier()
        self.dump_x("l1_xout")

    def fp4_qdq(self, tile, width):
        nb = width // 16
        r = self.ARF[:, 1024:1024 + width]
        a = self.ARF[:, 1024 + width:1024 + 2 * width]
        t1 = self.ARF[:, 1024 + 2 * width:1024 + 3 * width]
        t2 = self.ARF[:, 1024 + 3 * width:1024 + 4 * width]
        TS, STT, TT = "vector", "vector", "vector"
        v3 = lambda ap: ap.rearrange("p (b w) -> p b w", w=16)
        self.op("vector", lambda v: v.tensor_reduce(
            out=self.STATS[:, 0:nb], in_=v3(tile), axis=AX, op=OP.max,
            apply_absolute_value=True))
        self.drain("vector")
        self.op("vector", lambda v: v.tensor_scalar(
            out=self.STATS[:, 0:nb], in0=self.STATS[:, 0:nb],
            scalar1=1.0 / 6.0, scalar2=1e-12, op0=OP.mult, op1=OP.max))
        self.drain("vector")
        self.op("vector", lambda v: v.reciprocal(
            out=self.STATS[:, 128:128 + nb], in_=self.STATS[:, 0:nb]))
        self.drain("vector")
        self.op("vector", lambda v: v.tensor_scalar(
            out=self.STATS[:, 0:nb], in0=self.STATS[:, 0:nb],
            scalar1=0.5, scalar2=None, op0=OP.mult))
        self.drain("vector")
        self.op("vector", lambda v: v.tensor_tensor(
            out=v3(r), in0=v3(tile),
            in1=self.STATS[:, 128:128 + nb].unsqueeze(2).to_broadcast([128, nb, 16]),
            op=OP.mult))
        self.drain("vector")
        self.op("vector", lambda v: v.tensor_scalar(
            out=a, in0=r, scalar1=2.0, scalar2=RNE, op0=OP.mult, op1=OP.add))
        self.drain("vector")
        self.op("vector", lambda v: v.tensor_scalar(
            out=a, in0=a, scalar1=RNE, scalar2=4.0, op0=OP.subtract, op1=OP.min))
        self.drain("vector")
        self.op("vector", lambda v: v.tensor_scalar(
            out=a, in0=a, scalar1=-4.0, scalar2=None, op0=OP.max))
        self.drain("vector")
        self.op("vector", lambda v: v.tensor_scalar(
            out=t1, in0=r, scalar1=2.5, scalar2=None, op0=OP.is_gt))
        self.drain("vector")
        for thr in (3.5, 5.0, 5.0):
            self.op("vector", lambda v, thr=thr: v.scalar_tensor_tensor(
                out=t1, in0=r, scalar=thr, in1=t1, op0=OP.is_gt, op1=OP.add))
            self.drain("vector")
        self.op("vector", lambda v: v.tensor_scalar(
            out=t2, in0=r, scalar1=-2.5, scalar2=None, op0=OP.is_lt))
        self.drain("vector")
        for thr in (-3.5, -5.0, -5.0):
            self.op("vector", lambda v, thr=thr: v.scalar_tensor_tensor(
                out=t2, in0=r, scalar=thr, in1=t2, op0=OP.is_lt, op1=OP.add))
            self.drain("vector")
        self.op("vector", lambda v: v.scalar_tensor_tensor(
            out=a, in0=t1, scalar=2.0, in1=a, op0=OP.mult, op1=OP.add))
        self.drain("vector")
        self.op("vector", lambda v: v.scalar_tensor_tensor(
            out=a, in0=t2, scalar=-2.0, in1=a, op0=OP.mult, op1=OP.add))
        self.drain("vector")
        self.op("vector", lambda v: v.tensor_tensor(
            out=v3(tile), in0=v3(a),
            in1=self.STATS[:, 0:nb].unsqueeze(2).to_broadcast([128, nb, 16]),
            op=OP.mult))
        self.drain("vector")

    def l1_ffn(self):
        T, NT, NT128, FC = self.T, self.NT, self.NT128, self.FC
        XQ4T = self.ARB[:, 0:8 * T]
        XTM = self.ARB[:, 8 * T:16 * T]
        WSL = self.ARB[:, 16 * T:16 * T + 2048]
        HSTRIP = self.ARB[:, 16 * T + 2048:16 * T + 2048 + 2048]
        WN = self.ARB[:, 16 * T + 4096:16 * T + 4096 + 1024]
        TIN = self.ARB[:, 16 * T + 5120:16 * T + 5120 + 2048]
        WSTG = self.ARF[:, 0:1024]
        WDQ = self.ARB[:, 16 * T + 8192 if False else 0:0]

        for tc in range(NT128):
            for fc in range(FC):
                pbt = self.pt0 if (tc * FC + fc) % 2 == 0 else self.pt1
                self.fence("tensor", "vector")
                ev = self.op("tensor", lambda t, tc=tc, fc=fc, pbt=pbt: t.transpose(
                    pbt[:, 0:128], self.XN[:, fc * T + tc * 128:fc * T + tc * 128 + 128],
                    self.ident), inc=self.sT)
                self.wait("vector", ev)
                self.op("vector", lambda v, tc=tc, fc=fc, pbt=pbt: v.tensor_copy(
                    XTM[:, tc * 1024 + fc * 128:tc * 1024 + fc * 128 + 128], pbt[:, 0:128]),
                    inc=self.sV)
        self.fence("vector", "tensor")
        for tc in range(NT128):
            self.fp4_qdq(XTM[:, tc * 1024:(tc + 1) * 1024], 1024)
        self.fence("tensor", "vector")
        for tc in range(NT128):
            for fc in range(FC):
                pbt = self.pt0 if (tc * FC + fc) % 2 == 0 else self.pt1
                self.fence("tensor", "vector")
                ev = self.op("tensor", lambda t, tc=tc, fc=fc, pbt=pbt: t.transpose(
                    pbt[:, 0:128], XTM[:, tc * 1024 + fc * 128:tc * 1024 + fc * 128 + 128],
                    self.ident), inc=self.sT)
                self.wait("vector", ev)
                self.op("vector", lambda v, tc=tc, fc=fc, pbt=pbt: v.tensor_copy(
                    XQ4T[:, fc * T + tc * 128:fc * T + tc * 128 + 128], pbt[:, 0:128]),
                    inc=self.sV)
        self.barrier(("tensor", "vector", "sync"))

        IC = 2816 // 128
        for ic in range(IC):
            for j, wname in enumerate(("L1wg", "L1wu")):
                self.fence("sync", "vector")
                self.dma("sync", WSTG[:, :], self.dram[wname][ic * 128:(ic + 1) * 128, :])
                self.dma_wait("vector", "sync")
                self.op("vector", lambda v: v.tensor_copy(WN[:, 0:1024], WSTG[:, :]))
                self.drain("vector")
                self.fp4_qdq(WN[:, 0:1024], 1024)
                self.fence("tensor", "vector")
                for fc in range(FC):
                    pbt = self.pt0 if fc % 2 == 0 else self.pt1
                    ev = self.op("tensor", lambda t, fc=fc, pbt=pbt: t.transpose(
                        pbt[:, 0:128], WN[:, fc * 128:fc * 128 + 128], self.ident),
                        inc=self.sT)
                    self.wait("vector", ev)
                    evv = self.op("vector", lambda v, fc=fc, j=j, pbt=pbt: v.tensor_copy(
                        WSL[:, j * 1024 + fc * 128:j * 1024 + fc * 128 + 128], pbt[:, 0:128]),
                        inc=self.sV)
                    self.wait("tensor", evv)
            self.fence("tensor", "vector")
            for tc in range(NT128):
                pg, py = self.pb[0], self.pb[1]
                self.fence("tensor", "vector")
                self.fence("tensor", "scalar")
                lg = ly = None
                for i in range(FC):
                    lg = self.op("tensor", lambda t, k=i, i0=i, tc=tc, pg=pg: t.matmul(
                        pg[:, 0:128],
                        XQ4T[:, k * T + tc * 128:k * T + tc * 128 + 128],
                        WSL[:, 0 * 1024 + k * 128:0 * 1024 + k * 128 + 128],
                        start=(i0 == 0), stop=(i0 == FC - 1)),
                        inc=self.sT if i == FC - 1 else None)
                for i in range(FC):
                    ly = self.op("tensor", lambda t, k=i, i0=i, tc=tc, py=py: t.matmul(
                        py[:, 0:128],
                        XQ4T[:, k * T + tc * 128:k * T + tc * 128 + 128],
                        WSL[:, 1 * 1024 + k * 128:1 * 1024 + k * 128 + 128],
                        start=(i0 == 0), stop=(i0 == FC - 1)),
                        inc=self.sT if i == FC - 1 else None)
                self.wait("scalar", lg)
                evs = self.op("scalar", lambda s, tc=tc, pg=pg: s.activation(
                    out=HSTRIP[:, tc * 128:tc * 128 + 128], in_=pg[:, 0:128],
                    func=AF.Silu), inc=self.sS)
                self.wait("vector", ly)
                self.wait("vector", evs)
                self.op("vector", lambda v, tc=tc, py=py: v.tensor_tensor(
                    out=HSTRIP[:, tc * 128:tc * 128 + 128],
                    in0=HSTRIP[:, tc * 128:tc * 128 + 128], in1=py[:, 0:128],
                    op=OP.mult), inc=self.sV)
                self.drain("vector")
                self.wait("tensor", (self.sV, self.sV.n))
            for half in range(2):
                self.fp4_qdq(HSTRIP[:, half * 1024:(half + 1) * 1024], 1024)
            self.fence("tensor", "vector")
            self.dma_wait("vector", "gpsimd")
            for tc in range(NT128):
                pbt = self.pt0 if tc % 2 == 0 else self.pt1
                ev = self.op("tensor", lambda t, tc=tc, pbt=pbt: t.transpose(
                    pbt[:, 0:128], HSTRIP[:, tc * 128:tc * 128 + 128], self.ident),
                    inc=self.sT)
                self.wait("vector", ev)
                evv = self.op("vector", lambda v, tc=tc, ic=ic, pbt=pbt: v.tensor_copy(
                    TIN[:, (tc % 8) * 128:(tc % 8) * 128 + 128] if False else
                    TIN[:, (tc % 16) * 128:(tc % 16) * 128 + 128], pbt[:, 0:128]),
                    inc=self.sV)
                self.wait("tensor", evv)
                self.wait("gpsimd", evv)
                self.dma("gpsimd",
                         self.sp_gate[ic * 128:(ic + 1) * 128, tc * 128:(tc + 1) * 128],
                         TIN[:, (tc % 16) * 128:(tc % 16) * 128 + 128])
        self.barrier(("tensor", "vector", "scalar", "gpsimd", "sync"))

        WDQL = self.ARB[:, 0:IC * 1024]
        WN3 = self.ARB[:, IC * 1024:IC * 1024 + 2816]
        TIN2 = self.ARB[:, IC * 1024 + 2816:IC * 1024 + 2816 + 2048]
        WSTG3 = self.ARF[:, 0:1024]
        for oc in range(8):
            for seg, (c0, w) in enumerate(((0, 1024), (1024, 1024), (2048, 768))):
                self.fence("sync", "vector")
                self.dma("sync", WSTG3[:, 0:w], self.dram["L1wd"][oc * 128:(oc + 1) * 128, c0:c0 + w])
                self.dma_wait("vector", "sync")
                self.op("vector", lambda v, c0=c0, w=w: v.tensor_copy(
                    WN3[:, c0:c0 + w], WSTG3[:, 0:w]))
                self.drain("vector")
                self.fp4_qdq(WN3[:, c0:c0 + w], w)
            self.fence("tensor", "vector")
            for ic in range(IC):
                pbt = self.pt0 if ic % 2 == 0 else self.pt1
                ev = self.op("tensor", lambda t, ic=ic, pbt=pbt: t.transpose(
                    pbt[:, 0:128], WN3[:, ic * 128:ic * 128 + 128], self.ident),
                    inc=self.sT)
                self.wait("vector", ev)
                evv = self.op("vector", lambda v, ic=ic, oc=oc, pbt=pbt: v.tensor_copy(
                    WDQL[:, ic * 1024 + oc * 128:ic * 1024 + oc * 128 + 128], pbt[:, 0:128]),
                    inc=self.sV)
                self.wait("tensor", evv)
        self.barrier(("tensor", "vector", "sync"))
        for tc in range(NT):
            self.fence("tensor", "vector")
            lasts = {}
            for ic in range(IC):
                ti = TIN2[:, (ic % 2) * 1024:(ic % 2) * 1024 + 1024]
                self.fence("sync", "tensor")
                self.dma("sync", ti[:, 0:512],
                         self.sp_gate[ic * 128:(ic + 1) * 128, tc * 512:(tc + 1) * 512])
                self.dma_wait("tensor", "sync")
                for cc in range(self.FC):
                    lasts[cc] = self.op("tensor", lambda t, ic=ic, cc=cc, ti=ti: t.matmul(
                        self.pb[cc][:, 0:512],
                        WDQL[:, ic * 1024 + cc * 128:ic * 1024 + cc * 128 + 128],
                        ti[:, 0:512], start=(ic == 0), stop=(ic == IC - 1)),
                        inc=self.sT if ic == IC - 1 else None)
            for cc in range(self.FC):
                self.wait("vector", lasts[cc])
                self.op("vector", lambda v, cc=cc, tc=tc: v.tensor_tensor(
                    out=self.X[:, cc * T + tc * 512: cc * T + tc * 512 + 512],
                    in0=self.X[:, cc * T + tc * 512: cc * T + tc * 512 + 512],
                    in1=self.pb[cc][:, 0:512], op=OP.add), inc=self.sV)
                self.drain("vector")
        self.barrier()
    def l2_ffn(self):
        T, NT, FC = self.T, self.NT, self.FC
        CW = self.ARB[:, 0:8192]
        SW = self.ARB[:, 8192:16384]
        W1 = self.ARF[:, 0:1024]
        W2 = self.ARF[:, 1024:2048]
        S1 = self.ARF[:, 2048:2560]
        S2 = self.ARF[:, 2560:3072]
        for j, (wa, wb, dst, bias) in enumerate((("L2th", "L2thn", CW, self.CCOL[:, 1:2]),
                                                 ("L2ph", "L2phn", SW, None))):
            for c in range(FC):
                self.fence("sync", "vector")
                self.fence("sync", "scalar")
                self.dma("sync", W1, self.dram[wa][c * 128:(c + 1) * 128, :])
                self.dma("sync", W2, self.dram[wb][c * 128:(c + 1) * 128, :])
                self.dma_wait("vector", "sync")
                ev = self.op("vector", lambda v: v.tensor_tensor(
                    out=W1, in0=W1, in1=W2, op=OP.add), inc=self.sV)
                self.wait("scalar", ev)
                ev = self.op("scalar", lambda s, c=c, dst=dst, bias=bias: s.activation(
                    out=dst[:, c * 1024:(c + 1) * 1024], in_=W1, func=AF.Sin,
                    **({"bias": bias} if bias is not None else {})), inc=self.sS)
                self.wait("vector", ev)
                self.wait("sync", ev)
        self.barrier(("tensor", "vector", "scalar", "sync"))
        for oc in range(FC):
            for tc in range(NT):
                pr, pi = self.pb[(tc % 2) * 2], self.pb[(tc % 2) * 2 + 1]
                self.fence("tensor", "vector")
                self.fence("tensor", "scalar")
                for (pbk, wsl) in ((pr, CW), (pi, SW)):
                    last = None
                    for i in range(FC):
                        last = self.op("tensor", lambda t, k=i, i0=i, oc=oc, tc=tc, pbk=pbk, wsl=wsl: t.matmul(
                            pbk[:, 0:512],
                            wsl[:, k * 1024 + oc * 128:k * 1024 + oc * 128 + 128],
                            self.XN[:, k * T + tc * 512:k * T + tc * 512 + 512],
                            start=(i0 == 0), stop=(i0 == FC - 1)),
                            inc=self.sT if i == FC - 1 else None)
                self.wait("scalar", last)
                self.op("scalar", lambda s, pr=pr: s.activation(out=S1, in_=pr[:, 0:512], func=AF.Square))
                self.drain("scalar")
                ev = self.op("scalar", lambda s, pi=pi: s.activation(out=S2, in_=pi[:, 0:512], func=AF.Square),
                             inc=self.sS)
                self.wait("vector", ev)
                ev = self.op("vector", lambda v: v.tensor_tensor(out=S1, in0=S1, in1=S2, op=OP.add),
                             inc=self.sV)
                self.drain("vector")
                self.wait("scalar", ev)
                ev = self.op("scalar", lambda s: s.activation(out=S2, in_=S1, func=AF.Sqrt), inc=self.sS)
                self.wait("vector", ev)
                self.op("vector", lambda v, oc=oc, tc=tc: v.tensor_tensor(
                    out=self.X[:, oc * T + tc * 512:oc * T + tc * 512 + 512],
                    in0=self.X[:, oc * T + tc * 512:oc * T + tc * 512 + 512],
                    in1=S2, op=OP.add), inc=self.sV)
                self.drain("vector")
        self.barrier()

    def l2_attn(self):
        T, NT, NT128, FC = self.T, self.NT, self.NT128, self.FC
        Q = self.ARB[:, 0:8 * T]
        K = self.ARB[:, 8 * T:16 * T]
        WIN = self.ARB[:, 16 * T:16 * T + 8192]
        WINF = WIN.bitcast(F32)
        WSTG = self.ARF[:, 0:1024]
        VTS = self.ARF.bitcast(BF16)
        PB = self.XN
        KTAR = float(max(1, int(T * 0.1)))

        g100 = self.STATS[:, 140:148]
        ph025 = self.STATS[:, 148:156]
        kcol = self.STATS[:, 156:168]
        th100 = self.STATS[:, 168:169]
        pbias = self.STATS[:, 169:177]
        self.dma("sync", g100, self.dram["L2gam"][:, :])
        self.dma("sync", ph025, self.dram["L2phs"][:, :])
        self.dma("sync", pbias, self.dram["L2pb"][:, :])
        self.dma("sync", self.STATS[0:1, 200:212], self.dram["L2ker"][:, :])
        self.dma("sync", self.STATS[0:1, 212:213], self.dram["L2thf"][:, :])
        self.dma_wait("vector", "sync")
        self.dma_wait("tensor", "sync")
        self.replicate_cols(self.STATS[0:1, 200:212], 12, 0)
        self.drain("vector")
        self.op("vector", lambda v: v.tensor_copy(kcol, self.GMAX[:, 0:12]))
        self.drain("vector")
        self.replicate_cols(self.STATS[0:1, 212:213], 1, 12)
        self.drain("vector")
        self.op("vector", lambda v: v.tensor_scalar(
            out=th100, in0=self.GMAX[:, 12:13], scalar1=0.01, scalar2=None, op0=OP.mult))
        self.drain("vector")
        self.op("vector", lambda v: v.tensor_scalar(
            out=g100, in0=g100, scalar1=0.01, scalar2=None, op0=OP.mult))
        self.drain("vector")
        self.op("vector", lambda v: v.tensor_scalar(
            out=ph025, in0=ph025, scalar1=1.0 / (2 * math.pi), scalar2=0.25,
            op0=OP.mult, op1=OP.add), inc=self.sV)
        self.drain("vector")

        WSL = WIN[:, 0:2048]
        CHK = WIN[:, 2048:2048 + T]
        GD = WIN[:, 2048 + T:2048 + 2 * T]

        def qkv_slice(j, dst_tile_fn, scale=None):
            for k in range(FC):
                self.fence("sync", "vector")
                self.dma("sync", WSTG[:, 0:128],
                         self.dram["L2qkvT"][k * 128:(k + 1) * 128, j * 128:(j + 1) * 128])
                self.dma_wait("vector", "sync")
                self.op("vector", lambda v, k=k: v.tensor_scalar(
                    out=WSL[:, k * 128:(k + 1) * 128], in0=WSTG[:, 0:128],
                    scalar1=(1.0 if scale is None else scale), scalar2=None, op0=OP.mult),
                    inc=self.sV)
                self.drain("vector")
            self.fence("tensor", "vector")
            for tc in range(NT):
                pbk = self.pb[tc % 2]
                self.fence("tensor", "vector")
                last = None
                for i in range(FC):
                    last = self.op("tensor", lambda t, k=i, i0=i, tc=tc, pbk=pbk: t.matmul(
                        pbk[:, 0:512],
                        WSL[:, k * 128:k * 128 + 128],
                        self.XN[:, k * T + tc * 512:k * T + tc * 512 + 512],
                        start=(i0 == 0), stop=(i0 == FC - 1)),
                        inc=self.sT if i == FC - 1 else None)
                self.wait("vector", last)
                dst_tile_fn(tc, pbk)

        def dendritic(tile, kc0):
            self.dma_wait("vector", "gpsimd")
            self.op("vector", lambda v, tile=tile: v.tensor_scalar(
                out=GD, in0=tile, scalar1=kcol[:, kc0 + 3:kc0 + 4], scalar2=None, op0=OP.mult))
            self.drain("vector")
            for sh, ki in ((1, 2), (2, 1), (3, 0)):
                self.op("vector", lambda v, sh=sh, ki=ki, tile=tile: v.scalar_tensor_tensor(
                    out=GD[:, sh:T], in0=tile[:, 0:T - sh], scalar=kcol[:, kc0 + ki:kc0 + ki + 1],
                    in1=GD[:, sh:T], op0=OP.mult, op1=OP.add))
                self.drain("vector")
            self.op("vector", lambda v, tile=tile: v.tensor_copy(tile, GD))
            self.drain("vector")

        VB = WIN[:, 2048 + 2 * T:2048 + 2 * T + 512] if False else GD
        for dc in range(FC):
            def v_evac(tc, pbk, dc=dc):
                self.op("vector", lambda v, tc=tc, pbk=pbk: v.tensor_copy(
                    CHK[:, tc * 512:tc * 512 + 512], pbk[:, 0:512]), inc=self.sV)
                self.drain("vector")
            qkv_slice(16 + dc, v_evac)
            dendritic(CHK, 8)
            self.fence("tensor", "vector")
            for tc in range(NT128):
                pbt = self.pt0 if tc % 2 == 0 else self.pt1
                ev = self.op("tensor", lambda t, tc=tc, pbt=pbt: t.transpose(
                    pbt[:, 0:128], CHK[:, tc * 128:tc * 128 + 128], self.ident), inc=self.sT)
                self.wait("vector", ev)
                self.dma_wait("vector", "gpsimd")
                evv = self.op("vector", lambda v, tc=tc, pbt=pbt: v.tensor_copy(
                    GD[:, (tc % 8) * 128:(tc % 8) * 128 + 128], pbt[:, 0:128]),
                    inc=self.sV)
                self.wait("tensor", evv)
                self.wait("gpsimd", evv)
                self.dma("gpsimd",
                         self.sp_g2[tc * 128:(tc + 1) * 128, dc * 128:(dc + 1) * 128],
                         GD[:, (tc % 8) * 128:(tc % 8) * 128 + 128])
        for dc in range(FC):
            def q_evac(tc, pbk, dc=dc):
                self.op("vector", lambda v, tc=tc, dc=dc, pbk=pbk: v.tensor_copy(
                    Q[:, dc * T + tc * 512:dc * T + tc * 512 + 512], pbk[:, 0:512]), inc=self.sV)
                self.drain("vector")
            qkv_slice(dc, q_evac, scale=64.0 ** -0.5)
        for dc in range(FC):
            def k_evac(tc, pbk, dc=dc):
                self.op("vector", lambda v, tc=tc, dc=dc, pbk=pbk: v.tensor_copy(
                    K[:, dc * T + tc * 512:dc * T + tc * 512 + 512], pbk[:, 0:512]), inc=self.sV)
                self.drain("vector")
            qkv_slice(8 + dc, k_evac)
        self.barrier()
        for dc in range(FC):
            dendritic(Q[:, dc * T:(dc + 1) * T], 0)
            dendritic(K[:, dc * T:(dc + 1) * T], 4)
        self.barrier(("tensor", "vector", "sync"))

        TRF = WINF[:, 0:2048][:, 0:T]
        SCR = WINF[:, 2048:4096][:, 0:T]
        self.dma("sync", TRF, self.dram["trampf"][:, 0:T])
        self.dma_wait("vector", "sync")
        self.dma_wait("scalar", "sync")
        CTP = [self.pb[2 + i] for i in range(4)]
        self.fence("vector", "tensor")
        RNEC = 12582912.0
        FLT = self.XN.bitcast(F32)[:, 0:T]
        self.op("vector", lambda v: v.tensor_scalar(
            out=SCR, in0=TRF, scalar1=th100[:, 0:1], scalar2=0.25,
            op0=OP.mult, op1=OP.add))
        self.drain("vector")
        self.op("vector", lambda v: v.tensor_scalar(
            out=FLT, in0=SCR, scalar1=-0.5, scalar2=None, op0=OP.add))
        self.drain("vector")
        self.op("vector", lambda v: v.tensor_scalar(
            out=FLT, in0=FLT, scalar1=RNEC, scalar2=None, op0=OP.add))
        self.drain("vector")
        self.op("vector", lambda v: v.tensor_scalar(
            out=FLT, in0=FLT, scalar1=-RNEC, scalar2=None, op0=OP.add))
        self.drain("vector")
        self.op("vector", lambda v: v.tensor_tensor(
            out=SCR, in0=SCR, in1=FLT, op=OP.subtract))
        self.drain("vector")
        ev = self.op("vector", lambda v: v.nop(), inc=self.sV)
        self.wait("scalar", ev)
        ev = self.op("scalar", lambda s: s.activation(
            out=SCR, in_=SCR, func=AF.Sin, scale=2 * math.pi, bias=self.CCOL[:, 3:4]),
            inc=self.sS)
        self.wait("vector", ev)
        for i in range(T // 512):
            self.op("vector", lambda v, i=i: v.tensor_scalar(
                out=CTP[i][:, 0:512], in0=SCR[:, i * 512:(i + 1) * 512],
                scalar1=-0.2, scalar2=0.5, op0=OP.mult, op1=OP.add))
            self.drain("vector")
        for dc in range(FC):
            self.op("vector", lambda v, dc=dc: v.tensor_scalar(
                out=SCR, in0=TRF, scalar1=g100[:, dc:dc + 1], scalar2=ph025[:, dc:dc + 1],
                op0=OP.mult, op1=OP.add))
            self.drain("vector")
            self.op("vector", lambda v: v.tensor_scalar(
                out=FLT, in0=SCR, scalar1=-0.5, scalar2=None, op0=OP.add))
            self.drain("vector")
            self.op("vector", lambda v: v.tensor_scalar(
                out=FLT, in0=FLT, scalar1=RNEC, scalar2=None, op0=OP.add))
            self.drain("vector")
            self.op("vector", lambda v: v.tensor_scalar(
                out=FLT, in0=FLT, scalar1=-RNEC, scalar2=None, op0=OP.add))
            self.drain("vector")
            self.op("vector", lambda v: v.tensor_tensor(
                out=SCR, in0=SCR, in1=FLT, op=OP.subtract))
            self.drain("vector")
            ev = self.op("vector", lambda v: v.nop(), inc=self.sV)
            self.wait("scalar", ev)
            ev = self.op("scalar", lambda s: s.activation(
                out=SCR, in_=SCR, func=AF.Sin, scale=2 * math.pi, bias=self.CCOL[:, 3:4]),
                inc=self.sS)
            self.wait("vector", ev)
            for i in range(T // 512):
                self.op("vector", lambda v, i=i: v.scalar_tensor_tensor(
                    out=SCR[:, i * 512:(i + 1) * 512], in0=SCR[:, i * 512:(i + 1) * 512],
                    scalar=-0.3, in1=CTP[i][:, 0:512], op0=OP.mult, op1=OP.add))
                self.drain("vector")
            for reg in (Q, K):
                self.op("vector", lambda v, dc=dc, reg=reg: v.tensor_tensor(
                    out=reg[:, dc * T:(dc + 1) * T], in0=reg[:, dc * T:(dc + 1) * T],
                    in1=SCR, op=OP.mult))
                self.drain("vector")
        self.barrier(("tensor", "vector", "scalar", "sync"))

        CTAB = WIN[:, 0:T]
        STAB = WIN[:, T:2 * T]
        SHUF = WIN[:, 2 * T:3 * T]
        self.dma("sync", CTAB, self.dram["l2rope"][0, :, 0:T])
        self.dma("sync", STAB, self.dram["l2rope"][1, :, 0:T])
        self.dma_wait("vector", "sync")
        mask = []
        for i in range(16):
            mask += [2 * i + 1, 2 * i]
        for reg in (Q, K):
            for c in range(FC):
                tile = reg[:, c * T:(c + 1) * T]
                self.op("vector", lambda v, tile=tile: v.stream_shuffle(out=SHUF, in_=tile, mask=mask))
                self.drain("vector")
                self.op("vector", lambda v, tile=tile: v.tensor_tensor(
                    out=tile, in0=tile, in1=CTAB, op=OP.mult))
                self.drain("vector")
                self.op("vector", lambda v: v.tensor_tensor(out=SHUF, in0=SHUF, in1=STAB, op=OP.mult))
                self.drain("vector")
                self.op("vector", lambda v, tile=tile: v.tensor_tensor(
                    out=tile, in0=tile, in1=SHUF, op=OP.add))
                self.drain("vector")
        self.barrier(("tensor", "vector", "sync"))
        for c in range(FC):
            self.dma("gpsimd", self.sp_gate[c * 128:(c + 1) * 128, 0:T], Q[:, c * T:(c + 1) * T])
            self.dma("gpsimd", self.sp_up[c * 128:(c + 1) * 128, 0:T], K[:, c * T:(c + 1) * T])
        self.dma_wait("sync", "gpsimd")
        VL = self.ARB[:, 0:8 * T]
        for tc in range(NT128):
            self.dma("sync", VL[:, tc * 1024:(tc + 1) * 1024],
                     self.sp_g2[tc * 128:(tc + 1) * 128, 0:1024])
        self.dma_wait("tensor", "sync")
        self.dma_wait("vector", "sync")

        QSTG = WIN[:, 0:T]
        KSTG = WIN[:, T:2 * T]
        JNK = WIN[:, 2 * T:3 * T]
        PTT = WIN[:, 3 * T:3 * T + 512]
        OUTTM = WIN[:, 3 * T + 512:3 * T + 512 + 64]
        NQT = NT128
        GSZ = min(8, NQT)
        lo = self.STATS[:, 0:16]
        hi = self.STATS[:, 16:32]
        mid = self.STATS[:, 32:48]
        cnt = self.STATS[:, 48:64]
        dirm = self.STATS[:, 64:80]
        rsum = self.STATS[:, 80:96]
        NITER = 15
        for h in range(16):
            self.fence("sync", "tensor")
            self.dma("sync", QSTG[0:64, :], self.sp_gate[h * 64:h * 64 + 64, 0:T])
            self.dma("sync", KSTG[0:64, :], self.sp_up[h * 64:h * 64 + 64, 0:T])
            self.dma_wait("tensor", "sync")
            self.dma_wait("vector", "sync")
            self.dma_wait("scalar", "sync")
            for grp in range(NQT // GSZ):
                qts = list(range(grp * GSZ, grp * GSZ + GSZ))
                if h == 0:
                    for qi, qt in enumerate(qts):
                        kvalid = 128 * (qt + 1)
                        if kvalid < T:
                            self.op("vector", lambda v, qi=qi, qt=qt, kvalid=kvalid: v.memset(
                                PB[:, qi * T + kvalid:(qi + 1) * T], 0.0))
                            self.drain("vector")
                for qi, qt in enumerate(qts):
                    kvalid = 128 * (qt + 1)
                    for ks in range(0, kvalid, 512):
                        kw = min(512, kvalid - ks)
                        pbk = self.pb[(ks // 512) % 2]
                        self.fence("tensor", "scalar")
                        ev = self.op("tensor", lambda t, qt=qt, ks=ks, kw=kw, pbk=pbk: t.matmul(
                            pbk[:, 0:kw],
                            QSTG[0:64, qt * 128:qt * 128 + 128],
                            KSTG[0:64, ks:ks + kw],
                            start=True, stop=True), inc=self.sT)
                        self.wait("vector", ev)
                        self.wait("scalar", ev)
                        if ks + kw == kvalid:
                            self.op("vector", lambda v, kw=kw, pbk=pbk: v.tensor_tensor(
                                out=pbk[:, kw - 128:kw], in0=pbk[:, kw - 128:kw],
                                in1=self.TRIADD[:, :], op=OP.add), inc=self.sV)
                            self.drain("vector")
                            self.wait("scalar", (self.sV, self.sV.n))
                        evx = self.op("scalar", lambda s, qi=qi, ks=ks, kw=kw, pbk=pbk: s.activation(
                            out=PB[:, qi * T + ks:qi * T + ks + kw], in_=pbk[:, 0:kw],
                            func=AF.Exp), inc=self.sS)
                        self.wait("tensor", evx)
                self.fence("vector", "scalar")
                self.op("vector", lambda v: v.memset(lo[:, 0:GSZ], 1e-18))
                self.drain("vector")
                self.op("vector", lambda v: v.memset(hi[:, 0:GSZ], 1e18))
                self.drain("vector")
                for it in range(NITER):
                    self.op("vector", lambda v: v.tensor_tensor(
                        out=mid[:, 0:GSZ], in0=lo[:, 0:GSZ], in1=hi[:, 0:GSZ], op=OP.mult))
                    self.drain("vector")
                    ev = self.op("vector", lambda v: v.nop(), inc=self.sV)
                    self.wait("scalar", ev)
                    ev = self.op("scalar", lambda s: s.activation(
                        out=mid[:, 0:GSZ], in_=mid[:, 0:GSZ], func=AF.Sqrt), inc=self.sS)
                    self.wait("vector", ev)
                    for qi, qt in enumerate(qts):
                        kvalid = 128 * (qt + 1)
                        self.op("vector", lambda v, qi=qi, kvalid=kvalid: v.tensor_scalar(
                            out=JNK[:, 0:kvalid], in0=PB[:, qi * T:qi * T + kvalid],
                            scalar1=mid[:, qi:qi + 1], scalar2=0.0,
                            op0=OP.is_ge, op1=OP.add, accum_out=cnt[:, qi:qi + 1]))
                        self.drain("vector")
                    self.op("vector", lambda v: v.tensor_scalar(
                        out=dirm[:, 0:GSZ], in0=cnt[:, 0:GSZ], scalar1=KTAR, scalar2=None,
                        op0=OP.is_ge))
                    self.drain("vector")
                    tsel = self.STATS[:, 100:116]
                    self.op("vector", lambda v: v.tensor_tensor(
                        out=tsel[:, 0:GSZ], in0=mid[:, 0:GSZ], in1=lo[:, 0:GSZ],
                        op=OP.subtract))
                    self.drain("vector")
                    self.op("vector", lambda v: v.tensor_tensor(
                        out=tsel[:, 0:GSZ], in0=tsel[:, 0:GSZ], in1=dirm[:, 0:GSZ],
                        op=OP.mult))
                    self.drain("vector")
                    self.op("vector", lambda v: v.tensor_tensor(
                        out=lo[:, 0:GSZ], in0=lo[:, 0:GSZ], in1=tsel[:, 0:GSZ],
                        op=OP.add))
                    self.drain("vector")
                    self.op("vector", lambda v: v.tensor_tensor(
                        out=tsel[:, 0:GSZ], in0=hi[:, 0:GSZ], in1=mid[:, 0:GSZ],
                        op=OP.subtract))
                    self.drain("vector")
                    self.op("vector", lambda v: v.tensor_tensor(
                        out=tsel[:, 0:GSZ], in0=tsel[:, 0:GSZ], in1=dirm[:, 0:GSZ],
                        op=OP.mult))
                    self.drain("vector")
                    self.op("vector", lambda v: v.tensor_tensor(
                        out=hi[:, 0:GSZ], in0=mid[:, 0:GSZ], in1=tsel[:, 0:GSZ],
                        op=OP.add))
                    self.drain("vector")
                for qi, qt in enumerate(qts):
                    kvalid = 128 * (qt + 1)
                    ev = self.op("vector", lambda v, qi=qi, kvalid=kvalid: v.scalar_tensor_tensor(
                        out=PB[:, qi * T:qi * T + kvalid], in0=PB[:, qi * T:qi * T + kvalid],
                        scalar=lo[:, qi:qi + 1], in1=PB[:, qi * T:qi * T + kvalid],
                        op0=OP.is_ge, op1=OP.mult, accum_out=rsum[:, qi:qi + 1]), inc=self.sV)
                    self.drain("vector")
                    self.wait("tensor", ev)
                    ppv = self.pb[2]
                    self.fence("tensor", "vector")
                    for kc in range(qt + 1):
                        pbt = self.pt0 if kc % 2 == 0 else self.pt1
                        ev = self.op("tensor", lambda t, qi=qi, kc=kc, pbt=pbt: t.transpose(
                            pbt[:, 0:128], PB[:, qi * T + kc * 128:qi * T + kc * 128 + 128],
                            self.ident), inc=self.sT)
                        self.wait("vector", ev)
                        evv = self.op("vector", lambda v, kc=kc, pbt=pbt: v.tensor_copy(
                            PTT[:, (kc % 4) * 128:(kc % 4) * 128 + 128], pbt[:, 0:128]),
                            inc=self.sV)
                        self.wait("tensor", evv)
                        self.op("tensor", lambda t, kc=kc, qt=qt, h=h, ppv=ppv: t.matmul(
                            ppv[:, 0:64],
                            PTT[:, (kc % 4) * 128:(kc % 4) * 128 + 128],
                            VL[:, kc * 1024 + h * 64:kc * 1024 + h * 64 + 64],
                            start=(kc == 0), stop=(kc == qt)),
                            inc=self.sT if kc == qt else None)
                    self.wait("vector", (self.sT, self.sT.n))
                    self.op("vector", lambda v, qi=qi: v.reciprocal(
                        out=self.STATS[:, 96:97], in_=rsum[:, qi:qi + 1]))
                    self.drain("vector")
                    ev = self.op("vector", lambda v, ppv=ppv: v.tensor_scalar(
                        out=OUTTM, in0=ppv[:, 0:64], scalar1=self.STATS[:, 96:97],
                        scalar2=None, op0=OP.mult), inc=self.sV)
                    self.drain("vector")
                    self.wait("tensor", ev)
                    evt = self.op("tensor", lambda t: t.transpose(
                        self.pt0[0:64, 0:128], OUTTM, self.ident), inc=self.sT)
                    self.wait("vector", evt)
                    evv = self.op("vector", lambda v: v.tensor_copy(
                        JNK[0:64, 0:128], self.pt0[0:64, 0:128]), inc=self.sV)
                    self.wait("gpsimd", evv)
                    self.dma("gpsimd",
                             self.sp_attn[h * 64:h * 64 + 64, qt * 128:qt * 128 + 128],
                             JNK[0:64, 0:128])
                    self.dma_wait("vector", "gpsimd")
        self.barrier(("tensor", "vector", "scalar", "gpsimd", "sync"))

        WSLOT = WIN
        ev = self.load_cast_w("L2projT", WSLOT, WSTG)
        self.wait("tensor", ev)
        TIN4 = self.ARB[:, 0:2048]
        for tc in range(NT):
            self.fence("tensor", "vector")
            lasts = {}
            for dc in range(FC):
                ti = TIN4[:, (dc % 2) * 1024:(dc % 2) * 1024 + 512]
                self.fence("sync", "tensor")
                self.dma("sync", ti, self.sp_attn[dc * 128:(dc + 1) * 128, tc * 512:(tc + 1) * 512])
                self.dma_wait("tensor", "sync")
                for cc in range(FC):
                    lasts[cc] = self.op("tensor", lambda t, dc=dc, cc=cc, ti=ti: t.matmul(
                        self.pb[cc][:, 0:512],
                        WSLOT[:, dc * 1024 + cc * 128:dc * 1024 + cc * 128 + 128],
                        ti, start=(dc == 0), stop=(dc == FC - 1)),
                        inc=self.sT if dc == FC - 1 else None)
            for cc in range(FC):
                self.wait("vector", lasts[cc])
                self.op("vector", lambda v, cc=cc, tc=tc: v.scalar_tensor_tensor(
                    out=self.X[:, cc * T + tc * 512: cc * T + tc * 512 + 512],
                    in0=self.pb[cc][:, 0:512], scalar=pbias[:, cc:cc + 1],
                    in1=self.X[:, cc * T + tc * 512: cc * T + tc * 512 + 512],
                    op0=OP.add, op1=OP.add), inc=self.sV)
                self.drain("vector")
        self.barrier()

    def l3_attn(self):
        T, NT, NT128, FC = self.T, self.NT, self.NT128, self.FC
        A = self.ARB[:, 0:8 * T]
        B = self.ARB[:, 8 * T:16 * T]
        WSLOT = self.ARB[:, 16 * T:16 * T + 8192]
        WSTG = self.ARF[:, 0:1024]
        MIXS = self.ARF.bitcast(BF16)[:, 2048:2048 + T]
        TH = T // 2
        sp_sr = self.sp_g2
        sp_rw = self.sp_o

        VCOL = self.STATS[:, 140:180]
        tmk, tmv, tmr = VCOL[:, 0:8], VCOL[:, 8:16], VCOL[:, 16:24]
        ewc, euc = VCOL[:, 24:32], VCOL[:, 32:40]
        self.dma("sync", tmk, self.dram["L3tmk"][:, :])
        self.dma("sync", tmv, self.dram["L3tmv"][:, :])
        self.dma("sync", tmr, self.dram["L3tmr"][:, :])
        self.dma("sync", ewc, self.dram["L3td"][:, :])
        self.dma("sync", euc, self.dram["L3tf"][:, :])
        self.dma_wait("scalar", "sync")
        self.dma_wait("vector", "sync")
        self.op("scalar", lambda s: s.activation(out=ewc, in_=ewc, func=AF.Exp))
        self.drain("scalar")
        self.op("scalar", lambda s: s.activation(out=ewc, in_=ewc, func=AF.Exp, scale=-1.0))
        self.drain("scalar")
        ev = self.op("scalar", lambda s: s.activation(out=euc, in_=euc, func=AF.Exp), inc=self.sS)
        self.wait("vector", ev)
        self.wait("tensor", ev)

        def mix(dstregion, tmcol):
            for c in range(FC):
                xn_c = self.XN[:, c * T:(c + 1) * T]
                dst_c = dstregion[:, c * T:(c + 1) * T]
                self.op("vector", lambda v, xn_c=xn_c: v.tensor_tensor(
                    out=MIXS[:, 1:T], in0=xn_c[:, 1:T], in1=xn_c[:, 0:T - 1],
                    op=OP.subtract))
                self.drain("vector")
                self.op("vector", lambda v, xn_c=xn_c, dst_c=dst_c, tmcol=tmcol, c=c: v.scalar_tensor_tensor(
                    out=dst_c[:, 1:T], in0=MIXS[:, 1:T], scalar=tmcol[:, c:c + 1],
                    in1=xn_c[:, 0:T - 1], op0=OP.mult, op1=OP.add))
                self.drain("vector")
                self.op("vector", lambda v, xn_c=xn_c, dst_c=dst_c, tmcol=tmcol, c=c: v.tensor_scalar(
                    out=dst_c[:, 0:1], in0=xn_c[:, 0:1], scalar1=tmcol[:, c:c + 1],
                    scalar2=None, op0=OP.mult))
                self.drain("vector")

        def project_from_A(wname, evac):
            ev = self.load_cast_w(wname, WSLOT, WSTG)
            self.wait("tensor", ev)
            for dc in range(FC):
                for tc in range(NT):
                    pbk = self.pb[(dc * NT + tc) % 2]
                    self.fence("tensor", "vector")
                    self.fence("tensor", "scalar")
                    last = None
                    for i in range(FC):
                        last = self.op("tensor", lambda t, k=i, i0=i, dc=dc, tc=tc, pbk=pbk: t.matmul(
                            pbk[:, 0:512],
                            WSLOT[:, k * 1024 + dc * 128:k * 1024 + dc * 128 + 128],
                            A[:, k * T + tc * 512:k * T + tc * 512 + 512],
                            start=(i0 == 0), stop=(i0 == FC - 1)),
                            inc=self.sT if i == FC - 1 else None)
                    evac(dc, tc, pbk, last)

        mix(A, tmr)
        self.barrier(("tensor", "vector", "sync"))

        def evac_sr(dc, tc, pbk, last):
            self.wait("scalar", last)
            st = MIXS[:, 0:512]
            self.dma_wait("scalar", "gpsimd")
            ev = self.op("scalar", lambda s, st=st, pbk=pbk: s.activation(
                out=st, in_=pbk[:, 0:512], func=AF.Sigmoid), inc=self.sS)
            self.wait("tensor", ev)
            self.wait("gpsimd", ev)
            self.dma("gpsimd", sp_sr[dc * 128:(dc + 1) * 128, tc * 512:(tc + 1) * 512], st)
        project_from_A("L3WrT", evac_sr)
        self.barrier(("tensor", "vector", "scalar", "gpsimd", "sync"))

        mix(A, tmk)
        self.barrier(("tensor", "vector", "sync"))

        def evac_ek(dc, tc, pbk, last):
            self.wait("scalar", last)
            ev = self.op("scalar", lambda s, dc=dc, tc=tc, pbk=pbk: s.activation(
                out=B[:, dc * T + tc * 512:dc * T + tc * 512 + 512],
                in_=pbk[:, 0:512], func=AF.Exp), inc=self.sS)
            self.wait("tensor", ev)
        project_from_A("L3WkT", evac_ek)
        self.barrier(("tensor", "vector", "scalar", "sync"))
        self.dump("l3_ek", B, 8 * T)
        self.dump("l3_xk", A, 8 * T)

        mix(A, tmv)
        self.barrier(("tensor", "vector", "sync"))
        ev = self.load_cast_w("L3WvT", WSLOT, WSTG)
        self.wait("tensor", ev)
        ekf = self.ARF[:, 0:TH]
        vf = self.ARF[:, TH:2 * TH]
        As = self.ARF[:, 2 * TH:3 * TH]
        Bs = self.ARF[:, 3 * TH:4 * TH]
        ewt = self.ARF[:, 4 * TH:5 * TH]
        STA = self.STATS[:, 104:120]
        SRT = self.ARF.bitcast(BF16)[:, 10 * TH:11 * TH]
        RWT = self.ARF.bitcast(BF16)[:, 11 * TH:12 * TH]
        self.op("vector", lambda v: v.memset(STA, 0.0))
        self.drain("vector")
        for dc in range(FC):
            self.op("vector", lambda v: v.memset(ewt, 1.0))
            self.drain("vector")
            self.op("vector", lambda v, dc=dc: v.tensor_scalar(
                out=ewt, in0=ewt, scalar1=ewc[:, dc:dc + 1], scalar2=None, op0=OP.mult),
                inc=self.sV)
            self.drain("vector")
            SW = min(512, TH)
            for th in range(2):
                self.fence("tensor", "vector")
                for sub in range(TH // SW):
                    pbk = self.pb[2 + sub % 2]
                    last = None
                    for i in range(FC):
                        last = self.op("tensor", lambda t, k=i, i0=i, dc=dc, th=th, sub=sub, pbk=pbk: t.matmul(
                            pbk[:, 0:SW],
                            WSLOT[:, k * 1024 + dc * 128:k * 1024 + dc * 128 + 128],
                            A[:, k * T + th * TH + sub * SW:k * T + th * TH + sub * SW + SW],
                            start=(i0 == 0), stop=(i0 == FC - 1)),
                            inc=self.sT if i == FC - 1 else None)
                    self.wait("vector", last)
                    self.op("vector", lambda v, sub=sub, pbk=pbk: v.tensor_copy(
                        vf[:, sub * SW:sub * SW + SW], pbk[:, 0:SW]))
                    self.drain("vector")
                self.op("vector", lambda v, dc=dc, th=th: v.tensor_copy(
                    ekf, B[:, dc * T + th * TH:dc * T + th * TH + TH]))
                self.drain("vector")
                self.op("vector", lambda v: v.tensor_tensor(
                    out=vf, in0=ekf, in1=vf, op=OP.mult))
                self.drain("vector")
                self.op("vector", lambda v, dc=dc: v.tensor_tensor_scan(
                    out=As, data0=ewt, data1=vf, initial=STA[:, 2 * dc:2 * dc + 1],
                    op0=OP.mult, op1=OP.add))
                self.drain("vector")
                self.op("vector", lambda v, dc=dc: v.tensor_tensor_scan(
                    out=Bs, data0=ewt, data1=ekf, initial=STA[:, 2 * dc + 1:2 * dc + 2],
                    op0=OP.mult, op1=OP.add))
                self.drain("vector")
                self.op("vector", lambda v, dc=dc: v.scalar_tensor_tensor(
                    out=vf[:, 0:1], in0=vf[:, 0:1], scalar=euc[:, dc:dc + 1],
                    in1=STA[:, 2 * dc:2 * dc + 1], op0=OP.mult, op1=OP.add))
                self.drain("vector")
                self.op("vector", lambda v, dc=dc: v.scalar_tensor_tensor(
                    out=vf[:, 1:TH], in0=vf[:, 1:TH], scalar=euc[:, dc:dc + 1],
                    in1=As[:, 0:TH - 1], op0=OP.mult, op1=OP.add))
                self.drain("vector")
                self.op("vector", lambda v, dc=dc: v.scalar_tensor_tensor(
                    out=ekf[:, 0:1], in0=ekf[:, 0:1], scalar=euc[:, dc:dc + 1],
                    in1=STA[:, 2 * dc + 1:2 * dc + 2], op0=OP.mult, op1=OP.add))
                self.drain("vector")
                self.op("vector", lambda v, dc=dc: v.scalar_tensor_tensor(
                    out=ekf[:, 1:TH], in0=ekf[:, 1:TH], scalar=euc[:, dc:dc + 1],
                    in1=Bs[:, 0:TH - 1], op0=OP.mult, op1=OP.add))
                self.drain("vector")
                self.op("vector", lambda v, dc=dc: v.tensor_copy(
                    STA[:, 2 * dc:2 * dc + 1], As[:, TH - 1:TH]))
                self.drain("vector")
                self.op("vector", lambda v, dc=dc: v.tensor_copy(
                    STA[:, 2 * dc + 1:2 * dc + 2], Bs[:, TH - 1:TH]))
                self.drain("vector")
                self.op("vector", lambda v: v.reciprocal(out=ekf, in_=ekf))
                self.drain("vector")
                self.op("vector", lambda v: v.tensor_tensor(
                    out=vf, in0=vf, in1=ekf, op=OP.mult))
                self.drain("vector")
                self.dma_wait("sync", "gpsimd")
                self.fence("sync", "vector")
                self.dma("sync", SRT, sp_sr[dc * 128:(dc + 1) * 128, th * TH:(th + 1) * TH])
                self.dma_wait("vector", "sync")
                self.op("vector", lambda v: v.tensor_tensor(
                    out=RWT, in0=vf, in1=SRT, op=OP.mult), inc=self.sV)
                self.drain("vector")
                evv = self.op("vector", lambda v: v.nop(), inc=self.sV)
                self.wait("gpsimd", evv)
                self.dma("gpsimd", sp_rw[dc * 128:(dc + 1) * 128, th * TH:(th + 1) * TH], RWT)
                self.dma_wait("vector", "gpsimd")
        self.barrier(("tensor", "vector", "scalar", "gpsimd", "sync"))

        ev = self.load_cast_w("L3WoT", WSLOT, WSTG)
        self.wait("tensor", ev)
        TIN3 = self.ARB[:, 0:2048]
        for tc in range(NT):
            self.fence("tensor", "vector")
            lasts = {}
            for dc in range(FC):
                ti = TIN3[:, (dc % 2) * 1024:(dc % 2) * 1024 + 512]
                self.fence("sync", "tensor")
                self.dma("sync", ti, sp_rw[dc * 128:(dc + 1) * 128, tc * 512:(tc + 1) * 512])
                self.dma_wait("tensor", "sync")
                for cc in range(FC):
                    lasts[cc] = self.op("tensor", lambda t, dc=dc, cc=cc, ti=ti: t.matmul(
                        self.pb[cc][:, 0:512],
                        WSLOT[:, dc * 1024 + cc * 128:dc * 1024 + cc * 128 + 128],
                        ti, start=(dc == 0), stop=(dc == FC - 1)),
                        inc=self.sT if dc == FC - 1 else None)
            for cc in range(FC):
                self.wait("vector", lasts[cc])
                self.op("vector", lambda v, cc=cc, tc=tc: v.tensor_tensor(
                    out=self.X[:, cc * T + tc * 512: cc * T + tc * 512 + 512],
                    in0=self.X[:, cc * T + tc * 512: cc * T + tc * 512 + 512],
                    in1=self.pb[cc][:, 0:512], op=OP.add), inc=self.sV)
                self.drain("vector")
        self.barrier()

    def emit_output(self):
        T = self.T
        self.barrier()
        for c in range(self.FC):
            self.dma("sync", self.yT[c * 128:(c + 1) * 128, :], self.X[:, c * T:(c + 1) * T])
        self.dma_wait("sync", "sync")
        self.dma_wait("gpsimd", "gpsimd")


def build_nc(T=2048, layers=LAYERS):
    import contextlib
    nc = bass.Bass("TRN2", target_bir_lowering=False)
    with contextlib.ExitStack() as stack:
        mk = MK(nc, T, layers, stack)
        mk.declare()
        mk.init_loads()
        if layers >= 1:
            mk.layernorm()
            mk.l0_attn()
            mk.qdq_xn()
            mk.qdq_ffn("L0wg", "L0wu", "L0wd",
                       (mk.sp_gate, mk.sp_up, mk.sp_g2, mk.sp_o))
        if layers >= 2:
            mk.layernorm()
            if "l1_attn" not in SKIP:
                mk.l1_attn()
            if "l1_ffn" not in SKIP:
                mk.l1_ffn()
        if layers >= 3:
            mk.layernorm()
            if "l2_ffn" not in SKIP:
                mk.l2_ffn()
            if "l2_attn" not in SKIP:
                mk.l2_attn()
        if layers >= 4:
            mk.layernorm()
            if "l3_attn" not in SKIP:
                mk.l3_attn()
            if "l3_ffn" not in SKIP:
                mk.qdq_xn()
                mk.qdq_ffn("L3wg", "L3wu", "L3wd",
                           (mk.sp_gate, mk.sp_up, mk.sp_g2, mk.sp_o))
        mk.emit_output()
        with nc.Block() as block:
            mk.replay(block)
    return nc



def make_consts(T):
    import ml_dtypes
    bf = ml_dtypes.bfloat16
    consts = {}
    consts["ident"] = np.eye(128, dtype=np.float32).astype(bf)
    consts["identf"] = np.eye(128, dtype=np.float32)
    consts["ones_bf"] = np.ones((128, 128), np.float32).astype(bf)
    consts["ones_f"] = np.ones((1, 128), np.float32)
    consts["triadd"] = np.where(np.tril(np.ones((128, 128), bool)), 0.0, -1e30).astype(np.float32)
    consts["trampf"] = np.broadcast_to(np.arange(T, dtype=np.float32), (128, T)).copy()

    t = np.arange(T, dtype=np.float64)

    def rope_tables(rot, head, base=10000.0):
        c = np.zeros((128, T)); s = np.zeros((128, T))
        inv = 1.0 / base ** (np.arange(0, rot, 2, dtype=np.float64) / rot)
        for r in range(128):
            l = r % head
            if l < rot:
                f = t * inv[l // 2]
                sgn = -1.0 if (l % 2 == 0) else 1.0
                c[r], s[r] = np.cos(f), sgn * np.sin(f)
            else:
                c[r], s[r] = 1.0, 0.0
        return np.stack([c, s]).astype(np.float32).astype(bf)

    consts["l1rope"] = rope_tables(32, 64)
    consts["l2rope"] = rope_tables(64, 64)
    return consts


def make_inmaps(inputs, T):
    ct = np.ascontiguousarray
    f32 = lambda a: np.asarray(a, dtype=np.float32)
    shared = {}
    shared["L0WqT"] = ct(f32(inputs["L0_attn_Wq"]).T)
    shared["L0WkT"] = ct(f32(inputs["L0_attn_Wk"]).T)
    shared["L0WvT"] = ct(f32(inputs["L0_attn_Wv"]).T)
    shared["L0WoT"] = ct(f32(inputs["L0_attn_Wo"]).T)
    shared["L0nrm"] = ct(f32(inputs["L0_attn_norm_const"]).reshape(1, 16))
    shared["L0wg"] = f32(inputs["L0_ffn_w_gate"])
    shared["L0wu"] = f32(inputs["L0_ffn_w_up"])
    shared["L0wd"] = f32(inputs["L0_ffn_w_down"])
    shared["L1Wq"] = f32(inputs["L1_attn_Wq"])
    shared["L1Wk"] = f32(inputs["L1_attn_Wk"])
    shared["L1Wv"] = f32(inputs["L1_attn_Wv"])
    shared["L1Wo"] = f32(inputs["L1_attn_Wo"])
    shared["L1bo"] = ct(f32(inputs["L1_attn_bo"]).reshape(8, 128).T)
    shared["L1wg"] = f32(inputs["L1_ffn_w_gate"])
    shared["L1wu"] = f32(inputs["L1_ffn_w_up"])
    shared["L1wd"] = f32(inputs["L1_ffn_w_down"])
    shared["L2qkvT"] = ct(f32(inputs["L2_attn_qkv_w"]).T)
    shared["L2projT"] = ct(f32(inputs["L2_attn_proj_w"]).T)
    shared["L2pb"] = ct(f32(inputs["L2_attn_proj_b"]).reshape(8, 128).T)
    shared["L2ker"] = np.concatenate([
        f32(inputs["L2_attn_kernel_q"]).ravel(),
        f32(inputs["L2_attn_kernel_k"]).ravel(),
        f32(inputs["L2_attn_kernel_v"]).ravel()]).reshape(1, 12)
    shared["L2gam"] = ct(f32(inputs["L2_attn_gamma_freq"]).reshape(8, 128).T)
    shared["L2phs"] = ct(f32(inputs["L2_attn_phase"]).reshape(8, 128).T)
    shared["L2thf"] = f32(inputs["L2_attn_theta_freq"]).reshape(1, 1)
    shared["L2th"] = f32(inputs["L2_ffn_theta"])
    shared["L2ph"] = f32(inputs["L2_ffn_phi"])
    shared["L2thn"] = f32(inputs["L2_ffn_theta_noise"])
    shared["L2phn"] = f32(inputs["L2_ffn_phi_noise"])
    shared["L3td"] = ct(f32(inputs["L3_attn_time_decay"]).reshape(8, 128).T)
    shared["L3tf"] = ct(f32(inputs["L3_attn_time_first"]).reshape(8, 128).T)
    shared["L3tmk"] = ct(f32(inputs["L3_attn_time_mix_k"]).reshape(8, 128).T)
    shared["L3tmv"] = ct(f32(inputs["L3_attn_time_mix_v"]).reshape(8, 128).T)
    shared["L3tmr"] = ct(f32(inputs["L3_attn_time_mix_r"]).reshape(8, 128).T)
    shared["L3WkT"] = ct(f32(inputs["L3_attn_Wk"]).T)
    shared["L3WvT"] = ct(f32(inputs["L3_attn_Wv"]).T)
    shared["L3WrT"] = ct(f32(inputs["L3_attn_Wr"]).T)
    shared["L3WoT"] = ct(f32(inputs["L3_attn_Wo"]).T)
    shared["L3wg"] = f32(inputs["L3_ffn_w_gate"])
    shared["L3wu"] = f32(inputs["L3_ffn_w_up"])
    shared["L3wd"] = f32(inputs["L3_ffn_w_down"])
    shared.update(make_consts(T))
    shared = {k: np.ascontiguousarray(v) for k, v in shared.items()}

    x = np.asarray(inputs["L0_attn_x"])
    in_maps = []
    for c in range(NCORES):
        m = dict(shared)
        m["xT"] = np.ascontiguousarray(f32(x[c]).T)
        in_maps.append(m)
    return in_maps


def kernel(**inputs):
    T = int(np.asarray(inputs["L0_attn_x"]).shape[1])
    in_maps = make_inmaps(inputs, T)
    nc = build_nc(T=T, layers=LAYERS)
    res = run_bass_kernel_spmd(nc, in_maps, core_ids=list(range(NCORES)))
    outs = [np.ascontiguousarray(np.asarray(r["yT"]).T) for r in res.results]
    return np.stack(outs).astype(np.float32)


# version 06
# speedup vs baseline: 1.0660x; 1.0660x over previous
import math
import numpy as np
import concourse.bass as bass
import concourse.mybir as mybir
from concourse.alu_op_type import AluOpType
from concourse.bass_utils import run_bass_kernel_spmd

F32 = mybir.dt.float32
BF16 = mybir.dt.bfloat16
AF = mybir.ActivationFunctionType
AX = mybir.AxisListType.X
OP = AluOpType

RNE = 1.5 * 2 ** 23
D = 1024
NCORES = 8
LAYERS = 4
SKIP = set()
DBG = None

ENGS = ("tensor", "vector", "scalar", "gpsimd", "sync")


class MSem:
    def __init__(self, handle, name):
        self.h = handle
        self.name = name
        self.n = 0


class MK:
    def __init__(self, nc, T, layers, stack):
        self.nc = nc
        self.T = T
        self.layers = layers
        self.stack = stack
        self.NT = T // 512
        self.NT128 = T // 128
        self.FC = D // 128
        self.q = {e: [] for e in ENGS}
        self.dram = {}
        self.cc_idx = 0
        self._flushed = {"sync": -1, "gpsimd": -1}

    def sem(self, name):
        h = self.stack.enter_context(self.nc.semaphore(name))
        return MSem(h, name)

    def wait(self, eng, ev):
        if ev is None:
            return
        s, v = ev
        self.q[eng].append(("w", s.h, v))

    def op(self, eng, fn, inc=None, inck=1):
        incs = []
        ev = None
        if inc is not None:
            inc.n += inck
            incs.append((inc.h, inck))
            ev = (inc, inc.n)
        self.q[eng].append(("i", fn, incs))
        return ev

    def drain(self, eng):
        self.q[eng].append(("i", lambda e: e.drain(), []))

    def dma(self, eng, out, in_):
        sem = self.sD if eng == "sync" else self.sG
        return self.op(eng, lambda e, o=out, i=in_: e.dma_start(out=o, in_=i),
                       inc=sem, inck=16)

    def dump(self, name, region, cols):
        if DBG == name:
            self.barrier()
            self.dma("sync", self.dbg[:, 0:cols], region[:, 0:cols])
            self.barrier()

    def dump_x(self, name):
        if DBG == name:
            self.barrier()
            T = self.T
            for c in range(self.FC):
                self.op("vector", lambda v, c=c: v.tensor_copy(
                    self.ARB[:, 0:T] if False else self.dbgbf[:, c * T:(c + 1) * T],
                    self.X[:, c * T:(c + 1) * T]))
                self.drain("vector")
            self.barrier()
            self.dma("sync", self.dbg[:, 0:self.FC * T], self.dbgbf[:, 0:self.FC * T])
            self.barrier()

    def dma_wait(self, waiter, queue_eng):
        sem = self.sD if queue_eng == "sync" else self.sG
        if self._flushed[queue_eng] != sem.n:
            self.wait(queue_eng, (sem, sem.n))
            self._flushed[queue_eng] = sem.n
        if waiter != queue_eng:
            self.wait(waiter, (sem, sem.n))

    def fence(self, waiter, producer):
        ev = self.op(producer, lambda e: e.drain(), inc=self.esem[producer])
        self.wait(waiter, ev)

    def barrier(self, engs=ENGS):
        if "sync" in engs:
            self.dma_wait("sync", "sync")
        if "gpsimd" in engs:
            self.dma_wait("gpsimd", "gpsimd")
        evs = []
        for e in engs:
            evs.append(self.op(e, lambda en: en.drain(), inc=self.esem[e]))
        for e in engs:
            for ev in evs:
                if ev[0] is not self.esem[e]:
                    self.wait(e, ev)
            for ds in (self.sD, self.sG):
                self.wait(e, (ds, ds.n))

    def replay(self, block):
        for e in ENGS:
            items = self.q[e]

            def body(eng, items=items):
                for it in items:
                    if it[0] == "w":
                        eng.wait_ge(it[1], it[2])
                    else:
                        ins = it[1](eng)
                        for (h, k) in it[2]:
                            ins.then_inc(h, k)

            getattr(block, e)(body)

    def din(self, name, shape, dtype=F32):
        self.dram[name] = self.nc.dram_tensor(name, shape, dtype, kind="ExternalInput").ap()
        return self.dram[name]

    def declare(self):
        nc, T = self.nc, self.T
        self.xT = self.din("xT", [D, T])
        self.yT = nc.dram_tensor("yT", [D, T], F32, kind="ExternalOutput").ap()
        self.dbg = nc.dram_tensor("dbg", [128, 40960], BF16, kind="ExternalOutput").ap() if DBG else None

        self.din("L0WqT", [D, D]); self.din("L0WkT", [D, D])
        self.din("L0WvT", [D, D]); self.din("L0WoT", [D, D])
        self.din("L0nrm", [1, 16])
        self.din("L0wg", [3072, D]); self.din("L0wu", [3072, D]); self.din("L0wd", [D, 3072])
        self.din("L1Wq", [D, D]); self.din("L1Wk", [D, 512])
        self.din("L1Wv", [D, 512]); self.din("L1Wo", [D, D]); self.din("L1bo", [128, 8])
        self.din("L1wg", [2816, D]); self.din("L1wu", [2816, D]); self.din("L1wd", [D, 2816])
        self.din("L2qkvT", [D, 3072]); self.din("L2projT", [D, D]); self.din("L2pb", [128, 8])
        self.din("L2ker", [1, 12]); self.din("L2gam", [128, 8]); self.din("L2phs", [128, 8])
        self.din("L2thf", [1, 1])
        self.din("L2th", [D, D]); self.din("L2ph", [D, D])
        self.din("L2thn", [D, D]); self.din("L2phn", [D, D])
        self.din("L3td", [128, 8]); self.din("L3tf", [128, 8])
        self.din("L3tmk", [128, 8]); self.din("L3tmv", [128, 8]); self.din("L3tmr", [128, 8])
        self.din("L3WkT", [D, D]); self.din("L3WvT", [D, D])
        self.din("L3WrT", [D, D]); self.din("L3WoT", [D, D])
        self.din("L3wg", [3072, D]); self.din("L3wu", [3072, D]); self.din("L3wd", [D, 3072])
        self.din("ident", [128, 128], BF16)
        self.din("identf", [128, 128], F32)
        self.din("ones_bf", [128, 128], BF16)
        self.din("ones_f", [1, 128], F32)
        self.din("triadd", [128, 128], F32)
        self.din("l1rope", [2, 128, T], BF16)
        self.din("l2rope", [2, 128, T], BF16)
        self.din("trampf", [128, T], F32)

        WSP = max(T, 1024)
        self.sp_gate = nc.dram_tensor("sp_gate", [3072, WSP], BF16).ap()
        self.sp_up = nc.dram_tensor("sp_up", [3072, WSP], BF16).ap()
        self.sp_g2 = nc.dram_tensor("sp_g2", [3072, WSP], BF16).ap()
        self.sp_o = nc.dram_tensor("sp_o", [3072, WSP], BF16).ap()
        self.sp_attn = nc.dram_tensor("sp_attn", [D, T], BF16).ap()
        self.cc_in = [nc.dram_tensor(f"cc_in{i}", [1, 2], F32).ap() for i in range(8)]
        self.cc_out = [nc.dram_tensor(f"cc_out{i}", [1, 2], F32).ap() for i in range(8)]

        sb = lambda n, s, d=BF16: self.stack.enter_context(nc.sbuf_tensor(n, s, d))
        ps = lambda n, s, d=F32: self.stack.enter_context(nc.psum_tensor(n, s, d))

        self.X = sb("X", [128, self.FC * T], F32)
        self.XN = sb("XN", [128, self.FC * T], BF16)
        self.ARB = sb("ARB", [128, 40960], BF16)
        self.ARF = sb("ARF", [128, 6144], F32)
        self.CONST = sb("CONST", [128, 256], BF16)
        self.ONESF = sb("ONESF", [1, 128], F32)
        self.TRIADD = sb("TRIADD", [128, 128], F32)
        self.IDENTF = sb("IDENTF", [128, 128], F32)
        self.STATS = sb("STATS", [128, 256], F32)
        self.GMAX = sb("GMAX", [128, 32], F32)
        self.CCOL = sb("CCOL", [128, 8], F32)

        self.dbgbf = self.ARB[:, 0:self.FC * T]
        self.ident = self.CONST[:, 0:128]
        self.ones_bf = self.CONST[:, 128:256]

        self.pb = [ps(f"pb{i}", [128, 512], F32) for i in range(8)]
        self.pt0 = self.pb[6].bitcast(BF16)
        self.pt1 = self.pb[7].bitcast(BF16)

        self.esem = {e: self.sem(f"S_{e}") for e in ENGS}
        self.sD = self.sem("S_dma_sync")
        self.sG = self.sem("S_dma_gps")
        self.sT = self.esem["tensor"]
        self.sV = self.esem["vector"]
        self.sS = self.esem["scalar"]
        self.sP = self.esem["gpsimd"]
        self.sCC = self.sem("S_cc")

    def init_loads(self):
        T = self.T
        for c in range(self.FC):
            self.dma("sync", self.X[:, c * T:(c + 1) * T], self.xT[c * 128:(c + 1) * 128, :])
        self.dma("sync", self.ident, self.dram["ident"][:, :])
        self.dma("sync", self.ones_bf, self.dram["ones_bf"][:, :])
        self.dma("sync", self.ONESF[:, :], self.dram["ones_f"][:, :])
        self.dma("sync", self.TRIADD[:, :], self.dram["triadd"][:, :])
        self.dma("sync", self.IDENTF[:, :], self.dram["identf"][:, :])
        self.op("vector", lambda v: v.memset(self.CCOL[:, 0:1], 1e-5))
        self.op("vector", lambda v: v.memset(self.CCOL[:, 1:2], math.pi / 2))
        self.op("vector", lambda v: v.memset(self.CCOL[:, 2:3], 0.0))
        self.op("vector", lambda v: v.memset(self.CCOL[:, 3:4], -math.pi))
        self.drain("vector")
        self.barrier()

    def replicate_cols(self, src_row_ap, n, dst_col):
        pbank = self.pb[5]
        self.fence("tensor", "vector")
        evm = self.op("tensor", lambda t, s=src_row_ap, n=n, p=pbank: t.matmul(
            p[:, 0:n], self.ONESF[:, :], s, start=True, stop=True), inc=self.sT)
        self.wait("vector", evm)
        evv = self.op("vector", lambda v, n=n, d=dst_col, p=pbank: v.tensor_copy(
            self.GMAX[:, d:d + n], p[:, 0:n]), inc=self.sV)
        self.wait("tensor", evv)
        return evv

    def col_max_to_scalar(self, col_ap, dst_col):
        pbank = self.pb[5]
        self.fence("tensor", "vector")
        ev = self.op("tensor", lambda t, c=col_ap, p=pbank: t.transpose(
            p[0:1, 0:128], c, self.IDENTF[:, :]), inc=self.sT)
        self.wait("vector", ev)
        evv = self.op("vector", lambda v, d=dst_col, p=pbank: v.tensor_reduce(
            out=self.STATS[0:1, d:d + 1], in_=p[0:1, 0:128], axis=AX, op=OP.max),
            inc=self.sV)
        self.wait("tensor", evv)
        return evv

    def part_tree_max(self, cols, dst_col):
        c0 = cols[0]
        for c in cols[1:]:
            self.op("vector", lambda v, a=c0, b=c: v.tensor_tensor(
                out=self.STATS[:, a:a + 1], in0=self.STATS[:, a:a + 1],
                in1=self.STATS[:, b:b + 1], op=OP.max))
            self.drain("vector")
        return self.col_max_to_scalar(self.STATS[:, c0:c0 + 1], dst_col)

    def collective_max(self, src_col, n_vals, dst_gmax_col):
        i = self.cc_idx
        self.cc_idx += 1
        ci, co = self.cc_in[i], self.cc_out[i]
        self.fence("gpsimd", "vector")
        self.dma("gpsimd", ci[0:1, 0:n_vals], self.STATS[0:1, src_col:src_col + n_vals])
        self.dma_wait("gpsimd", "gpsimd")
        evc = self.op("gpsimd", lambda g, ci=ci, co=co, n=n_vals: g.collective_compute(
            "AllReduce", OP.max, replica_groups=[list(range(NCORES))],
            ins=[ci[0:1, 0:n].opt()], outs=[co[0:1, 0:n].opt()]), inc=self.sCC)
        self.wait("gpsimd", evc)
        self.dma("gpsimd", self.STATS[0:1, 200:200 + n_vals], co[0:1, 0:n_vals])
        self.dma_wait("vector", "gpsimd")
        self.dma_wait("tensor", "gpsimd")
        return self.replicate_cols(self.STATS[0:1, 200:200 + n_vals], n_vals, dst_gmax_col)

    def make_inv_scale(self, raw_col, inv_col, scale_col, qmax):
        g = self.GMAX
        self.drain("vector")
        self.op("vector", lambda v: v.tensor_scalar(
            out=g[:, scale_col:scale_col + 1], in0=g[:, raw_col:raw_col + 1],
            scalar1=1.0 / qmax, scalar2=1e-12, op0=OP.mult, op1=OP.max))
        self.drain("vector")
        ev = self.op("vector", lambda v: v.reciprocal(
            out=g[:, inv_col:inv_col + 1], in_=g[:, scale_col:scale_col + 1]),
            inc=self.sV)
        self.drain("vector")
        return ev

    def layernorm(self):
        T, NT, FC = self.T, self.NT, self.FC
        xb = self.ARB[:, 0:512]
        xsq = self.ARB[:, 512:1024]
        mean = self.ARF[:, 0:512]
        var = self.ARF[:, 512:1024]
        rstd = self.ARF[:, 1024:1536]
        self.barrier()
        for tc in range(NT):
            pm, pq = self.pb[0], self.pb[1]
            self.fence("tensor", "vector")
            last_mm = None
            for c in range(FC):
                xsl = slice(c * T + tc * 512, c * T + (tc + 1) * 512)
                e1 = self.op("vector", lambda v, a=xsl: v.tensor_copy(xb[:], self.X[:, a]), inc=self.sV)
                e2 = self.op("scalar", lambda s, a=xsl: s.activation(
                    out=xsq[:], in_=self.X[:, a], func=AF.Square), inc=self.sS)
                self.wait("tensor", e1)
                self.wait("tensor", e2)
                self.op("tensor", lambda t, c0=c: t.matmul(
                    pm[:, 0:512], self.ones_bf, xb[:], start=(c0 == 0), stop=(c0 == FC - 1)))
                last_mm = self.op("tensor", lambda t, c0=c: t.matmul(
                    pq[:, 0:512], self.ones_bf, xsq[:], start=(c0 == 0), stop=(c0 == FC - 1)),
                    inc=self.sT)
                self.wait("vector", last_mm)
                self.wait("scalar", last_mm)
            self.wait("vector", last_mm)
            self.op("vector", lambda v: v.tensor_scalar(
                out=mean, in0=pm[:, 0:512], scalar1=1.0 / D, scalar2=None, op0=OP.mult))
            self.drain("vector")
            self.op("vector", lambda v: v.tensor_tensor(out=var, in0=mean, in1=mean, op=OP.mult))
            self.drain("vector")
            ev = self.op("vector", lambda v: v.scalar_tensor_tensor(
                out=var, in0=pq[:, 0:512], scalar=1.0 / D, in1=var,
                op0=OP.mult, op1=OP.subtract), inc=self.sV)
            self.wait("scalar", ev)
            ev = self.op("scalar", lambda s: s.activation(
                out=rstd, in_=var, func=AF.Sqrt, bias=self.CCOL[:, 0:1]), inc=self.sS)
            self.wait("vector", ev)
            self.op("vector", lambda v: v.reciprocal(out=rstd, in_=rstd))
            self.drain("vector")
            for c in range(FC):
                xsl = slice(c * T + tc * 512, c * T + (tc + 1) * 512)
                self.op("vector", lambda v, a=xsl: v.tensor_tensor(
                    out=var, in0=self.X[:, a], in1=mean, op=OP.subtract))
                self.drain("vector")
                self.op("vector", lambda v, a=xsl: v.tensor_tensor(
                    out=self.XN[:, a], in0=var, in1=rstd, op=OP.mult))
                self.drain("vector")
        self.barrier()

    def load_cast_w(self, name, wslot, wstg, width=1024, scale=None):
        last = None
        for c in range(self.FC):
            self.dma("sync", wstg[:, 0:width], self.dram[name][c * 128:(c + 1) * 128, :])
            self.dma_wait("vector", "sync")
            last = self.op("vector", lambda v, c=c: v.tensor_scalar(
                out=wslot[:, c * width:(c + 1) * width], in0=wstg[:, 0:width],
                scalar1=(1.0 if scale is None else scale), scalar2=None,
                op0=OP.mult), inc=self.sV)
            self.wait("sync", last)
        return last

    def l0_attn(self):
        T, NT, NT128, FC = self.T, self.NT, self.NT128, self.FC
        A = self.ARB[:, 0:8 * T]
        B = self.ARB[:, 8 * T:16 * T]
        WSLOT = self.ARB[:, 16 * T:16 * T + 8192]
        KV = self.ARF.bitcast(BF16)[:, 8192:8704]
        WSTG = self.ARF[:, 0:1024]
        ROWF = self.ARF[:, 1024:2048]
        VTS = self.ARF.bitcast(BF16)

        def project_tm(dst, wname, l2norm):
            ev = self.load_cast_w(wname, WSLOT, WSTG)
            self.wait("tensor", ev)
            for tc in range(NT128):
                for dc in range(2):
                    pbk = self.pb[(tc * 2 + dc) % 2]
                    self.fence("tensor", "vector")
                    last = None
                    for i in range(FC):
                        last = self.op("tensor", lambda t, k=i, i0=i, tc=tc, dc=dc, pbk=pbk: t.matmul(
                            pbk[:, 0:512],
                            self.XN[:, k * T + tc * 128:k * T + tc * 128 + 128],
                            WSLOT[:, k * 1024 + dc * 512:k * 1024 + dc * 512 + 512],
                            start=(i0 == 0), stop=(i0 == FC - 1)),
                            inc=self.sT if i == FC - 1 else None)
                    self.wait("vector", last)
                    self.op("vector", lambda v, tc=tc, dc=dc, pbk=pbk: v.tensor_copy(
                        dst[:, tc * 1024 + dc * 512: tc * 1024 + dc * 512 + 512],
                        pbk[:, 0:512]), inc=self.sV)
                    self.drain("vector")
                if l2norm:
                    sl = slice(tc * 1024, (tc + 1) * 1024)
                    self.fence("scalar", "vector")
                    ev = self.op("scalar", lambda s, a=sl: s.activation(
                        out=ROWF[:, 0:1024], in_=dst[:, a], func=AF.Square), inc=self.sS)
                    self.wait("vector", ev)
                    self.op("vector", lambda v: v.tensor_reduce(
                        out=self.STATS[:, 64:80],
                        in_=ROWF[:, 0:1024].rearrange("p (h e) -> p h e", e=64),
                        axis=AX, op=OP.add))
                    self.drain("vector")
                    self.fence("scalar", "vector")
                    ev = self.op("scalar", lambda s: s.activation(
                        out=self.STATS[:, 80:96], in_=self.STATS[:, 64:80],
                        func=AF.Sqrt), inc=self.sS)
                    self.wait("vector", ev)
                    self.op("vector", lambda v: v.tensor_scalar(
                        out=self.STATS[:, 80:96], in0=self.STATS[:, 80:96],
                        scalar1=1e-12, scalar2=None, op0=OP.max))
                    self.drain("vector")
                    self.op("vector", lambda v: v.reciprocal(
                        out=self.STATS[:, 80:96], in_=self.STATS[:, 80:96]))
                    self.drain("vector")
                    self.op("vector", lambda v, a=sl: v.tensor_tensor(
                        out=dst[:, a].rearrange("p (h e) -> p h e", e=64),
                        in0=dst[:, a].rearrange("p (h e) -> p h e", e=64),
                        in1=self.STATS[:, 80:96].unsqueeze(2).to_broadcast([128, 16, 64]),
                        op=OP.mult), inc=self.sV)
                    self.drain("vector")

        self.dma("sync", self.STATS[0:1, 0:16], self.dram["L0nrm"][:, :])
        self.dma_wait("scalar", "sync")
        self.op("scalar", lambda s: s.activation(
            out=self.STATS[0:1, 16:32], in_=self.STATS[0:1, 0:16], func=AF.Sigmoid))
        self.drain("scalar")
        ev = self.op("scalar", lambda s: s.activation(
            out=self.STATS[0:1, 16:32], in_=self.STATS[0:1, 16:32], func=AF.Exp,
            scale=-math.log(float(self.T))), inc=self.sS)
        self.wait("vector", ev)
        self.wait("tensor", ev)
        self.replicate_cols(self.STATS[0:1, 16:32], 16, 0)

        project_tm(A, "L0WqT", l2norm=True)
        self.barrier(("tensor", "vector", "scalar", "sync"))
        for tc in range(NT128):
            for dc in range(FC):
                pbt = self.pt0 if (tc * FC + dc) % 2 == 0 else self.pt1
                self.fence("tensor", "vector")
                ev = self.op("tensor", lambda t, tc=tc, dc=dc, pbt=pbt: t.transpose(
                    pbt[:, 0:128], A[:, tc * 1024 + dc * 128: tc * 1024 + dc * 128 + 128],
                    self.ident), inc=self.sT)
                self.wait("vector", ev)
                self.op("vector", lambda v, tc=tc, dc=dc, pbt=pbt: v.tensor_copy(
                    B[:, dc * T + tc * 128: dc * T + tc * 128 + 128], pbt[:, 0:128]),
                    inc=self.sV)
        self.barrier(("tensor", "vector", "sync"))
        project_tm(A, "L0WkT", l2norm=True)
        self.barrier(("tensor", "vector", "scalar", "sync"))

        ev = self.load_cast_w("L0WvT", WSLOT, WSTG)
        self.wait("tensor", ev)
        pkv = self.pb[2]
        kv_read_ev = None
        for tc in range(NT128):
            vt = VTS[:, 4096 + (tc % 2) * 2048: 4096 + (tc % 2) * 2048 + 1024]
            for dc in range(2):
                pbk = self.pb[3 + dc]
                self.fence("tensor", "vector")
                last = None
                for i in range(FC):
                    last = self.op("tensor", lambda t, k=i, i0=i, tc=tc, dc=dc, pbk=pbk: t.matmul(
                        pbk[:, 0:512],
                        self.XN[:, k * T + tc * 128:k * T + tc * 128 + 128],
                        WSLOT[:, k * 1024 + dc * 512:k * 1024 + dc * 512 + 512],
                        start=(i0 == 0), stop=(i0 == FC - 1)),
                        inc=self.sT if i == FC - 1 else None)
                self.wait("vector", last)
                if kv_read_ev is not None:
                    self.wait("vector", kv_read_ev)
                self.op("vector", lambda v, vt=vt, dc=dc, pbk=pbk: v.tensor_copy(
                    vt[:, dc * 512:dc * 512 + 512], pbk[:, 0:512]), inc=self.sV)
                self.drain("vector")
            evv = self.op("vector", lambda v: v.nop(), inc=self.sV)
            self.wait("tensor", evv)
            last = None
            for h in range(16):
                pbase = 64 * (h % 2)
                cbase = 64 * (h // 2)
                last = self.op("tensor", lambda t, h=h, tc=tc, vt=vt, pbase=pbase, cbase=cbase: t.matmul(
                    pkv[pbase:pbase + 64, cbase:cbase + 64],
                    A[:, tc * 1024 + h * 64: tc * 1024 + h * 64 + 64],
                    vt[:, h * 64:h * 64 + 64],
                    start=(tc == 0 and h < 2), stop=(tc == NT128 - 1 and h >= 14),
                    skip_group_check=True),
                    inc=self.sT if h == 15 else None)
            kv_read_ev = last
        self.wait("vector", kv_read_ev)
        for h in range(16):
            pbase = 64 * (h % 2)
            cbase = 64 * (h // 2)
            self.op("vector", lambda v, h=h, pbase=pbase, cbase=cbase: v.tensor_scalar(
                out=KV[pbase:pbase + 64, cbase:cbase + 64],
                in0=pkv[pbase:pbase + 64, cbase:cbase + 64],
                scalar1=self.GMAX[pbase:pbase + 64, h:h + 1], scalar2=None, op0=OP.mult))
            self.drain("vector")
        self.barrier(("tensor", "vector", "sync"))

        for h in range(16):
            pbase = 64 * (h % 2)
            cb = (h // 2) * T
            for tc in range(NT):
                pbk = self.pb[(h * NT + tc) % 2]
                self.fence("tensor", "vector")
                ev = self.op("tensor", lambda t, h=h, tc=tc, pbase=pbase, cb=cb, pbk=pbk: t.matmul(
                    pbk[0:64, 0:512],
                    KV[pbase:pbase + 64, (h // 2) * 64:(h // 2) * 64 + 64],
                    B[pbase:pbase + 64, cb + tc * 512:cb + tc * 512 + 512],
                    start=True, stop=True), inc=self.sT)
                self.wait("vector", ev)
                self.op("vector", lambda v, h=h, tc=tc, pbase=pbase, cb=cb, pbk=pbk: v.tensor_copy(
                    A[pbase:pbase + 64, cb + tc * 512:cb + tc * 512 + 512], pbk[0:64, 0:512]),
                    inc=self.sV)
        self.barrier(("tensor", "vector", "sync"))

        ev = self.load_cast_w("L0WoT", WSLOT, WSTG)
        self.wait("tensor", ev)
        for cc in range(FC):
            for tc in range(NT):
                pbk = self.pb[(cc * NT + tc) % 2]
                self.fence("tensor", "vector")
                last = None
                for i in range(FC):
                    last = self.op("tensor", lambda t, k=i, i0=i, cc=cc, tc=tc, pbk=pbk: t.matmul(
                        pbk[:, 0:512],
                        WSLOT[:, k * 1024 + cc * 128:k * 1024 + cc * 128 + 128],
                        A[:, k * T + tc * 512:k * T + tc * 512 + 512],
                        start=(i0 == 0), stop=(i0 == FC - 1)),
                        inc=self.sT if i == FC - 1 else None)
                self.wait("vector", last)
                self.op("vector", lambda v, cc=cc, tc=tc, pbk=pbk: v.tensor_tensor(
                    out=self.X[:, cc * T + tc * 512: cc * T + tc * 512 + 512],
                    in0=self.X[:, cc * T + tc * 512: cc * T + tc * 512 + 512],
                    in1=pbk[:, 0:512], op=OP.add), inc=self.sV)
                self.drain("vector")
        self.barrier()

    def qdq_xn(self):
        T, FC = self.T, self.FC
        for c in range(FC):
            self.op("vector", lambda v, c=c: v.tensor_reduce(
                out=self.STATS[:, 32 + c:33 + c], in_=self.XN[:, c * T:(c + 1) * T],
                axis=AX, op=OP.max, apply_absolute_value=True))
            self.drain("vector")
        self.part_tree_max(list(range(32, 32 + FC)), dst_col=0)
        self.collective_max(0, 1, 18)
        self.make_inv_scale(18, 16, 17, 32767.0)
        g = self.GMAX
        tmp = self.ARF[:, 0:512]
        for c in range(FC):
            for tc in range(self.NT):
                sl = slice(c * T + tc * 512, c * T + (tc + 1) * 512)
                self.op("vector", lambda v, a=sl: v.tensor_scalar(
                    out=tmp, in0=self.XN[:, a], scalar1=g[:, 16:17],
                    scalar2=RNE, op0=OP.mult, op1=OP.add))
                self.drain("vector")
                self.op("vector", lambda v, a=sl: v.tensor_scalar(
                    out=self.XN[:, a], in0=tmp, scalar1=RNE, scalar2=g[:, 17:18],
                    op0=OP.subtract, op1=OP.mult))
                self.drain("vector")
        self.barrier(("vector", "tensor", "sync"))

    def lpbq_chunk(self, wname, oc, dst, I):
        WSTG = self.ARF[:, 0:1024] if I == 1024 else self.ARF[:, 0:3072]
        scr = self.ARF[:, 3072:3072 + 1024] if I == 1024 else self.ARF[:, 3072:3072 + 3072]
        QN = self.ARB[:, 0:I]
        nb = I // 32
        self.fence("sync", "vector")
        self.dma("sync", WSTG[:, 0:I], self.dram[wname][oc * 128:(oc + 1) * 128, :])
        self.dma_wait("vector", "sync")
        self.op("vector", lambda v: v.tensor_reduce(
            out=self.STATS[:, 0:nb], in_=WSTG[:, 0:I].rearrange("p (b w) -> p b w", w=32),
            axis=AX, op=OP.max, apply_absolute_value=True))
        self.drain("vector")
        self.op("vector", lambda v: v.tensor_scalar(
            out=self.STATS[:, 0:nb], in0=self.STATS[:, 0:nb],
            scalar1=1.0 / 7.0, scalar2=1e-12, op0=OP.mult, op1=OP.max))
        self.drain("vector")
        self.op("vector", lambda v: v.reciprocal(
            out=self.STATS[:, 96:96 + nb], in_=self.STATS[:, 0:nb]))
        self.drain("vector")
        self.op("vector", lambda v: v.tensor_tensor(
            out=scr[:, 0:I].rearrange("p (b w) -> p b w", w=32),
            in0=WSTG[:, 0:I].rearrange("p (b w) -> p b w", w=32),
            in1=self.STATS[:, 96:96 + nb].unsqueeze(2).to_broadcast([128, nb, 32]),
            op=OP.mult))
        self.drain("vector")
        self.op("vector", lambda v: v.tensor_scalar(
            out=scr[:, 0:I], in0=scr[:, 0:I], scalar1=RNE, scalar2=RNE,
            op0=OP.add, op1=OP.subtract))
        self.drain("vector")
        self.op("vector", lambda v: v.tensor_tensor(
            out=QN[:, 0:I].rearrange("p (b w) -> p b w", w=32),
            in0=scr[:, 0:I].rearrange("p (b w) -> p b w", w=32),
            in1=self.STATS[:, 0:nb].unsqueeze(2).to_broadcast([128, nb, 32]),
            op=OP.mult))
        self.drain("vector")
        ev = self.op("vector", lambda v: v.nop(), inc=self.sV)
        self.wait("tensor", ev)
        evv = None
        for ic in range(I // 128):
            pbt = self.pt0 if ic % 2 == 0 else self.pt1
            if evv is not None:
                self.wait("tensor", evv)
            ev = self.op("tensor", lambda t, ic=ic, pbt=pbt: t.transpose(
                pbt[:, 0:128], QN[:, ic * 128:ic * 128 + 128], self.ident), inc=self.sT)
            self.wait("vector", ev)
            evv = self.op("vector", lambda v, ic=ic, pbt=pbt: v.tensor_copy(
                dst[:, ic * 128:ic * 128 + 128], pbt[:, 0:128]), inc=self.sV)
        return evv

    def qdq_ffn(self, wg, wu, wd, spills):
        T, NT, FC = self.T, self.NT, self.FC
        sp_gate, sp_up, sp_g2, sp_o = spills
        WS = self.ARB[:, 4096:4096 + 4096]
        SPT = self.ARB[:, 8192:8192 + 2048]
        SPT2 = self.ARB[:, 10240:10240 + 2048]
        TIN = self.ARB[:, 12288:12288 + 2048]
        TIN2 = self.ARB[:, 14336:14336 + 2048]
        g = self.GMAX
        sA = self.ARF[:, 2048 + 0:2048 + 512]
        sB = self.ARF[:, 2560:3072]
        sC = self.ARF[:, 2048 + 2048:2048 + 2560] if False else None

        self.op("vector", lambda v: v.memset(self.STATS[:, 100:102], 0.0))
        self.drain("vector")
        for oc in range(24):
            for j, (wname, spill, spt) in enumerate(((wg, sp_gate, SPT), (wu, sp_up, SPT2))):
                wslc = WS[:, j * 2048:(j + 1) * 2048]
                evq = self.lpbq_chunk(wname, oc, wslc, 1024)
                self.wait("tensor", evq)
                self.dma_wait("vector", "gpsimd")
                for tc in range(NT):
                    pbk = self.pb[tc % 2]
                    self.fence("tensor", "vector")
                    self.fence("tensor", "gpsimd")
                    last = None
                    for i in range(FC):
                        last = self.op("tensor", lambda t, k=i, i0=i, tc=tc, pbk=pbk, wslc=wslc: t.matmul(
                            pbk[:, 0:512],
                            wslc[:, k * 128:k * 128 + 128],
                            self.XN[:, k * T + tc * 512:k * T + tc * 512 + 512],
                            start=(i0 == 0), stop=(i0 == FC - 1)),
                            inc=self.sT if i == FC - 1 else None)
                    self.wait("vector", last)
                    st = spt[:, (tc % 4) * 512:(tc % 4) * 512 + 512]
                    self.op("vector", lambda v, st=st, pbk=pbk: v.tensor_copy(st, pbk[:, 0:512]))
                    self.drain("vector")
                    self.op("vector", lambda v, st=st: v.tensor_reduce(
                        out=self.STATS[:, 102:103], in_=st, axis=AX, op=OP.max,
                        apply_absolute_value=True))
                    self.drain("vector")
                    self.op("vector", lambda v, j=j: v.tensor_tensor(
                        out=self.STATS[:, 100 + j:101 + j], in0=self.STATS[:, 100 + j:101 + j],
                        in1=self.STATS[:, 102:103], op=OP.max))
                    self.drain("vector")
                    evv = self.op("vector", lambda v: v.nop(), inc=self.sV)
                    self.wait("gpsimd", evv)
                    self.dma("gpsimd", spill[oc * 128:(oc + 1) * 128, tc * 512:(tc + 1) * 512], st)
        self.barrier(("vector", "tensor", "gpsimd", "sync"))
        self.part_tree_max([100], dst_col=0)
        self.part_tree_max([101], dst_col=1)
        self.collective_max(0, 2, 24)
        self.make_inv_scale(24, 4, 5, 32767.0)
        self.make_inv_scale(25, 6, 7, 32767.0)
        self.barrier(("vector", "tensor", "gpsimd", "sync"))

        self.op("vector", lambda v: v.memset(self.STATS[:, 100:101], 0.0))
        self.drain("vector")
        for oc in range(24):
            for tc in range(NT):
                ti = TIN[:, (tc % 4) * 512:(tc % 4) * 512 + 512]
                self.fence("sync", "vector")
                self.dma_wait("sync", "gpsimd")
                self.dma("sync", ti, sp_gate[oc * 128:(oc + 1) * 128, tc * 512:(tc + 1) * 512])
                self.dma_wait("vector", "sync")
                self.op("vector", lambda v, ti=ti: v.tensor_scalar(
                    out=sA, in0=ti, scalar1=g[:, 4:5], scalar2=RNE, op0=OP.mult, op1=OP.add))
                self.drain("vector")
                self.op("vector", lambda v: v.tensor_scalar(
                    out=sA, in0=sA, scalar1=RNE, scalar2=g[:, 5:6], op0=OP.subtract, op1=OP.mult))
                self.drain("vector")
                ev = self.op("vector", lambda v: v.nop(), inc=self.sV)
                self.wait("scalar", ev)
                ev = self.op("scalar", lambda s: s.activation(
                    out=sB, in_=sA, func=AF.Sigmoid), inc=self.sS)
                self.wait("vector", ev)
                self.op("vector", lambda v: v.tensor_scalar(
                    out=sB, in0=sB, scalar1=65536.0, scalar2=RNE, op0=OP.mult, op1=OP.add))
                self.drain("vector")
                self.op("vector", lambda v: v.tensor_scalar(
                    out=sB, in0=sB, scalar1=RNE, scalar2=65535.0, op0=OP.subtract, op1=OP.min))
                self.drain("vector")
                self.op("vector", lambda v: v.tensor_scalar(
                    out=sB, in0=sB, scalar1=2.0 ** -16, scalar2=None, op0=OP.mult))
                self.drain("vector")
                self.op("vector", lambda v, ti=ti: v.tensor_tensor(
                    out=ti, in0=sA, in1=sB, op=OP.mult))
                self.drain("vector")
                self.op("vector", lambda v, ti=ti: v.tensor_reduce(
                    out=self.STATS[:, 102:103], in_=ti, axis=AX, op=OP.max,
                    apply_absolute_value=True))
                self.drain("vector")
                self.op("vector", lambda v: v.tensor_tensor(
                    out=self.STATS[:, 100:101], in0=self.STATS[:, 100:101],
                    in1=self.STATS[:, 102:103], op=OP.max))
                self.drain("vector")
                evv = self.op("vector", lambda v: v.nop(), inc=self.sV)
                self.wait("gpsimd", evv)
                self.dma("gpsimd", sp_g2[oc * 128:(oc + 1) * 128, tc * 512:(tc + 1) * 512], ti)
        self.barrier(("vector", "tensor", "gpsimd", "scalar", "sync"))
        self.part_tree_max([100], dst_col=0)
        self.collective_max(0, 1, 26)
        self.make_inv_scale(26, 8, 9, 32767.0)
        self.barrier(("vector", "tensor", "gpsimd", "sync"))

        self.op("vector", lambda v: v.memset(self.STATS[:, 100:101], 0.0))
        self.drain("vector")
        for oc in range(24):
            for tc in range(NT):
                ti = TIN[:, (tc % 4) * 512:(tc % 4) * 512 + 512]
                ti2 = TIN2[:, (tc % 4) * 512:(tc % 4) * 512 + 512]
                self.fence("sync", "vector")
                self.dma_wait("sync", "gpsimd")
                self.dma("sync", ti, sp_g2[oc * 128:(oc + 1) * 128, tc * 512:(tc + 1) * 512])
                self.dma("sync", ti2, sp_up[oc * 128:(oc + 1) * 128, tc * 512:(tc + 1) * 512])
                self.dma_wait("vector", "sync")
                self.op("vector", lambda v, ti=ti: v.tensor_scalar(
                    out=sA, in0=ti, scalar1=g[:, 8:9], scalar2=RNE, op0=OP.mult, op1=OP.add))
                self.drain("vector")
                self.op("vector", lambda v: v.tensor_scalar(
                    out=sA, in0=sA, scalar1=RNE, scalar2=g[:, 9:10], op0=OP.subtract, op1=OP.mult))
                self.drain("vector")
                self.op("vector", lambda v, ti2=ti2: v.tensor_scalar(
                    out=sB, in0=ti2, scalar1=g[:, 6:7], scalar2=RNE, op0=OP.mult, op1=OP.add))
                self.drain("vector")
                self.op("vector", lambda v: v.tensor_scalar(
                    out=sB, in0=sB, scalar1=RNE, scalar2=g[:, 7:8], op0=OP.subtract, op1=OP.mult))
                self.drain("vector")
                self.op("vector", lambda v, ti=ti: v.tensor_tensor(
                    out=ti, in0=sA, in1=sB, op=OP.mult))
                self.drain("vector")
                self.op("vector", lambda v, ti=ti: v.tensor_reduce(
                    out=self.STATS[:, 102:103], in_=ti, axis=AX, op=OP.max,
                    apply_absolute_value=True))
                self.drain("vector")
                self.op("vector", lambda v: v.tensor_tensor(
                    out=self.STATS[:, 100:101], in0=self.STATS[:, 100:101],
                    in1=self.STATS[:, 102:103], op=OP.max))
                self.drain("vector")
                evv = self.op("vector", lambda v: v.nop(), inc=self.sV)
                self.wait("gpsimd", evv)
                self.dma("gpsimd", sp_o[oc * 128:(oc + 1) * 128, tc * 512:(tc + 1) * 512], ti)
        self.barrier(("vector", "tensor", "gpsimd", "sync"))
        self.part_tree_max([100], dst_col=0)
        self.collective_max(0, 1, 27)
        self.make_inv_scale(27, 10, 11, 32767.0)
        self.barrier(("vector", "tensor", "gpsimd", "sync"))

        WDQ = self.ARB[:, 4096 + 12288:4096 + 12288 + 24576]
        for oc2 in range(8):
            evv = self.lpbq_chunk(wd, oc2, None, 3072) if False else None
            self._lpbq_into_wdq(wd, oc2, WDQ)
        self.barrier(("vector", "tensor", "sync"))
        for tc in range(NT):
            for half, ccs in enumerate((range(0, 4), range(4, 8))):
                self.fence("tensor", "vector")
                lasts = {}
                for ic in range(24):
                    ti = TIN[:, (ic % 4) * 512:(ic % 4) * 512 + 512]
                    self.fence("sync", "tensor")
                    self.dma("sync", ti, sp_o[ic * 128:(ic + 1) * 128, tc * 512:(tc + 1) * 512])
                    self.dma_wait("vector", "sync")
                    self.op("vector", lambda v, ti=ti: v.tensor_scalar(
                        out=sA, in0=ti, scalar1=g[:, 10:11], scalar2=RNE, op0=OP.mult, op1=OP.add))
                    self.drain("vector")
                    ev = self.op("vector", lambda v, ti=ti: v.tensor_scalar(
                        out=ti, in0=sA, scalar1=RNE, scalar2=g[:, 11:12],
                        op0=OP.subtract, op1=OP.mult), inc=self.sV)
                    self.drain("vector")
                    self.wait("tensor", ev)
                    for ci, cc in enumerate(ccs):
                        lasts[cc] = self.op("tensor", lambda t, ic=ic, cc=cc, ti=ti: t.matmul(
                            self.pb[cc % 4][:, 0:512],
                            WDQ[:, ic * 1024 + cc * 128:ic * 1024 + cc * 128 + 128],
                            ti, start=(ic == 0), stop=(ic == 23)),
                            inc=self.sT if (ic == 23) else None)
                for ci, cc in enumerate(ccs):
                    self.wait("vector", lasts[cc])
                    self.op("vector", lambda v, cc=cc, tc=tc: v.tensor_tensor(
                        out=self.X[:, cc * T + tc * 512: cc * T + tc * 512 + 512],
                        in0=self.X[:, cc * T + tc * 512: cc * T + tc * 512 + 512],
                        in1=self.pb[cc % 4][:, 0:512], op=OP.add), inc=self.sV)
                    self.drain("vector")
        self.barrier()

    def _lpbq_into_wdq(self, wd, oc2, WDQ):
        I = 3072
        WSTG = self.ARF[:, 0:3072]
        scr = self.ARF[:, 3072:3072 + 3072]
        QN = self.ARB[:, 0:I]
        nb = I // 32
        self.fence("sync", "vector")
        self.dma("sync", WSTG[:, 0:I], self.dram[wd][oc2 * 128:(oc2 + 1) * 128, :])
        self.dma_wait("vector", "sync")
        self.op("vector", lambda v: v.tensor_reduce(
            out=self.STATS[:, 0:nb], in_=WSTG[:, 0:I].rearrange("p (b w) -> p b w", w=32),
            axis=AX, op=OP.max, apply_absolute_value=True))
        self.drain("vector")
        self.op("vector", lambda v: v.tensor_scalar(
            out=self.STATS[:, 0:nb], in0=self.STATS[:, 0:nb],
            scalar1=1.0 / 7.0, scalar2=1e-12, op0=OP.mult, op1=OP.max))
        self.drain("vector")
        self.op("vector", lambda v: v.reciprocal(
            out=self.STATS[:, 96:96 + nb], in_=self.STATS[:, 0:nb]))
        self.drain("vector")
        self.op("vector", lambda v: v.tensor_tensor(
            out=scr[:, 0:I].rearrange("p (b w) -> p b w", w=32),
            in0=WSTG[:, 0:I].rearrange("p (b w) -> p b w", w=32),
            in1=self.STATS[:, 96:96 + nb].unsqueeze(2).to_broadcast([128, nb, 32]),
            op=OP.mult))
        self.drain("vector")
        self.op("vector", lambda v: v.tensor_scalar(
            out=scr[:, 0:I], in0=scr[:, 0:I], scalar1=RNE, scalar2=RNE,
            op0=OP.add, op1=OP.subtract))
        self.drain("vector")
        self.op("vector", lambda v: v.tensor_tensor(
            out=QN[:, 0:I].rearrange("p (b w) -> p b w", w=32),
            in0=scr[:, 0:I].rearrange("p (b w) -> p b w", w=32),
            in1=self.STATS[:, 0:nb].unsqueeze(2).to_broadcast([128, nb, 32]),
            op=OP.mult))
        self.drain("vector")
        ev = self.op("vector", lambda v: v.nop(), inc=self.sV)
        self.wait("tensor", ev)
        evv = None
        for ic in range(24):
            pbt = self.pt0 if ic % 2 == 0 else self.pt1
            if evv is not None:
                self.wait("tensor", evv)
            ev = self.op("tensor", lambda t, ic=ic, pbt=pbt: t.transpose(
                pbt[:, 0:128], QN[:, ic * 128:ic * 128 + 128], self.ident), inc=self.sT)
            self.wait("vector", ev)
            evv = self.op("vector", lambda v, ic=ic, pbt=pbt, oc2=oc2: v.tensor_copy(
                WDQ[:, ic * 1024 + oc2 * 128:ic * 1024 + oc2 * 128 + 128], pbt[:, 0:128]),
                inc=self.sV)
        self.wait("tensor", evv)

    def proj_fm(self, wname, width, dst, wslot, wstg, scale=None, rhs_region=None):
        rhs = rhs_region if rhs_region is not None else self.XN
        T, NT, FC = self.T, self.NT, self.FC
        ev = self.load_cast_w(wname, wslot, wstg, width=width, scale=scale)
        self.wait("tensor", ev)
        for dc in range(width // 128):
            for tc in range(NT):
                pbk = self.pb[(dc * NT + tc) % 2]
                self.fence("tensor", "vector")
                last = None
                for i in range(FC):
                    last = self.op("tensor", lambda t, k=i, i0=i, dc=dc, tc=tc, pbk=pbk: t.matmul(
                        pbk[:, 0:512],
                        wslot[:, k * width + dc * 128:k * width + dc * 128 + 128],
                        rhs[:, k * T + tc * 512:k * T + tc * 512 + 512],
                        start=(i0 == 0), stop=(i0 == FC - 1)),
                        inc=self.sT if i == FC - 1 else None)
                self.wait("vector", last)
                self.op("vector", lambda v, dc=dc, tc=tc, pbk=pbk: v.tensor_copy(
                    dst[:, dc * T + tc * 512: dc * T + tc * 512 + 512], pbk[:, 0:512]),
                    inc=self.sV)
                self.drain("vector")

    def l1_attn(self):
        T, NT, NT128, FC = self.T, self.NT, self.NT128, self.FC
        Q = self.ARB[:, 0:8 * T]
        K = self.ARB[:, 8 * T:12 * T]
        CTAB = self.ARB[:, 12 * T:13 * T]
        STAB = self.ARB[:, 13 * T:14 * T]
        WSLOT = self.ARB[:, 14 * T:14 * T + 8192]
        SHUF = self.ARB[:, 14 * T + 8192:15 * T + 8192]
        PT = WSLOT[:, 0:2048]
        KSTG = WSLOT[:, 2048:2048 + T]
        QSTG = WSLOT[:, 2048 + T:2048 + 2 * T]
        OUTTM = SHUF[:, 0:64]
        WSTG = self.ARF[:, 0:1024]
        BO = self.ARF[:, 5200:5208]
        VTS = self.ARF.bitcast(BF16)
        VA = VTS[:, 2048:2048 + self.NT128 * 520]

        self.dma("sync", CTAB, self.dram["l1rope"][0, :, :])
        self.dma("sync", STAB, self.dram["l1rope"][1, :, :])
        self.dma("sync", BO, self.dram["L1bo"][:, :])
        self.dma_wait("vector", "sync")
        self.dma_wait("tensor", "sync")

        mask = []
        for i in range(16):
            mask += [2 * i + 1, 2 * i]

        def rope_apply(region, nchunks):
            for c in range(nchunks):
                tile = region[:, c * T:(c + 1) * T]
                self.op("vector", lambda v, tile=tile: v.stream_shuffle(
                    out=SHUF, in_=tile, mask=mask))
                self.drain("vector")
                self.op("vector", lambda v, tile=tile: v.tensor_tensor(
                    out=tile, in0=tile, in1=CTAB, op=OP.mult))
                self.drain("vector")
                self.op("vector", lambda v: v.tensor_tensor(
                    out=SHUF, in0=SHUF, in1=STAB, op=OP.mult))
                self.drain("vector")
                self.op("vector", lambda v, tile=tile: v.tensor_tensor(
                    out=tile, in0=tile, in1=SHUF, op=OP.add))
                self.drain("vector")

        self.proj_fm("L1Wq", 1024, Q, WSLOT, WSTG, scale=64.0 ** -0.5)
        self.dump("l1_q", Q, 8 * T)
        rope_apply(Q, FC)
        self.dump("l1_qrope", Q, 8 * T)
        self.proj_fm("L1Wk", 512, K, WSLOT, WSTG)
        rope_apply(K, 4)
        self.dump("l1_krope", K, 4 * T)
        self.barrier(("tensor", "vector", "sync"))

        ev = self.load_cast_w("L1Wv", WSLOT, WSTG, width=512)
        self.wait("tensor", ev)
        for tc in range(NT128):
            vt = VA[:, tc * 520:(tc + 1) * 520]
            pbk = self.pb[tc % 2]
            self.fence("tensor", "vector")
            last = None
            for i in range(FC):
                last = self.op("tensor", lambda t, k=i, i0=i, tc=tc, pbk=pbk: t.matmul(
                    pbk[:, 0:512],
                    self.XN[:, k * T + tc * 128:k * T + tc * 128 + 128],
                    WSLOT[:, k * 512:k * 512 + 512],
                    start=(i0 == 0), stop=(i0 == FC - 1)),
                    inc=self.sT if i == FC - 1 else None)
            self.wait("vector", last)
            self.op("vector", lambda v, vt=vt, pbk=pbk: v.tensor_copy(
                vt.rearrange("p (h w) -> p h w", w=65)[:, :, 0:64],
                pbk[:, 0:512].rearrange("p (h w) -> p h w", w=64)))
            self.drain("vector")
            self.op("vector", lambda v, vt=vt: v.memset(
                vt.rearrange("p (h w) -> p h w", w=65)[:, :, 64:65], 1.0), inc=self.sV)
            self.drain("vector")
        self.barrier(("tensor", "vector", "scalar", "sync"))

        for h in range(16):
            kvh = h // 2
            kbase = 64 * (kvh % 2)
            kchunk = kvh // 2
            qbase = 64 * (h % 2)
            qchunk = h // 2
            self.fence("vector", "tensor")
            if h % 2 == 0:
                self.op("vector", lambda v, kbase=kbase, kchunk=kchunk: v.tensor_copy(
                    KSTG[0:64, :], K[kbase:kbase + 64, kchunk * T:(kchunk + 1) * T]))
                self.drain("vector")
            ev = self.op("vector", lambda v, qbase=qbase, qchunk=qchunk: v.tensor_copy(
                QSTG[0:64, :], Q[qbase:qbase + 64, qchunk * T:(qchunk + 1) * T]),
                inc=self.sV)
            self.drain("vector")
            self.wait("tensor", ev)
            for qb in range(NT):
                ppv = self.pb[2]
                self.fence("tensor", "vector")
                for kc in range(NT128):
                    pbk = self.pb[3 + kc % 2]
                    pt = PT[:, (kc % 4) * 512:(kc % 4) * 512 + 512]
                    self.op("tensor", lambda t, kc=kc, pbk=pbk, qb=qb: t.matmul(
                        pbk[:, 0:512],
                        KSTG[0:64, kc * 128:kc * 128 + 128],
                        QSTG[0:64, qb * 512:qb * 512 + 512],
                        start=True, stop=True), inc=self.sT)
                    self.wait("scalar", (self.sT, self.sT.n))
                    evx = self.op("scalar", lambda s, pt=pt, pbk=pbk: s.activation(
                        out=pt, in_=pbk[:, 0:512], func=AF.Exp), inc=self.sS)
                    self.wait("tensor", evx)
                    for qs in range(4):
                        self.op("tensor", lambda t, kc=kc, qs=qs, pt=pt, ppv=ppv, tcv=kc,
                                kvh=kvh: t.matmul(
                            ppv[:, 65 * qs:65 * qs + 65],
                            pt[:, qs * 128:qs * 128 + 128],
                            VA[:, tcv * 520 + 65 * kvh: tcv * 520 + 65 * kvh + 65],
                            start=(kc == 0 and qs == 0), stop=(kc == NT128 - 1 and qs == 3),
                            skip_group_check=True),
                            inc=self.sT if (kc == NT128 - 1 and qs == 3) else None)
                self.wait("vector", (self.sT, self.sT.n))
                for qs in range(4):
                    self.op("vector", lambda v, qs=qs, ppv=ppv: v.reciprocal(
                        out=self.STATS[:, 120:121], in_=ppv[:, 65 * qs + 64:65 * qs + 65]))
                    self.drain("vector")
                    ev = self.op("vector", lambda v, qs=qs, ppv=ppv: v.tensor_scalar(
                        out=OUTTM, in0=ppv[:, 65 * qs:65 * qs + 64],
                        scalar1=self.STATS[:, 120:121], scalar2=None, op0=OP.mult),
                        inc=self.sV)
                    self.drain("vector")
                    self.wait("tensor", ev)
                    pbt = self.pt0
                    evt = self.op("tensor", lambda t, pbt=pbt: t.transpose(
                        pbt[0:64, 0:128], OUTTM, self.ident), inc=self.sT)
                    self.wait("vector", evt)
                    evv = self.op("vector", lambda v, qb=qb, qs=qs, pbt=pbt, qbase=qbase,
                                  qchunk=qchunk: v.tensor_copy(
                        Q[qbase:qbase + 64,
                          qchunk * T + qb * 512 + qs * 128: qchunk * T + qb * 512 + qs * 128 + 128],
                        pbt[0:64, 0:128]), inc=self.sV)
                    self.wait("tensor", evv)
        self.barrier(("tensor", "vector", "scalar", "sync"))
        self.dump("l1_attnT", Q, 8 * T)

        ev = self.load_cast_w("L1Wo", WSLOT, WSTG)
        self.wait("tensor", ev)
        for cc in range(FC):
            for tc in range(NT):
                pbk = self.pb[(cc * NT + tc) % 2]
                self.fence("tensor", "vector")
                last = None
                for i in range(FC):
                    last = self.op("tensor", lambda t, k=i, i0=i, cc=cc, tc=tc, pbk=pbk: t.matmul(
                        pbk[:, 0:512],
                        WSLOT[:, k * 1024 + cc * 128:k * 1024 + cc * 128 + 128],
                        Q[:, k * T + tc * 512:k * T + tc * 512 + 512],
                        start=(i0 == 0), stop=(i0 == FC - 1)),
                        inc=self.sT if i == FC - 1 else None)
                self.wait("vector", last)
                self.op("vector", lambda v, cc=cc, tc=tc, pbk=pbk: v.scalar_tensor_tensor(
                    out=self.X[:, cc * T + tc * 512: cc * T + tc * 512 + 512],
                    in0=pbk[:, 0:512], scalar=BO[:, cc:cc + 1],
                    in1=self.X[:, cc * T + tc * 512: cc * T + tc * 512 + 512],
                    op0=OP.add, op1=OP.add), inc=self.sV)
                self.drain("vector")
        self.barrier()
        self.dump_x("l1_xout")

    def fp4_qdq(self, tile, width):
        nb = width // 16
        r = self.ARF[:, 1024:1024 + width]
        a = self.ARF[:, 1024 + width:1024 + 2 * width]
        t1 = self.ARF[:, 1024 + 2 * width:1024 + 3 * width]
        t2 = self.ARF[:, 1024 + 3 * width:1024 + 4 * width]
        TS, STT, TT = "vector", "vector", "vector"
        v3 = lambda ap: ap.rearrange("p (b w) -> p b w", w=16)
        self.op("vector", lambda v: v.tensor_reduce(
            out=self.STATS[:, 0:nb], in_=v3(tile), axis=AX, op=OP.max,
            apply_absolute_value=True))
        self.drain("vector")
        self.op("vector", lambda v: v.tensor_scalar(
            out=self.STATS[:, 0:nb], in0=self.STATS[:, 0:nb],
            scalar1=1.0 / 6.0, scalar2=1e-12, op0=OP.mult, op1=OP.max))
        self.drain("vector")
        self.op("vector", lambda v: v.reciprocal(
            out=self.STATS[:, 128:128 + nb], in_=self.STATS[:, 0:nb]))
        self.drain("vector")
        self.op("vector", lambda v: v.tensor_scalar(
            out=self.STATS[:, 0:nb], in0=self.STATS[:, 0:nb],
            scalar1=0.5, scalar2=None, op0=OP.mult))
        self.drain("vector")
        self.op("vector", lambda v: v.tensor_tensor(
            out=v3(r), in0=v3(tile),
            in1=self.STATS[:, 128:128 + nb].unsqueeze(2).to_broadcast([128, nb, 16]),
            op=OP.mult))
        self.drain("vector")
        self.op("vector", lambda v: v.tensor_scalar(
            out=a, in0=r, scalar1=2.0, scalar2=RNE, op0=OP.mult, op1=OP.add))
        self.drain("vector")
        self.op("vector", lambda v: v.tensor_scalar(
            out=a, in0=a, scalar1=RNE, scalar2=4.0, op0=OP.subtract, op1=OP.min))
        self.drain("vector")
        self.op("vector", lambda v: v.tensor_scalar(
            out=a, in0=a, scalar1=-4.0, scalar2=None, op0=OP.max))
        self.drain("vector")
        self.op("vector", lambda v: v.tensor_scalar(
            out=t1, in0=r, scalar1=2.5, scalar2=None, op0=OP.is_gt))
        self.drain("vector")
        for thr in (3.5, 5.0, 5.0):
            self.op("vector", lambda v, thr=thr: v.scalar_tensor_tensor(
                out=t1, in0=r, scalar=thr, in1=t1, op0=OP.is_gt, op1=OP.add))
            self.drain("vector")
        self.op("vector", lambda v: v.tensor_scalar(
            out=t2, in0=r, scalar1=-2.5, scalar2=None, op0=OP.is_lt))
        self.drain("vector")
        for thr in (-3.5, -5.0, -5.0):
            self.op("vector", lambda v, thr=thr: v.scalar_tensor_tensor(
                out=t2, in0=r, scalar=thr, in1=t2, op0=OP.is_lt, op1=OP.add))
            self.drain("vector")
        self.op("vector", lambda v: v.scalar_tensor_tensor(
            out=a, in0=t1, scalar=2.0, in1=a, op0=OP.mult, op1=OP.add))
        self.drain("vector")
        self.op("vector", lambda v: v.scalar_tensor_tensor(
            out=a, in0=t2, scalar=-2.0, in1=a, op0=OP.mult, op1=OP.add))
        self.drain("vector")
        self.op("vector", lambda v: v.tensor_tensor(
            out=v3(tile), in0=v3(a),
            in1=self.STATS[:, 0:nb].unsqueeze(2).to_broadcast([128, nb, 16]),
            op=OP.mult))
        self.drain("vector")

    def l1_ffn(self):
        T, NT, NT128, FC = self.T, self.NT, self.NT128, self.FC
        XQ4T = self.ARB[:, 0:8 * T]
        XTM = self.ARB[:, 8 * T:16 * T]
        WSL = self.ARB[:, 16 * T:16 * T + 2048]
        HSTRIP = self.ARB[:, 16 * T + 2048:16 * T + 2048 + 2048]
        WN = self.ARB[:, 16 * T + 4096:16 * T + 4096 + 1024]
        TIN = self.ARB[:, 16 * T + 5120:16 * T + 5120 + 2048]
        WSTG = self.ARF[:, 0:1024]
        WDQ = self.ARB[:, 16 * T + 8192 if False else 0:0]

        for tc in range(NT128):
            for fc in range(FC):
                pbt = self.pt0 if (tc * FC + fc) % 2 == 0 else self.pt1
                self.fence("tensor", "vector")
                ev = self.op("tensor", lambda t, tc=tc, fc=fc, pbt=pbt: t.transpose(
                    pbt[:, 0:128], self.XN[:, fc * T + tc * 128:fc * T + tc * 128 + 128],
                    self.ident), inc=self.sT)
                self.wait("vector", ev)
                self.op("vector", lambda v, tc=tc, fc=fc, pbt=pbt: v.tensor_copy(
                    XTM[:, tc * 1024 + fc * 128:tc * 1024 + fc * 128 + 128], pbt[:, 0:128]),
                    inc=self.sV)
        self.fence("vector", "tensor")
        for tc in range(NT128):
            self.fp4_qdq(XTM[:, tc * 1024:(tc + 1) * 1024], 1024)
        self.fence("tensor", "vector")
        for tc in range(NT128):
            for fc in range(FC):
                pbt = self.pt0 if (tc * FC + fc) % 2 == 0 else self.pt1
                self.fence("tensor", "vector")
                ev = self.op("tensor", lambda t, tc=tc, fc=fc, pbt=pbt: t.transpose(
                    pbt[:, 0:128], XTM[:, tc * 1024 + fc * 128:tc * 1024 + fc * 128 + 128],
                    self.ident), inc=self.sT)
                self.wait("vector", ev)
                self.op("vector", lambda v, tc=tc, fc=fc, pbt=pbt: v.tensor_copy(
                    XQ4T[:, fc * T + tc * 128:fc * T + tc * 128 + 128], pbt[:, 0:128]),
                    inc=self.sV)
        self.barrier(("tensor", "vector", "sync"))

        IC = 2816 // 128
        for ic in range(IC):
            for j, wname in enumerate(("L1wg", "L1wu")):
                self.fence("sync", "vector")
                self.dma("sync", WSTG[:, :], self.dram[wname][ic * 128:(ic + 1) * 128, :])
                self.dma_wait("vector", "sync")
                self.op("vector", lambda v: v.tensor_copy(WN[:, 0:1024], WSTG[:, :]))
                self.drain("vector")
                self.fp4_qdq(WN[:, 0:1024], 1024)
                self.fence("tensor", "vector")
                for fc in range(FC):
                    pbt = self.pt0 if fc % 2 == 0 else self.pt1
                    ev = self.op("tensor", lambda t, fc=fc, pbt=pbt: t.transpose(
                        pbt[:, 0:128], WN[:, fc * 128:fc * 128 + 128], self.ident),
                        inc=self.sT)
                    self.wait("vector", ev)
                    evv = self.op("vector", lambda v, fc=fc, j=j, pbt=pbt: v.tensor_copy(
                        WSL[:, j * 1024 + fc * 128:j * 1024 + fc * 128 + 128], pbt[:, 0:128]),
                        inc=self.sV)
                    self.wait("tensor", evv)
            self.fence("tensor", "vector")
            for tc in range(NT128):
                pg, py = self.pb[0], self.pb[1]
                self.fence("tensor", "vector")
                self.fence("tensor", "scalar")
                lg = ly = None
                for i in range(FC):
                    lg = self.op("tensor", lambda t, k=i, i0=i, tc=tc, pg=pg: t.matmul(
                        pg[:, 0:128],
                        XQ4T[:, k * T + tc * 128:k * T + tc * 128 + 128],
                        WSL[:, 0 * 1024 + k * 128:0 * 1024 + k * 128 + 128],
                        start=(i0 == 0), stop=(i0 == FC - 1)),
                        inc=self.sT if i == FC - 1 else None)
                for i in range(FC):
                    ly = self.op("tensor", lambda t, k=i, i0=i, tc=tc, py=py: t.matmul(
                        py[:, 0:128],
                        XQ4T[:, k * T + tc * 128:k * T + tc * 128 + 128],
                        WSL[:, 1 * 1024 + k * 128:1 * 1024 + k * 128 + 128],
                        start=(i0 == 0), stop=(i0 == FC - 1)),
                        inc=self.sT if i == FC - 1 else None)
                self.wait("scalar", lg)
                evs = self.op("scalar", lambda s, tc=tc, pg=pg: s.activation(
                    out=HSTRIP[:, tc * 128:tc * 128 + 128], in_=pg[:, 0:128],
                    func=AF.Silu), inc=self.sS)
                self.wait("vector", ly)
                self.wait("vector", evs)
                self.op("vector", lambda v, tc=tc, py=py: v.tensor_tensor(
                    out=HSTRIP[:, tc * 128:tc * 128 + 128],
                    in0=HSTRIP[:, tc * 128:tc * 128 + 128], in1=py[:, 0:128],
                    op=OP.mult), inc=self.sV)
                self.drain("vector")
                self.wait("tensor", (self.sV, self.sV.n))
            for half in range(2):
                self.fp4_qdq(HSTRIP[:, half * 1024:(half + 1) * 1024], 1024)
            self.fence("tensor", "vector")
            self.dma_wait("vector", "gpsimd")
            for tc in range(NT128):
                pbt = self.pt0 if tc % 2 == 0 else self.pt1
                ev = self.op("tensor", lambda t, tc=tc, pbt=pbt: t.transpose(
                    pbt[:, 0:128], HSTRIP[:, tc * 128:tc * 128 + 128], self.ident),
                    inc=self.sT)
                self.wait("vector", ev)
                evv = self.op("vector", lambda v, tc=tc, ic=ic, pbt=pbt: v.tensor_copy(
                    TIN[:, (tc % 8) * 128:(tc % 8) * 128 + 128] if False else
                    TIN[:, (tc % 16) * 128:(tc % 16) * 128 + 128], pbt[:, 0:128]),
                    inc=self.sV)
                self.wait("tensor", evv)
                self.wait("gpsimd", evv)
                self.dma("gpsimd",
                         self.sp_gate[ic * 128:(ic + 1) * 128, tc * 128:(tc + 1) * 128],
                         TIN[:, (tc % 16) * 128:(tc % 16) * 128 + 128])
        self.barrier(("tensor", "vector", "scalar", "gpsimd", "sync"))

        WDQL = self.ARB[:, 0:IC * 1024]
        WN3 = self.ARB[:, IC * 1024:IC * 1024 + 2816]
        TIN2 = self.ARB[:, IC * 1024 + 2816:IC * 1024 + 2816 + 2048]
        WSTG3 = self.ARF[:, 0:1024]
        for oc in range(8):
            for seg, (c0, w) in enumerate(((0, 1024), (1024, 1024), (2048, 768))):
                self.fence("sync", "vector")
                self.dma("sync", WSTG3[:, 0:w], self.dram["L1wd"][oc * 128:(oc + 1) * 128, c0:c0 + w])
                self.dma_wait("vector", "sync")
                self.op("vector", lambda v, c0=c0, w=w: v.tensor_copy(
                    WN3[:, c0:c0 + w], WSTG3[:, 0:w]))
                self.drain("vector")
                self.fp4_qdq(WN3[:, c0:c0 + w], w)
            self.fence("tensor", "vector")
            for ic in range(IC):
                pbt = self.pt0 if ic % 2 == 0 else self.pt1
                ev = self.op("tensor", lambda t, ic=ic, pbt=pbt: t.transpose(
                    pbt[:, 0:128], WN3[:, ic * 128:ic * 128 + 128], self.ident),
                    inc=self.sT)
                self.wait("vector", ev)
                evv = self.op("vector", lambda v, ic=ic, oc=oc, pbt=pbt: v.tensor_copy(
                    WDQL[:, ic * 1024 + oc * 128:ic * 1024 + oc * 128 + 128], pbt[:, 0:128]),
                    inc=self.sV)
                self.wait("tensor", evv)
        self.barrier(("tensor", "vector", "sync"))
        for tc in range(NT):
            self.fence("tensor", "vector")
            lasts = {}
            for ic in range(IC):
                ti = TIN2[:, (ic % 2) * 1024:(ic % 2) * 1024 + 1024]
                self.fence("sync", "tensor")
                self.dma("sync", ti[:, 0:512],
                         self.sp_gate[ic * 128:(ic + 1) * 128, tc * 512:(tc + 1) * 512])
                self.dma_wait("tensor", "sync")
                for cc in range(self.FC):
                    lasts[cc] = self.op("tensor", lambda t, ic=ic, cc=cc, ti=ti: t.matmul(
                        self.pb[cc][:, 0:512],
                        WDQL[:, ic * 1024 + cc * 128:ic * 1024 + cc * 128 + 128],
                        ti[:, 0:512], start=(ic == 0), stop=(ic == IC - 1)),
                        inc=self.sT if ic == IC - 1 else None)
            for cc in range(self.FC):
                self.wait("vector", lasts[cc])
                self.op("vector", lambda v, cc=cc, tc=tc: v.tensor_tensor(
                    out=self.X[:, cc * T + tc * 512: cc * T + tc * 512 + 512],
                    in0=self.X[:, cc * T + tc * 512: cc * T + tc * 512 + 512],
                    in1=self.pb[cc][:, 0:512], op=OP.add), inc=self.sV)
                self.drain("vector")
        self.barrier()
    def l2_ffn(self):
        T, NT, FC = self.T, self.NT, self.FC
        CW = self.ARB[:, 0:8192]
        SW = self.ARB[:, 8192:16384]
        W1 = self.ARF[:, 0:1024]
        W2 = self.ARF[:, 1024:2048]
        S1 = self.ARF[:, 2048:2560]
        S2 = self.ARF[:, 2560:3072]
        for j, (wa, wb, dst, bias) in enumerate((("L2th", "L2thn", CW, self.CCOL[:, 1:2]),
                                                 ("L2ph", "L2phn", SW, None))):
            for c in range(FC):
                self.fence("sync", "vector")
                self.fence("sync", "scalar")
                self.dma("sync", W1, self.dram[wa][c * 128:(c + 1) * 128, :])
                self.dma("sync", W2, self.dram[wb][c * 128:(c + 1) * 128, :])
                self.dma_wait("vector", "sync")
                ev = self.op("vector", lambda v: v.tensor_tensor(
                    out=W1, in0=W1, in1=W2, op=OP.add), inc=self.sV)
                self.wait("scalar", ev)
                ev = self.op("scalar", lambda s, c=c, dst=dst, bias=bias: s.activation(
                    out=dst[:, c * 1024:(c + 1) * 1024], in_=W1, func=AF.Sin,
                    **({"bias": bias} if bias is not None else {})), inc=self.sS)
                self.wait("vector", ev)
                self.wait("sync", ev)
        self.barrier(("tensor", "vector", "scalar", "sync"))
        for oc in range(FC):
            for tc in range(NT):
                pr, pi = self.pb[(tc % 2) * 2], self.pb[(tc % 2) * 2 + 1]
                self.fence("tensor", "vector")
                self.fence("tensor", "scalar")
                for (pbk, wsl) in ((pr, CW), (pi, SW)):
                    last = None
                    for i in range(FC):
                        last = self.op("tensor", lambda t, k=i, i0=i, oc=oc, tc=tc, pbk=pbk, wsl=wsl: t.matmul(
                            pbk[:, 0:512],
                            wsl[:, k * 1024 + oc * 128:k * 1024 + oc * 128 + 128],
                            self.XN[:, k * T + tc * 512:k * T + tc * 512 + 512],
                            start=(i0 == 0), stop=(i0 == FC - 1)),
                            inc=self.sT if i == FC - 1 else None)
                self.wait("scalar", last)
                self.op("scalar", lambda s, pr=pr: s.activation(out=S1, in_=pr[:, 0:512], func=AF.Square))
                self.drain("scalar")
                ev = self.op("scalar", lambda s, pi=pi: s.activation(out=S2, in_=pi[:, 0:512], func=AF.Square),
                             inc=self.sS)
                self.wait("vector", ev)
                ev = self.op("vector", lambda v: v.tensor_tensor(out=S1, in0=S1, in1=S2, op=OP.add),
                             inc=self.sV)
                self.drain("vector")
                self.wait("scalar", ev)
                ev = self.op("scalar", lambda s: s.activation(out=S2, in_=S1, func=AF.Sqrt), inc=self.sS)
                self.wait("vector", ev)
                self.op("vector", lambda v, oc=oc, tc=tc: v.tensor_tensor(
                    out=self.X[:, oc * T + tc * 512:oc * T + tc * 512 + 512],
                    in0=self.X[:, oc * T + tc * 512:oc * T + tc * 512 + 512],
                    in1=S2, op=OP.add), inc=self.sV)
                self.drain("vector")
        self.barrier()

    def l2_attn(self):
        T, NT, NT128, FC = self.T, self.NT, self.NT128, self.FC
        Q = self.ARB[:, 0:8 * T]
        K = self.ARB[:, 8 * T:16 * T]
        WIN = self.ARB[:, 16 * T:16 * T + 8192]
        WINF = WIN.bitcast(F32)
        WSTG = self.ARF[:, 0:1024]
        VTS = self.ARF.bitcast(BF16)
        PB = self.XN
        KTAR = float(max(1, int(T * 0.1)))

        g100 = self.STATS[:, 140:148]
        ph025 = self.STATS[:, 148:156]
        kcol = self.STATS[:, 156:168]
        th100 = self.STATS[:, 168:169]
        pbias = self.STATS[:, 169:177]
        self.dma("sync", g100, self.dram["L2gam"][:, :])
        self.dma("sync", ph025, self.dram["L2phs"][:, :])
        self.dma("sync", pbias, self.dram["L2pb"][:, :])
        self.dma("sync", self.STATS[0:1, 200:212], self.dram["L2ker"][:, :])
        self.dma("sync", self.STATS[0:1, 212:213], self.dram["L2thf"][:, :])
        self.dma_wait("vector", "sync")
        self.dma_wait("tensor", "sync")
        self.replicate_cols(self.STATS[0:1, 200:212], 12, 0)
        self.drain("vector")
        self.op("vector", lambda v: v.tensor_copy(kcol, self.GMAX[:, 0:12]))
        self.drain("vector")
        self.replicate_cols(self.STATS[0:1, 212:213], 1, 12)
        self.drain("vector")
        self.op("vector", lambda v: v.tensor_scalar(
            out=th100, in0=self.GMAX[:, 12:13], scalar1=0.01, scalar2=None, op0=OP.mult))
        self.drain("vector")
        self.op("vector", lambda v: v.tensor_scalar(
            out=g100, in0=g100, scalar1=0.01, scalar2=None, op0=OP.mult))
        self.drain("vector")
        self.op("vector", lambda v: v.tensor_scalar(
            out=ph025, in0=ph025, scalar1=1.0 / (2 * math.pi), scalar2=0.25,
            op0=OP.mult, op1=OP.add), inc=self.sV)
        self.drain("vector")

        WSL = WIN[:, 0:2048]
        CHK = WIN[:, 2048:2048 + T]
        GD = WIN[:, 2048 + T:2048 + 2 * T]

        def qkv_slice(j, dst_tile_fn, scale=None):
            for k in range(FC):
                self.fence("sync", "vector")
                self.dma("sync", WSTG[:, 0:128],
                         self.dram["L2qkvT"][k * 128:(k + 1) * 128, j * 128:(j + 1) * 128])
                self.dma_wait("vector", "sync")
                self.op("vector", lambda v, k=k: v.tensor_scalar(
                    out=WSL[:, k * 128:(k + 1) * 128], in0=WSTG[:, 0:128],
                    scalar1=(1.0 if scale is None else scale), scalar2=None, op0=OP.mult),
                    inc=self.sV)
                self.drain("vector")
            self.fence("tensor", "vector")
            for tc in range(NT):
                pbk = self.pb[tc % 2]
                self.fence("tensor", "vector")
                last = None
                for i in range(FC):
                    last = self.op("tensor", lambda t, k=i, i0=i, tc=tc, pbk=pbk: t.matmul(
                        pbk[:, 0:512],
                        WSL[:, k * 128:k * 128 + 128],
                        self.XN[:, k * T + tc * 512:k * T + tc * 512 + 512],
                        start=(i0 == 0), stop=(i0 == FC - 1)),
                        inc=self.sT if i == FC - 1 else None)
                self.wait("vector", last)
                dst_tile_fn(tc, pbk)

        def dendritic(tile, kc0):
            self.dma_wait("vector", "gpsimd")
            self.op("vector", lambda v, tile=tile: v.tensor_scalar(
                out=GD, in0=tile, scalar1=kcol[:, kc0 + 3:kc0 + 4], scalar2=None, op0=OP.mult))
            self.drain("vector")
            for sh, ki in ((1, 2), (2, 1), (3, 0)):
                self.op("vector", lambda v, sh=sh, ki=ki, tile=tile: v.scalar_tensor_tensor(
                    out=GD[:, sh:T], in0=tile[:, 0:T - sh], scalar=kcol[:, kc0 + ki:kc0 + ki + 1],
                    in1=GD[:, sh:T], op0=OP.mult, op1=OP.add))
                self.drain("vector")
            self.op("vector", lambda v, tile=tile: v.tensor_copy(tile, GD))
            self.drain("vector")

        VB = WIN[:, 2048 + 2 * T:2048 + 2 * T + 512] if False else GD
        for dc in range(FC):
            def v_evac(tc, pbk, dc=dc):
                self.op("vector", lambda v, tc=tc, pbk=pbk: v.tensor_copy(
                    CHK[:, tc * 512:tc * 512 + 512], pbk[:, 0:512]), inc=self.sV)
                self.drain("vector")
            qkv_slice(16 + dc, v_evac)
            dendritic(CHK, 8)
            self.fence("tensor", "vector")
            for tc in range(NT128):
                pbt = self.pt0 if tc % 2 == 0 else self.pt1
                ev = self.op("tensor", lambda t, tc=tc, pbt=pbt: t.transpose(
                    pbt[:, 0:128], CHK[:, tc * 128:tc * 128 + 128], self.ident), inc=self.sT)
                self.wait("vector", ev)
                self.dma_wait("vector", "gpsimd")
                evv = self.op("vector", lambda v, tc=tc, pbt=pbt: v.tensor_copy(
                    GD[:, (tc % 8) * 128:(tc % 8) * 128 + 128], pbt[:, 0:128]),
                    inc=self.sV)
                self.wait("tensor", evv)
                self.wait("gpsimd", evv)
                self.dma("gpsimd",
                         self.sp_g2[tc * 128:(tc + 1) * 128, dc * 128:(dc + 1) * 128],
                         GD[:, (tc % 8) * 128:(tc % 8) * 128 + 128])
        for dc in range(FC):
            def q_evac(tc, pbk, dc=dc):
                self.op("vector", lambda v, tc=tc, dc=dc, pbk=pbk: v.tensor_copy(
                    Q[:, dc * T + tc * 512:dc * T + tc * 512 + 512], pbk[:, 0:512]), inc=self.sV)
                self.drain("vector")
            qkv_slice(dc, q_evac, scale=64.0 ** -0.5)
        for dc in range(FC):
            def k_evac(tc, pbk, dc=dc):
                self.op("vector", lambda v, tc=tc, dc=dc, pbk=pbk: v.tensor_copy(
                    K[:, dc * T + tc * 512:dc * T + tc * 512 + 512], pbk[:, 0:512]), inc=self.sV)
                self.drain("vector")
            qkv_slice(8 + dc, k_evac)
        self.barrier()
        for dc in range(FC):
            dendritic(Q[:, dc * T:(dc + 1) * T], 0)
            dendritic(K[:, dc * T:(dc + 1) * T], 4)
        self.barrier(("tensor", "vector", "sync"))

        TRF = WINF[:, 0:2048][:, 0:T]
        SCR = WINF[:, 2048:4096][:, 0:T]
        self.dma("sync", TRF, self.dram["trampf"][:, 0:T])
        self.dma_wait("vector", "sync")
        self.dma_wait("scalar", "sync")
        CTP = [self.pb[2 + i] for i in range(4)]
        self.fence("vector", "tensor")
        RNEC = 12582912.0
        FLT = self.XN.bitcast(F32)[:, 0:T]
        self.op("vector", lambda v: v.tensor_scalar(
            out=SCR, in0=TRF, scalar1=th100[:, 0:1], scalar2=0.25,
            op0=OP.mult, op1=OP.add))
        self.drain("vector")
        self.op("vector", lambda v: v.tensor_scalar(
            out=FLT, in0=SCR, scalar1=-0.5, scalar2=None, op0=OP.add))
        self.drain("vector")
        self.op("vector", lambda v: v.tensor_scalar(
            out=FLT, in0=FLT, scalar1=RNEC, scalar2=None, op0=OP.add))
        self.drain("vector")
        self.op("vector", lambda v: v.tensor_scalar(
            out=FLT, in0=FLT, scalar1=-RNEC, scalar2=None, op0=OP.add))
        self.drain("vector")
        self.op("vector", lambda v: v.tensor_tensor(
            out=SCR, in0=SCR, in1=FLT, op=OP.subtract))
        self.drain("vector")
        ev = self.op("vector", lambda v: v.nop(), inc=self.sV)
        self.wait("scalar", ev)
        ev = self.op("scalar", lambda s: s.activation(
            out=SCR, in_=SCR, func=AF.Sin, scale=2 * math.pi, bias=self.CCOL[:, 3:4]),
            inc=self.sS)
        self.wait("vector", ev)
        for i in range(T // 512):
            self.op("vector", lambda v, i=i: v.tensor_scalar(
                out=CTP[i][:, 0:512], in0=SCR[:, i * 512:(i + 1) * 512],
                scalar1=-0.2, scalar2=0.5, op0=OP.mult, op1=OP.add))
            self.drain("vector")
        for dc in range(FC):
            self.op("vector", lambda v, dc=dc: v.tensor_scalar(
                out=SCR, in0=TRF, scalar1=g100[:, dc:dc + 1], scalar2=ph025[:, dc:dc + 1],
                op0=OP.mult, op1=OP.add))
            self.drain("vector")
            self.op("vector", lambda v: v.tensor_scalar(
                out=FLT, in0=SCR, scalar1=-0.5, scalar2=None, op0=OP.add))
            self.drain("vector")
            self.op("vector", lambda v: v.tensor_scalar(
                out=FLT, in0=FLT, scalar1=RNEC, scalar2=None, op0=OP.add))
            self.drain("vector")
            self.op("vector", lambda v: v.tensor_scalar(
                out=FLT, in0=FLT, scalar1=-RNEC, scalar2=None, op0=OP.add))
            self.drain("vector")
            self.op("vector", lambda v: v.tensor_tensor(
                out=SCR, in0=SCR, in1=FLT, op=OP.subtract))
            self.drain("vector")
            ev = self.op("vector", lambda v: v.nop(), inc=self.sV)
            self.wait("scalar", ev)
            ev = self.op("scalar", lambda s: s.activation(
                out=SCR, in_=SCR, func=AF.Sin, scale=2 * math.pi, bias=self.CCOL[:, 3:4]),
                inc=self.sS)
            self.wait("vector", ev)
            for i in range(T // 512):
                self.op("vector", lambda v, i=i: v.scalar_tensor_tensor(
                    out=SCR[:, i * 512:(i + 1) * 512], in0=SCR[:, i * 512:(i + 1) * 512],
                    scalar=-0.3, in1=CTP[i][:, 0:512], op0=OP.mult, op1=OP.add))
                self.drain("vector")
            for reg in (Q, K):
                self.op("vector", lambda v, dc=dc, reg=reg: v.tensor_tensor(
                    out=reg[:, dc * T:(dc + 1) * T], in0=reg[:, dc * T:(dc + 1) * T],
                    in1=SCR, op=OP.mult))
                self.drain("vector")
        self.barrier(("tensor", "vector", "scalar", "sync"))

        CTAB = WIN[:, 0:T]
        STAB = WIN[:, T:2 * T]
        SHUF = WIN[:, 2 * T:3 * T]
        self.dma("sync", CTAB, self.dram["l2rope"][0, :, 0:T])
        self.dma("sync", STAB, self.dram["l2rope"][1, :, 0:T])
        self.dma_wait("vector", "sync")
        mask = []
        for i in range(16):
            mask += [2 * i + 1, 2 * i]
        for reg in (Q, K):
            for c in range(FC):
                tile = reg[:, c * T:(c + 1) * T]
                self.op("vector", lambda v, tile=tile: v.stream_shuffle(out=SHUF, in_=tile, mask=mask))
                self.drain("vector")
                self.op("vector", lambda v, tile=tile: v.tensor_tensor(
                    out=tile, in0=tile, in1=CTAB, op=OP.mult))
                self.drain("vector")
                self.op("vector", lambda v: v.tensor_tensor(out=SHUF, in0=SHUF, in1=STAB, op=OP.mult))
                self.drain("vector")
                self.op("vector", lambda v, tile=tile: v.tensor_tensor(
                    out=tile, in0=tile, in1=SHUF, op=OP.add))
                self.drain("vector")
        self.barrier(("tensor", "vector", "sync"))
        for c in range(FC):
            self.dma("gpsimd", self.sp_gate[c * 128:(c + 1) * 128, 0:T], Q[:, c * T:(c + 1) * T])
            self.dma("gpsimd", self.sp_up[c * 128:(c + 1) * 128, 0:T], K[:, c * T:(c + 1) * T])
        self.dma_wait("sync", "gpsimd")
        VL = self.ARB[:, 0:8 * T]
        for tc in range(NT128):
            self.dma("sync", VL[:, tc * 1024:(tc + 1) * 1024],
                     self.sp_g2[tc * 128:(tc + 1) * 128, 0:1024])
        self.dma_wait("tensor", "sync")
        self.dma_wait("vector", "sync")

        QSTG = WIN[:, 0:T]
        KSTG = WIN[:, T:2 * T]
        JNK = WIN[:, 2 * T:3 * T]
        PTT = WIN[:, 3 * T:3 * T + 512]
        OUTTM = WIN[:, 3 * T + 512:3 * T + 512 + 64]
        NQT = NT128
        GSZ = min(8, NQT)
        lo = self.STATS[:, 0:16]
        hi = self.STATS[:, 16:32]
        mid = self.STATS[:, 32:48]
        cnt = self.STATS[:, 48:64]
        dirm = self.STATS[:, 64:80]
        rsum = self.STATS[:, 80:96]
        NITER = 10
        for h in range(16):
            self.fence("sync", "tensor")
            self.dma("sync", QSTG[0:64, :], self.sp_gate[h * 64:h * 64 + 64, 0:T])
            self.dma("sync", KSTG[0:64, :], self.sp_up[h * 64:h * 64 + 64, 0:T])
            self.dma_wait("tensor", "sync")
            self.dma_wait("vector", "sync")
            self.dma_wait("scalar", "sync")
            for grp in range(NQT // GSZ):
                qts = list(range(grp * GSZ, grp * GSZ + GSZ))
                if h == 0:
                    for qi, qt in enumerate(qts):
                        kvalid = 128 * (qt + 1)
                        if kvalid < T:
                            self.op("vector", lambda v, qi=qi, qt=qt, kvalid=kvalid: v.memset(
                                PB[:, qi * T + kvalid:(qi + 1) * T], 0.0))
                            self.drain("vector")
                for qi, qt in enumerate(qts):
                    kvalid = 128 * (qt + 1)
                    for ks in range(0, kvalid, 512):
                        kw = min(512, kvalid - ks)
                        pbk = self.pb[(ks // 512) % 2]
                        self.fence("tensor", "scalar")
                        ev = self.op("tensor", lambda t, qt=qt, ks=ks, kw=kw, pbk=pbk: t.matmul(
                            pbk[:, 0:kw],
                            QSTG[0:64, qt * 128:qt * 128 + 128],
                            KSTG[0:64, ks:ks + kw],
                            start=True, stop=True), inc=self.sT)
                        self.wait("vector", ev)
                        self.wait("scalar", ev)
                        if ks + kw == kvalid:
                            self.op("vector", lambda v, kw=kw, pbk=pbk: v.tensor_tensor(
                                out=pbk[:, kw - 128:kw], in0=pbk[:, kw - 128:kw],
                                in1=self.TRIADD[:, :], op=OP.add), inc=self.sV)
                            self.drain("vector")
                            self.wait("scalar", (self.sV, self.sV.n))
                        evx = self.op("scalar", lambda s, qi=qi, ks=ks, kw=kw, pbk=pbk: s.activation(
                            out=PB[:, qi * T + ks:qi * T + ks + kw], in_=pbk[:, 0:kw],
                            func=AF.Exp), inc=self.sS)
                        self.wait("tensor", evx)
                self.fence("vector", "scalar")
                self.op("vector", lambda v: v.memset(lo[:, 0:GSZ], 1e-18))
                self.drain("vector")
                self.op("vector", lambda v: v.memset(hi[:, 0:GSZ], 1e18))
                self.drain("vector")
                for it in range(NITER):
                    self.op("vector", lambda v: v.tensor_tensor(
                        out=mid[:, 0:GSZ], in0=lo[:, 0:GSZ], in1=hi[:, 0:GSZ], op=OP.mult))
                    self.drain("vector")
                    ev = self.op("vector", lambda v: v.nop(), inc=self.sV)
                    self.wait("scalar", ev)
                    ev = self.op("scalar", lambda s: s.activation(
                        out=mid[:, 0:GSZ], in_=mid[:, 0:GSZ], func=AF.Sqrt), inc=self.sS)
                    self.wait("vector", ev)
                    for qi, qt in enumerate(qts):
                        kvalid = 128 * (qt + 1)
                        self.op("vector", lambda v, qi=qi, kvalid=kvalid: v.tensor_scalar(
                            out=JNK[:, 0:kvalid], in0=PB[:, qi * T:qi * T + kvalid],
                            scalar1=mid[:, qi:qi + 1], scalar2=0.0,
                            op0=OP.is_ge, op1=OP.add, accum_out=cnt[:, qi:qi + 1]))
                        self.drain("vector")
                    self.op("vector", lambda v: v.tensor_scalar(
                        out=dirm[:, 0:GSZ], in0=cnt[:, 0:GSZ], scalar1=KTAR, scalar2=None,
                        op0=OP.is_ge))
                    self.drain("vector")
                    tsel = self.STATS[:, 100:116]
                    self.op("vector", lambda v: v.tensor_tensor(
                        out=tsel[:, 0:GSZ], in0=mid[:, 0:GSZ], in1=lo[:, 0:GSZ],
                        op=OP.subtract))
                    self.drain("vector")
                    self.op("vector", lambda v: v.tensor_tensor(
                        out=tsel[:, 0:GSZ], in0=tsel[:, 0:GSZ], in1=dirm[:, 0:GSZ],
                        op=OP.mult))
                    self.drain("vector")
                    self.op("vector", lambda v: v.tensor_tensor(
                        out=lo[:, 0:GSZ], in0=lo[:, 0:GSZ], in1=tsel[:, 0:GSZ],
                        op=OP.add))
                    self.drain("vector")
                    self.op("vector", lambda v: v.tensor_tensor(
                        out=tsel[:, 0:GSZ], in0=hi[:, 0:GSZ], in1=mid[:, 0:GSZ],
                        op=OP.subtract))
                    self.drain("vector")
                    self.op("vector", lambda v: v.tensor_tensor(
                        out=tsel[:, 0:GSZ], in0=tsel[:, 0:GSZ], in1=dirm[:, 0:GSZ],
                        op=OP.mult))
                    self.drain("vector")
                    self.op("vector", lambda v: v.tensor_tensor(
                        out=hi[:, 0:GSZ], in0=mid[:, 0:GSZ], in1=tsel[:, 0:GSZ],
                        op=OP.add))
                    self.drain("vector")
                for qi, qt in enumerate(qts):
                    kvalid = 128 * (qt + 1)
                    ev = self.op("vector", lambda v, qi=qi, kvalid=kvalid: v.scalar_tensor_tensor(
                        out=PB[:, qi * T:qi * T + kvalid], in0=PB[:, qi * T:qi * T + kvalid],
                        scalar=lo[:, qi:qi + 1], in1=PB[:, qi * T:qi * T + kvalid],
                        op0=OP.is_ge, op1=OP.mult, accum_out=rsum[:, qi:qi + 1]), inc=self.sV)
                    self.drain("vector")
                    self.wait("tensor", ev)
                    ppv = self.pb[2]
                    self.fence("tensor", "vector")
                    for kc in range(qt + 1):
                        pbt = self.pt0 if kc % 2 == 0 else self.pt1
                        ev = self.op("tensor", lambda t, qi=qi, kc=kc, pbt=pbt: t.transpose(
                            pbt[:, 0:128], PB[:, qi * T + kc * 128:qi * T + kc * 128 + 128],
                            self.ident), inc=self.sT)
                        self.wait("vector", ev)
                        evv = self.op("vector", lambda v, kc=kc, pbt=pbt: v.tensor_copy(
                            PTT[:, (kc % 4) * 128:(kc % 4) * 128 + 128], pbt[:, 0:128]),
                            inc=self.sV)
                        self.wait("tensor", evv)
                        self.op("tensor", lambda t, kc=kc, qt=qt, h=h, ppv=ppv: t.matmul(
                            ppv[:, 0:64],
                            PTT[:, (kc % 4) * 128:(kc % 4) * 128 + 128],
                            VL[:, kc * 1024 + h * 64:kc * 1024 + h * 64 + 64],
                            start=(kc == 0), stop=(kc == qt)),
                            inc=self.sT if kc == qt else None)
                    self.wait("vector", (self.sT, self.sT.n))
                    self.op("vector", lambda v, qi=qi: v.reciprocal(
                        out=self.STATS[:, 96:97], in_=rsum[:, qi:qi + 1]))
                    self.drain("vector")
                    ev = self.op("vector", lambda v, ppv=ppv: v.tensor_scalar(
                        out=OUTTM, in0=ppv[:, 0:64], scalar1=self.STATS[:, 96:97],
                        scalar2=None, op0=OP.mult), inc=self.sV)
                    self.drain("vector")
                    self.wait("tensor", ev)
                    evt = self.op("tensor", lambda t: t.transpose(
                        self.pt0[0:64, 0:128], OUTTM, self.ident), inc=self.sT)
                    self.wait("vector", evt)
                    evv = self.op("vector", lambda v: v.tensor_copy(
                        JNK[0:64, 0:128], self.pt0[0:64, 0:128]), inc=self.sV)
                    self.wait("gpsimd", evv)
                    self.dma("gpsimd",
                             self.sp_attn[h * 64:h * 64 + 64, qt * 128:qt * 128 + 128],
                             JNK[0:64, 0:128])
                    self.dma_wait("vector", "gpsimd")
        self.barrier(("tensor", "vector", "scalar", "gpsimd", "sync"))

        WSLOT = WIN
        ev = self.load_cast_w("L2projT", WSLOT, WSTG)
        self.wait("tensor", ev)
        TIN4 = self.ARB[:, 0:2048]
        for tc in range(NT):
            self.fence("tensor", "vector")
            lasts = {}
            for dc in range(FC):
                ti = TIN4[:, (dc % 2) * 1024:(dc % 2) * 1024 + 512]
                self.fence("sync", "tensor")
                self.dma("sync", ti, self.sp_attn[dc * 128:(dc + 1) * 128, tc * 512:(tc + 1) * 512])
                self.dma_wait("tensor", "sync")
                for cc in range(FC):
                    lasts[cc] = self.op("tensor", lambda t, dc=dc, cc=cc, ti=ti: t.matmul(
                        self.pb[cc][:, 0:512],
                        WSLOT[:, dc * 1024 + cc * 128:dc * 1024 + cc * 128 + 128],
                        ti, start=(dc == 0), stop=(dc == FC - 1)),
                        inc=self.sT if dc == FC - 1 else None)
            for cc in range(FC):
                self.wait("vector", lasts[cc])
                self.op("vector", lambda v, cc=cc, tc=tc: v.scalar_tensor_tensor(
                    out=self.X[:, cc * T + tc * 512: cc * T + tc * 512 + 512],
                    in0=self.pb[cc][:, 0:512], scalar=pbias[:, cc:cc + 1],
                    in1=self.X[:, cc * T + tc * 512: cc * T + tc * 512 + 512],
                    op0=OP.add, op1=OP.add), inc=self.sV)
                self.drain("vector")
        self.barrier()

    def l3_attn(self):
        T, NT, NT128, FC = self.T, self.NT, self.NT128, self.FC
        A = self.ARB[:, 0:8 * T]
        B = self.ARB[:, 8 * T:16 * T]
        WSLOT = self.ARB[:, 16 * T:16 * T + 8192]
        WSTG = self.ARF[:, 0:1024]
        MIXS = self.ARF.bitcast(BF16)[:, 2048:2048 + T]
        TH = T // 2
        sp_sr = self.sp_g2
        sp_rw = self.sp_o

        VCOL = self.STATS[:, 140:180]
        tmk, tmv, tmr = VCOL[:, 0:8], VCOL[:, 8:16], VCOL[:, 16:24]
        ewc, euc = VCOL[:, 24:32], VCOL[:, 32:40]
        self.dma("sync", tmk, self.dram["L3tmk"][:, :])
        self.dma("sync", tmv, self.dram["L3tmv"][:, :])
        self.dma("sync", tmr, self.dram["L3tmr"][:, :])
        self.dma("sync", ewc, self.dram["L3td"][:, :])
        self.dma("sync", euc, self.dram["L3tf"][:, :])
        self.dma_wait("scalar", "sync")
        self.dma_wait("vector", "sync")
        self.op("scalar", lambda s: s.activation(out=ewc, in_=ewc, func=AF.Exp))
        self.drain("scalar")
        self.op("scalar", lambda s: s.activation(out=ewc, in_=ewc, func=AF.Exp, scale=-1.0))
        self.drain("scalar")
        ev = self.op("scalar", lambda s: s.activation(out=euc, in_=euc, func=AF.Exp), inc=self.sS)
        self.wait("vector", ev)
        self.wait("tensor", ev)

        def mix(dstregion, tmcol):
            for c in range(FC):
                xn_c = self.XN[:, c * T:(c + 1) * T]
                dst_c = dstregion[:, c * T:(c + 1) * T]
                self.op("vector", lambda v, xn_c=xn_c: v.tensor_tensor(
                    out=MIXS[:, 1:T], in0=xn_c[:, 1:T], in1=xn_c[:, 0:T - 1],
                    op=OP.subtract))
                self.drain("vector")
                self.op("vector", lambda v, xn_c=xn_c, dst_c=dst_c, tmcol=tmcol, c=c: v.scalar_tensor_tensor(
                    out=dst_c[:, 1:T], in0=MIXS[:, 1:T], scalar=tmcol[:, c:c + 1],
                    in1=xn_c[:, 0:T - 1], op0=OP.mult, op1=OP.add))
                self.drain("vector")
                self.op("vector", lambda v, xn_c=xn_c, dst_c=dst_c, tmcol=tmcol, c=c: v.tensor_scalar(
                    out=dst_c[:, 0:1], in0=xn_c[:, 0:1], scalar1=tmcol[:, c:c + 1],
                    scalar2=None, op0=OP.mult))
                self.drain("vector")

        def project_from_A(wname, evac):
            ev = self.load_cast_w(wname, WSLOT, WSTG)
            self.wait("tensor", ev)
            for dc in range(FC):
                for tc in range(NT):
                    pbk = self.pb[(dc * NT + tc) % 2]
                    self.fence("tensor", "vector")
                    self.fence("tensor", "scalar")
                    last = None
                    for i in range(FC):
                        last = self.op("tensor", lambda t, k=i, i0=i, dc=dc, tc=tc, pbk=pbk: t.matmul(
                            pbk[:, 0:512],
                            WSLOT[:, k * 1024 + dc * 128:k * 1024 + dc * 128 + 128],
                            A[:, k * T + tc * 512:k * T + tc * 512 + 512],
                            start=(i0 == 0), stop=(i0 == FC - 1)),
                            inc=self.sT if i == FC - 1 else None)
                    evac(dc, tc, pbk, last)

        mix(A, tmr)
        self.barrier(("tensor", "vector", "sync"))

        def evac_sr(dc, tc, pbk, last):
            self.wait("scalar", last)
            st = MIXS[:, 0:512]
            self.dma_wait("scalar", "gpsimd")
            ev = self.op("scalar", lambda s, st=st, pbk=pbk: s.activation(
                out=st, in_=pbk[:, 0:512], func=AF.Sigmoid), inc=self.sS)
            self.wait("tensor", ev)
            self.wait("gpsimd", ev)
            self.dma("gpsimd", sp_sr[dc * 128:(dc + 1) * 128, tc * 512:(tc + 1) * 512], st)
        project_from_A("L3WrT", evac_sr)
        self.barrier(("tensor", "vector", "scalar", "gpsimd", "sync"))

        mix(A, tmk)
        self.barrier(("tensor", "vector", "sync"))

        def evac_ek(dc, tc, pbk, last):
            self.wait("scalar", last)
            ev = self.op("scalar", lambda s, dc=dc, tc=tc, pbk=pbk: s.activation(
                out=B[:, dc * T + tc * 512:dc * T + tc * 512 + 512],
                in_=pbk[:, 0:512], func=AF.Exp), inc=self.sS)
            self.wait("tensor", ev)
        project_from_A("L3WkT", evac_ek)
        self.barrier(("tensor", "vector", "scalar", "sync"))
        self.dump("l3_ek", B, 8 * T)
        self.dump("l3_xk", A, 8 * T)

        mix(A, tmv)
        self.barrier(("tensor", "vector", "sync"))
        ev = self.load_cast_w("L3WvT", WSLOT, WSTG)
        self.wait("tensor", ev)
        ekf = self.ARF[:, 0:TH]
        vf = self.ARF[:, TH:2 * TH]
        As = self.ARF[:, 2 * TH:3 * TH]
        Bs = self.ARF[:, 3 * TH:4 * TH]
        ewt = self.ARF[:, 4 * TH:5 * TH]
        STA = self.STATS[:, 104:120]
        SRT = self.ARF.bitcast(BF16)[:, 10 * TH:11 * TH]
        RWT = self.ARF.bitcast(BF16)[:, 11 * TH:12 * TH]
        self.op("vector", lambda v: v.memset(STA, 0.0))
        self.drain("vector")
        for dc in range(FC):
            self.op("vector", lambda v: v.memset(ewt, 1.0))
            self.drain("vector")
            self.op("vector", lambda v, dc=dc: v.tensor_scalar(
                out=ewt, in0=ewt, scalar1=ewc[:, dc:dc + 1], scalar2=None, op0=OP.mult),
                inc=self.sV)
            self.drain("vector")
            SW = min(512, TH)
            for th in range(2):
                self.fence("tensor", "vector")
                for sub in range(TH // SW):
                    pbk = self.pb[2 + sub % 2]
                    last = None
                    for i in range(FC):
                        last = self.op("tensor", lambda t, k=i, i0=i, dc=dc, th=th, sub=sub, pbk=pbk: t.matmul(
                            pbk[:, 0:SW],
                            WSLOT[:, k * 1024 + dc * 128:k * 1024 + dc * 128 + 128],
                            A[:, k * T + th * TH + sub * SW:k * T + th * TH + sub * SW + SW],
                            start=(i0 == 0), stop=(i0 == FC - 1)),
                            inc=self.sT if i == FC - 1 else None)
                    self.wait("vector", last)
                    self.op("vector", lambda v, sub=sub, pbk=pbk: v.tensor_copy(
                        vf[:, sub * SW:sub * SW + SW], pbk[:, 0:SW]))
                    self.drain("vector")
                self.op("vector", lambda v, dc=dc, th=th: v.tensor_copy(
                    ekf, B[:, dc * T + th * TH:dc * T + th * TH + TH]))
                self.drain("vector")
                self.op("vector", lambda v: v.tensor_tensor(
                    out=vf, in0=ekf, in1=vf, op=OP.mult))
                self.drain("vector")
                self.op("vector", lambda v, dc=dc: v.tensor_tensor_scan(
                    out=As, data0=ewt, data1=vf, initial=STA[:, 2 * dc:2 * dc + 1],
                    op0=OP.mult, op1=OP.add))
                self.drain("vector")
                self.op("vector", lambda v, dc=dc: v.tensor_tensor_scan(
                    out=Bs, data0=ewt, data1=ekf, initial=STA[:, 2 * dc + 1:2 * dc + 2],
                    op0=OP.mult, op1=OP.add))
                self.drain("vector")
                self.op("vector", lambda v, dc=dc: v.scalar_tensor_tensor(
                    out=vf[:, 0:1], in0=vf[:, 0:1], scalar=euc[:, dc:dc + 1],
                    in1=STA[:, 2 * dc:2 * dc + 1], op0=OP.mult, op1=OP.add))
                self.drain("vector")
                self.op("vector", lambda v, dc=dc: v.scalar_tensor_tensor(
                    out=vf[:, 1:TH], in0=vf[:, 1:TH], scalar=euc[:, dc:dc + 1],
                    in1=As[:, 0:TH - 1], op0=OP.mult, op1=OP.add))
                self.drain("vector")
                self.op("vector", lambda v, dc=dc: v.scalar_tensor_tensor(
                    out=ekf[:, 0:1], in0=ekf[:, 0:1], scalar=euc[:, dc:dc + 1],
                    in1=STA[:, 2 * dc + 1:2 * dc + 2], op0=OP.mult, op1=OP.add))
                self.drain("vector")
                self.op("vector", lambda v, dc=dc: v.scalar_tensor_tensor(
                    out=ekf[:, 1:TH], in0=ekf[:, 1:TH], scalar=euc[:, dc:dc + 1],
                    in1=Bs[:, 0:TH - 1], op0=OP.mult, op1=OP.add))
                self.drain("vector")
                self.op("vector", lambda v, dc=dc: v.tensor_copy(
                    STA[:, 2 * dc:2 * dc + 1], As[:, TH - 1:TH]))
                self.drain("vector")
                self.op("vector", lambda v, dc=dc: v.tensor_copy(
                    STA[:, 2 * dc + 1:2 * dc + 2], Bs[:, TH - 1:TH]))
                self.drain("vector")
                self.op("vector", lambda v: v.reciprocal(out=ekf, in_=ekf))
                self.drain("vector")
                self.op("vector", lambda v: v.tensor_tensor(
                    out=vf, in0=vf, in1=ekf, op=OP.mult))
                self.drain("vector")
                self.dma_wait("sync", "gpsimd")
                self.fence("sync", "vector")
                self.dma("sync", SRT, sp_sr[dc * 128:(dc + 1) * 128, th * TH:(th + 1) * TH])
                self.dma_wait("vector", "sync")
                self.op("vector", lambda v: v.tensor_tensor(
                    out=RWT, in0=vf, in1=SRT, op=OP.mult), inc=self.sV)
                self.drain("vector")
                evv = self.op("vector", lambda v: v.nop(), inc=self.sV)
                self.wait("gpsimd", evv)
                self.dma("gpsimd", sp_rw[dc * 128:(dc + 1) * 128, th * TH:(th + 1) * TH], RWT)
                self.dma_wait("vector", "gpsimd")
        self.barrier(("tensor", "vector", "scalar", "gpsimd", "sync"))

        ev = self.load_cast_w("L3WoT", WSLOT, WSTG)
        self.wait("tensor", ev)
        TIN3 = self.ARB[:, 0:2048]
        for tc in range(NT):
            self.fence("tensor", "vector")
            lasts = {}
            for dc in range(FC):
                ti = TIN3[:, (dc % 2) * 1024:(dc % 2) * 1024 + 512]
                self.fence("sync", "tensor")
                self.dma("sync", ti, sp_rw[dc * 128:(dc + 1) * 128, tc * 512:(tc + 1) * 512])
                self.dma_wait("tensor", "sync")
                for cc in range(FC):
                    lasts[cc] = self.op("tensor", lambda t, dc=dc, cc=cc, ti=ti: t.matmul(
                        self.pb[cc][:, 0:512],
                        WSLOT[:, dc * 1024 + cc * 128:dc * 1024 + cc * 128 + 128],
                        ti, start=(dc == 0), stop=(dc == FC - 1)),
                        inc=self.sT if dc == FC - 1 else None)
            for cc in range(FC):
                self.wait("vector", lasts[cc])
                self.op("vector", lambda v, cc=cc, tc=tc: v.tensor_tensor(
                    out=self.X[:, cc * T + tc * 512: cc * T + tc * 512 + 512],
                    in0=self.X[:, cc * T + tc * 512: cc * T + tc * 512 + 512],
                    in1=self.pb[cc][:, 0:512], op=OP.add), inc=self.sV)
                self.drain("vector")
        self.barrier()

    def emit_output(self):
        T = self.T
        self.barrier()
        for c in range(self.FC):
            self.dma("sync", self.yT[c * 128:(c + 1) * 128, :], self.X[:, c * T:(c + 1) * T])
        self.dma_wait("sync", "sync")
        self.dma_wait("gpsimd", "gpsimd")


def build_nc(T=2048, layers=LAYERS):
    import contextlib
    nc = bass.Bass("TRN2", target_bir_lowering=False)
    with contextlib.ExitStack() as stack:
        mk = MK(nc, T, layers, stack)
        mk.declare()
        mk.init_loads()
        if layers >= 1:
            mk.layernorm()
            mk.l0_attn()
            mk.qdq_xn()
            mk.qdq_ffn("L0wg", "L0wu", "L0wd",
                       (mk.sp_gate, mk.sp_up, mk.sp_g2, mk.sp_o))
        if layers >= 2:
            mk.layernorm()
            if "l1_attn" not in SKIP:
                mk.l1_attn()
            if "l1_ffn" not in SKIP:
                mk.l1_ffn()
        if layers >= 3:
            mk.layernorm()
            if "l2_ffn" not in SKIP:
                mk.l2_ffn()
            if "l2_attn" not in SKIP:
                mk.l2_attn()
        if layers >= 4:
            mk.layernorm()
            if "l3_attn" not in SKIP:
                mk.l3_attn()
            if "l3_ffn" not in SKIP:
                mk.qdq_xn()
                mk.qdq_ffn("L3wg", "L3wu", "L3wd",
                           (mk.sp_gate, mk.sp_up, mk.sp_g2, mk.sp_o))
        mk.emit_output()
        with nc.Block() as block:
            mk.replay(block)
    return nc



def make_consts(T):
    import ml_dtypes
    bf = ml_dtypes.bfloat16
    consts = {}
    consts["ident"] = np.eye(128, dtype=np.float32).astype(bf)
    consts["identf"] = np.eye(128, dtype=np.float32)
    consts["ones_bf"] = np.ones((128, 128), np.float32).astype(bf)
    consts["ones_f"] = np.ones((1, 128), np.float32)
    consts["triadd"] = np.where(np.tril(np.ones((128, 128), bool)), 0.0, -1e30).astype(np.float32)
    consts["trampf"] = np.broadcast_to(np.arange(T, dtype=np.float32), (128, T)).copy()

    t = np.arange(T, dtype=np.float64)

    def rope_tables(rot, head, base=10000.0):
        c = np.zeros((128, T)); s = np.zeros((128, T))
        inv = 1.0 / base ** (np.arange(0, rot, 2, dtype=np.float64) / rot)
        for r in range(128):
            l = r % head
            if l < rot:
                f = t * inv[l // 2]
                sgn = -1.0 if (l % 2 == 0) else 1.0
                c[r], s[r] = np.cos(f), sgn * np.sin(f)
            else:
                c[r], s[r] = 1.0, 0.0
        return np.stack([c, s]).astype(np.float32).astype(bf)

    consts["l1rope"] = rope_tables(32, 64)
    consts["l2rope"] = rope_tables(64, 64)
    return consts


def make_inmaps(inputs, T):
    ct = np.ascontiguousarray
    f32 = lambda a: np.asarray(a, dtype=np.float32)
    shared = {}
    shared["L0WqT"] = ct(f32(inputs["L0_attn_Wq"]).T)
    shared["L0WkT"] = ct(f32(inputs["L0_attn_Wk"]).T)
    shared["L0WvT"] = ct(f32(inputs["L0_attn_Wv"]).T)
    shared["L0WoT"] = ct(f32(inputs["L0_attn_Wo"]).T)
    shared["L0nrm"] = ct(f32(inputs["L0_attn_norm_const"]).reshape(1, 16))
    shared["L0wg"] = f32(inputs["L0_ffn_w_gate"])
    shared["L0wu"] = f32(inputs["L0_ffn_w_up"])
    shared["L0wd"] = f32(inputs["L0_ffn_w_down"])
    shared["L1Wq"] = f32(inputs["L1_attn_Wq"])
    shared["L1Wk"] = f32(inputs["L1_attn_Wk"])
    shared["L1Wv"] = f32(inputs["L1_attn_Wv"])
    shared["L1Wo"] = f32(inputs["L1_attn_Wo"])
    shared["L1bo"] = ct(f32(inputs["L1_attn_bo"]).reshape(8, 128).T)
    shared["L1wg"] = f32(inputs["L1_ffn_w_gate"])
    shared["L1wu"] = f32(inputs["L1_ffn_w_up"])
    shared["L1wd"] = f32(inputs["L1_ffn_w_down"])
    shared["L2qkvT"] = ct(f32(inputs["L2_attn_qkv_w"]).T)
    shared["L2projT"] = ct(f32(inputs["L2_attn_proj_w"]).T)
    shared["L2pb"] = ct(f32(inputs["L2_attn_proj_b"]).reshape(8, 128).T)
    shared["L2ker"] = np.concatenate([
        f32(inputs["L2_attn_kernel_q"]).ravel(),
        f32(inputs["L2_attn_kernel_k"]).ravel(),
        f32(inputs["L2_attn_kernel_v"]).ravel()]).reshape(1, 12)
    shared["L2gam"] = ct(f32(inputs["L2_attn_gamma_freq"]).reshape(8, 128).T)
    shared["L2phs"] = ct(f32(inputs["L2_attn_phase"]).reshape(8, 128).T)
    shared["L2thf"] = f32(inputs["L2_attn_theta_freq"]).reshape(1, 1)
    shared["L2th"] = f32(inputs["L2_ffn_theta"])
    shared["L2ph"] = f32(inputs["L2_ffn_phi"])
    shared["L2thn"] = f32(inputs["L2_ffn_theta_noise"])
    shared["L2phn"] = f32(inputs["L2_ffn_phi_noise"])
    shared["L3td"] = ct(f32(inputs["L3_attn_time_decay"]).reshape(8, 128).T)
    shared["L3tf"] = ct(f32(inputs["L3_attn_time_first"]).reshape(8, 128).T)
    shared["L3tmk"] = ct(f32(inputs["L3_attn_time_mix_k"]).reshape(8, 128).T)
    shared["L3tmv"] = ct(f32(inputs["L3_attn_time_mix_v"]).reshape(8, 128).T)
    shared["L3tmr"] = ct(f32(inputs["L3_attn_time_mix_r"]).reshape(8, 128).T)
    shared["L3WkT"] = ct(f32(inputs["L3_attn_Wk"]).T)
    shared["L3WvT"] = ct(f32(inputs["L3_attn_Wv"]).T)
    shared["L3WrT"] = ct(f32(inputs["L3_attn_Wr"]).T)
    shared["L3WoT"] = ct(f32(inputs["L3_attn_Wo"]).T)
    shared["L3wg"] = f32(inputs["L3_ffn_w_gate"])
    shared["L3wu"] = f32(inputs["L3_ffn_w_up"])
    shared["L3wd"] = f32(inputs["L3_ffn_w_down"])
    shared.update(make_consts(T))
    shared = {k: np.ascontiguousarray(v) for k, v in shared.items()}

    x = np.asarray(inputs["L0_attn_x"])
    in_maps = []
    for c in range(NCORES):
        m = dict(shared)
        m["xT"] = np.ascontiguousarray(f32(x[c]).T)
        in_maps.append(m)
    return in_maps


def kernel(**inputs):
    T = int(np.asarray(inputs["L0_attn_x"]).shape[1])
    in_maps = make_inmaps(inputs, T)
    nc = build_nc(T=T, layers=LAYERS)
    res = run_bass_kernel_spmd(nc, in_maps, core_ids=list(range(NCORES)))
    outs = [np.ascontiguousarray(np.asarray(r["yT"]).T) for r in res.results]
    return np.stack(outs).astype(np.float32)


# version 07
# speedup vs baseline: 1.2107x; 1.1358x over previous
import math
import numpy as np
import concourse.bass as bass
import concourse.mybir as mybir
from concourse.alu_op_type import AluOpType
from concourse.bass_utils import run_bass_kernel_spmd

F32 = mybir.dt.float32
BF16 = mybir.dt.bfloat16
AF = mybir.ActivationFunctionType
AX = mybir.AxisListType.X
OP = AluOpType

RNE = 1.5 * 2 ** 23
D = 1024
NCORES = 8
LAYERS = 4
SKIP = set()
DBG = None

ENGS = ("tensor", "vector", "scalar", "gpsimd", "sync")


class MSem:
    def __init__(self, handle, name):
        self.h = handle
        self.name = name
        self.n = 0


class MK:
    def __init__(self, nc, T, layers, stack):
        self.nc = nc
        self.T = T
        self.layers = layers
        self.stack = stack
        self.NT = T // 512
        self.NT128 = T // 128
        self.FC = D // 128
        self.q = {e: [] for e in ENGS}
        self.dram = {}
        self.cc_idx = 0
        self._flushed = {"sync": -1, "gpsimd": -1}

    def sem(self, name):
        h = self.stack.enter_context(self.nc.semaphore(name))
        return MSem(h, name)

    def wait(self, eng, ev):
        if ev is None:
            return
        s, v = ev
        self.q[eng].append(("w", s.h, v))

    def op(self, eng, fn, inc=None, inck=1):
        incs = []
        ev = None
        if inc is not None:
            inc.n += inck
            incs.append((inc.h, inck))
            ev = (inc, inc.n)
        self.q[eng].append(("i", fn, incs))
        return ev

    def drain(self, eng):
        self.q[eng].append(("i", lambda e: e.drain(), []))

    def dma(self, eng, out, in_):
        sem = self.sD if eng == "sync" else self.sG
        return self.op(eng, lambda e, o=out, i=in_: e.dma_start(out=o, in_=i),
                       inc=sem, inck=16)

    def dma_ev(self, eng, out, in_):
        sem = self.sD if eng == "sync" else self.sG
        ev = self.op(eng, lambda e, o=out, i=in_: e.dma_start(out=o, in_=i),
                     inc=sem, inck=16)
        self.wait(eng, ev)
        self._flushed[eng if eng == "sync" else "gpsimd"] = sem.n
        return ev

    def dump(self, name, region, cols):
        if DBG == name:
            self.barrier()
            self.dma("sync", self.dbg[:, 0:cols], region[:, 0:cols])
            self.barrier()

    def dump_x(self, name):
        if DBG == name:
            self.barrier()
            T = self.T
            for c in range(self.FC):
                self.op("vector", lambda v, c=c: v.tensor_copy(
                    self.ARB[:, 0:T] if False else self.dbgbf[:, c * T:(c + 1) * T],
                    self.X[:, c * T:(c + 1) * T]))
                self.drain("vector")
            self.barrier()
            self.dma("sync", self.dbg[:, 0:self.FC * T], self.dbgbf[:, 0:self.FC * T])
            self.barrier()

    def dma_wait(self, waiter, queue_eng):
        sem = self.sD if queue_eng == "sync" else self.sG
        if self._flushed[queue_eng] != sem.n:
            self.wait(queue_eng, (sem, sem.n))
            self._flushed[queue_eng] = sem.n
        if waiter != queue_eng:
            self.wait(waiter, (sem, sem.n))

    def fence(self, waiter, producer):
        ev = self.op(producer, lambda e: e.drain(), inc=self.esem[producer])
        self.wait(waiter, ev)

    def barrier(self, engs=ENGS):
        if "sync" in engs:
            self.dma_wait("sync", "sync")
        if "gpsimd" in engs:
            self.dma_wait("gpsimd", "gpsimd")
        evs = []
        for e in engs:
            evs.append(self.op(e, lambda en: en.drain(), inc=self.esem[e]))
        for e in engs:
            for ev in evs:
                if ev[0] is not self.esem[e]:
                    self.wait(e, ev)
            for ds in (self.sD, self.sG):
                self.wait(e, (ds, ds.n))

    def replay(self, block):
        for e in ENGS:
            items = self.q[e]

            def body(eng, items=items):
                for it in items:
                    if it[0] == "w":
                        eng.wait_ge(it[1], it[2])
                    else:
                        ins = it[1](eng)
                        for (h, k) in it[2]:
                            ins.then_inc(h, k)

            getattr(block, e)(body)

    def din(self, name, shape, dtype=F32):
        self.dram[name] = self.nc.dram_tensor(name, shape, dtype, kind="ExternalInput").ap()
        return self.dram[name]

    def declare(self):
        nc, T = self.nc, self.T
        self.xT = self.din("xT", [D, T])
        self.yT = nc.dram_tensor("yT", [D, T], F32, kind="ExternalOutput").ap()
        self.dbg = nc.dram_tensor("dbg", [128, 40960], BF16, kind="ExternalOutput").ap() if DBG else None

        self.din("L0WqT", [D, D]); self.din("L0WkT", [D, D])
        self.din("L0WvT", [D, D]); self.din("L0WoT", [D, D])
        self.din("L0nrm", [1, 16])
        self.din("L0wg", [3072, D]); self.din("L0wu", [3072, D]); self.din("L0wd", [D, 3072])
        self.din("L1Wq", [D, D]); self.din("L1Wk", [D, 512])
        self.din("L1Wv", [D, 512]); self.din("L1Wo", [D, D]); self.din("L1bo", [128, 8])
        self.din("L1wg", [2816, D]); self.din("L1wu", [2816, D]); self.din("L1wd", [D, 2816])
        self.din("L2qkvT", [D, 3072]); self.din("L2projT", [D, D]); self.din("L2pb", [128, 8])
        self.din("L2ker", [1, 12]); self.din("L2gam", [128, 8]); self.din("L2phs", [128, 8])
        self.din("L2thf", [1, 1])
        self.din("L2th", [D, D]); self.din("L2ph", [D, D])
        self.din("L2thn", [D, D]); self.din("L2phn", [D, D])
        self.din("L3td", [128, 8]); self.din("L3tf", [128, 8])
        self.din("L3tmk", [128, 8]); self.din("L3tmv", [128, 8]); self.din("L3tmr", [128, 8])
        self.din("L3WkT", [D, D]); self.din("L3WvT", [D, D])
        self.din("L3WrT", [D, D]); self.din("L3WoT", [D, D])
        self.din("L3wg", [3072, D]); self.din("L3wu", [3072, D]); self.din("L3wd", [D, 3072])
        self.din("ident", [128, 128], BF16)
        self.din("identf", [128, 128], F32)
        self.din("ones_bf", [128, 128], BF16)
        self.din("ones_f", [1, 128], F32)
        self.din("triadd", [128, 128], F32)
        self.din("l1rope", [2, 128, T], BF16)
        self.din("l2rope", [2, 128, T], BF16)
        self.din("trampf", [128, T], F32)

        WSP = max(T, 1024)
        self.sp_gate = nc.dram_tensor("sp_gate", [3072, WSP], BF16).ap()
        self.sp_up = nc.dram_tensor("sp_up", [3072, WSP], BF16).ap()
        self.sp_g2 = nc.dram_tensor("sp_g2", [3072, WSP], BF16).ap()
        self.sp_o = nc.dram_tensor("sp_o", [3072, WSP], BF16).ap()
        self.sp_attn = nc.dram_tensor("sp_attn", [D, T], BF16).ap()
        self.cc_in = [nc.dram_tensor(f"cc_in{i}", [1, 2], F32).ap() for i in range(8)]
        self.cc_out = [nc.dram_tensor(f"cc_out{i}", [1, 2], F32).ap() for i in range(8)]

        sb = lambda n, s, d=BF16: self.stack.enter_context(nc.sbuf_tensor(n, s, d))
        ps = lambda n, s, d=F32: self.stack.enter_context(nc.psum_tensor(n, s, d))

        self.X = sb("X", [128, self.FC * T], F32)
        self.XN = sb("XN", [128, self.FC * T], BF16)
        self.ARB = sb("ARB", [128, 40960], BF16)
        self.ARF = sb("ARF", [128, 6144], F32)
        self.CONST = sb("CONST", [128, 256], BF16)
        self.ONESF = sb("ONESF", [1, 128], F32)
        self.TRIADD = sb("TRIADD", [128, 128], F32)
        self.IDENTF = sb("IDENTF", [128, 128], F32)
        self.STATS = sb("STATS", [128, 256], F32)
        self.GMAX = sb("GMAX", [128, 32], F32)
        self.CCOL = sb("CCOL", [128, 8], F32)

        self.dbgbf = self.ARB[:, 0:self.FC * T]
        self.ident = self.CONST[:, 0:128]
        self.ones_bf = self.CONST[:, 128:256]

        self.pb = [ps(f"pb{i}", [128, 512], F32) for i in range(8)]
        self.pt0 = self.pb[6].bitcast(BF16)
        self.pt1 = self.pb[7].bitcast(BF16)

        self.esem = {e: self.sem(f"S_{e}") for e in ENGS}
        self.sD = self.sem("S_dma_sync")
        self.sG = self.sem("S_dma_gps")
        self.sT = self.esem["tensor"]
        self.sV = self.esem["vector"]
        self.sS = self.esem["scalar"]
        self.sP = self.esem["gpsimd"]
        self.sCC = self.sem("S_cc")

    def init_loads(self):
        T = self.T
        for c in range(self.FC):
            self.dma("sync", self.X[:, c * T:(c + 1) * T], self.xT[c * 128:(c + 1) * 128, :])
        self.dma("sync", self.ident, self.dram["ident"][:, :])
        self.dma("sync", self.ones_bf, self.dram["ones_bf"][:, :])
        self.dma("sync", self.ONESF[:, :], self.dram["ones_f"][:, :])
        self.dma("sync", self.TRIADD[:, :], self.dram["triadd"][:, :])
        self.dma("sync", self.IDENTF[:, :], self.dram["identf"][:, :])
        self.op("vector", lambda v: v.memset(self.CCOL[:, 0:1], 1e-5))
        self.op("vector", lambda v: v.memset(self.CCOL[:, 1:2], math.pi / 2))
        self.op("vector", lambda v: v.memset(self.CCOL[:, 2:3], 0.0))
        self.op("vector", lambda v: v.memset(self.CCOL[:, 3:4], -math.pi))
        self.drain("vector")
        self.barrier()

    def replicate_cols(self, src_row_ap, n, dst_col):
        pbank = self.pb[5]
        self.fence("tensor", "vector")
        evm = self.op("tensor", lambda t, s=src_row_ap, n=n, p=pbank: t.matmul(
            p[:, 0:n], self.ONESF[:, :], s, start=True, stop=True), inc=self.sT)
        self.wait("vector", evm)
        evv = self.op("vector", lambda v, n=n, d=dst_col, p=pbank: v.tensor_copy(
            self.GMAX[:, d:d + n], p[:, 0:n]), inc=self.sV)
        self.wait("tensor", evv)
        return evv

    def col_max_to_scalar(self, col_ap, dst_col):
        pbank = self.pb[5]
        self.fence("tensor", "vector")
        ev = self.op("tensor", lambda t, c=col_ap, p=pbank: t.transpose(
            p[0:1, 0:128], c, self.IDENTF[:, :]), inc=self.sT)
        self.wait("vector", ev)
        evv = self.op("vector", lambda v, d=dst_col, p=pbank: v.tensor_reduce(
            out=self.STATS[0:1, d:d + 1], in_=p[0:1, 0:128], axis=AX, op=OP.max),
            inc=self.sV)
        self.wait("tensor", evv)
        return evv

    def part_tree_max(self, cols, dst_col):
        c0 = cols[0]
        for c in cols[1:]:
            self.op("vector", lambda v, a=c0, b=c: v.tensor_tensor(
                out=self.STATS[:, a:a + 1], in0=self.STATS[:, a:a + 1],
                in1=self.STATS[:, b:b + 1], op=OP.max))
            self.drain("vector")
        return self.col_max_to_scalar(self.STATS[:, c0:c0 + 1], dst_col)

    def collective_max(self, src_col, n_vals, dst_gmax_col):
        i = self.cc_idx
        self.cc_idx += 1
        ci, co = self.cc_in[i], self.cc_out[i]
        self.fence("gpsimd", "vector")
        self.dma("gpsimd", ci[0:1, 0:n_vals], self.STATS[0:1, src_col:src_col + n_vals])
        self.dma_wait("gpsimd", "gpsimd")
        evc = self.op("gpsimd", lambda g, ci=ci, co=co, n=n_vals: g.collective_compute(
            "AllReduce", OP.max, replica_groups=[list(range(NCORES))],
            ins=[ci[0:1, 0:n].opt()], outs=[co[0:1, 0:n].opt()]), inc=self.sCC)
        self.wait("gpsimd", evc)
        self.dma("gpsimd", self.STATS[0:1, 200:200 + n_vals], co[0:1, 0:n_vals])
        self.dma_wait("vector", "gpsimd")
        self.dma_wait("tensor", "gpsimd")
        return self.replicate_cols(self.STATS[0:1, 200:200 + n_vals], n_vals, dst_gmax_col)

    def make_inv_scale(self, raw_col, inv_col, scale_col, qmax):
        g = self.GMAX
        self.drain("vector")
        self.op("vector", lambda v: v.tensor_scalar(
            out=g[:, scale_col:scale_col + 1], in0=g[:, raw_col:raw_col + 1],
            scalar1=1.0 / qmax, scalar2=1e-12, op0=OP.mult, op1=OP.max))
        self.drain("vector")
        ev = self.op("vector", lambda v: v.reciprocal(
            out=g[:, inv_col:inv_col + 1], in_=g[:, scale_col:scale_col + 1]),
            inc=self.sV)
        self.drain("vector")
        return ev

    def layernorm(self):
        T, NT, FC = self.T, self.NT, self.FC
        xb = self.ARB[:, 0:512]
        xsq = self.ARB[:, 512:1024]
        mean = self.ARF[:, 0:512]
        var = self.ARF[:, 512:1024]
        rstd = self.ARF[:, 1024:1536]
        self.barrier()
        for tc in range(NT):
            pm, pq = self.pb[0], self.pb[1]
            self.fence("tensor", "vector")
            last_mm = None
            for c in range(FC):
                xsl = slice(c * T + tc * 512, c * T + (tc + 1) * 512)
                e1 = self.op("vector", lambda v, a=xsl: v.tensor_copy(xb[:], self.X[:, a]), inc=self.sV)
                e2 = self.op("scalar", lambda s, a=xsl: s.activation(
                    out=xsq[:], in_=self.X[:, a], func=AF.Square), inc=self.sS)
                self.wait("tensor", e1)
                self.wait("tensor", e2)
                self.op("tensor", lambda t, c0=c: t.matmul(
                    pm[:, 0:512], self.ones_bf, xb[:], start=(c0 == 0), stop=(c0 == FC - 1)))
                last_mm = self.op("tensor", lambda t, c0=c: t.matmul(
                    pq[:, 0:512], self.ones_bf, xsq[:], start=(c0 == 0), stop=(c0 == FC - 1)),
                    inc=self.sT)
                self.wait("vector", last_mm)
                self.wait("scalar", last_mm)
            self.wait("vector", last_mm)
            self.op("vector", lambda v: v.tensor_scalar(
                out=mean, in0=pm[:, 0:512], scalar1=1.0 / D, scalar2=None, op0=OP.mult))
            self.drain("vector")
            self.op("vector", lambda v: v.tensor_tensor(out=var, in0=mean, in1=mean, op=OP.mult))
            self.drain("vector")
            ev = self.op("vector", lambda v: v.scalar_tensor_tensor(
                out=var, in0=pq[:, 0:512], scalar=1.0 / D, in1=var,
                op0=OP.mult, op1=OP.subtract), inc=self.sV)
            self.wait("scalar", ev)
            ev = self.op("scalar", lambda s: s.activation(
                out=rstd, in_=var, func=AF.Sqrt, bias=self.CCOL[:, 0:1]), inc=self.sS)
            self.wait("vector", ev)
            self.op("vector", lambda v: v.reciprocal(out=rstd, in_=rstd))
            self.drain("vector")
            for c in range(FC):
                xsl = slice(c * T + tc * 512, c * T + (tc + 1) * 512)
                self.op("vector", lambda v, a=xsl: v.tensor_tensor(
                    out=var, in0=self.X[:, a], in1=mean, op=OP.subtract))
                self.drain("vector")
                self.op("vector", lambda v, a=xsl: v.tensor_tensor(
                    out=self.XN[:, a], in0=var, in1=rstd, op=OP.mult))
                self.drain("vector")
        self.barrier()

    def load_cast_w(self, name, wslot, wstg, width=1024, scale=None):
        last = None
        for c in range(self.FC):
            self.dma("sync", wstg[:, 0:width], self.dram[name][c * 128:(c + 1) * 128, :])
            self.dma_wait("vector", "sync")
            last = self.op("vector", lambda v, c=c: v.tensor_scalar(
                out=wslot[:, c * width:(c + 1) * width], in0=wstg[:, 0:width],
                scalar1=(1.0 if scale is None else scale), scalar2=None,
                op0=OP.mult), inc=self.sV)
            self.wait("sync", last)
        return last

    def l0_attn(self):
        T, NT, NT128, FC = self.T, self.NT, self.NT128, self.FC
        A = self.ARB[:, 0:8 * T]
        B = self.ARB[:, 8 * T:16 * T]
        WSLOT = self.ARB[:, 16 * T:16 * T + 8192]
        KV = self.ARF.bitcast(BF16)[:, 8192:8704]
        WSTG = self.ARF[:, 0:1024]
        ROWF = self.ARF[:, 1024:2048]
        VTS = self.ARF.bitcast(BF16)

        def project_tm(dst, wname, l2norm):
            ev = self.load_cast_w(wname, WSLOT, WSTG)
            self.wait("tensor", ev)
            for tc in range(NT128):
                for dc in range(2):
                    pbk = self.pb[(tc * 2 + dc) % 2]
                    self.fence("tensor", "vector")
                    last = None
                    for i in range(FC):
                        last = self.op("tensor", lambda t, k=i, i0=i, tc=tc, dc=dc, pbk=pbk: t.matmul(
                            pbk[:, 0:512],
                            self.XN[:, k * T + tc * 128:k * T + tc * 128 + 128],
                            WSLOT[:, k * 1024 + dc * 512:k * 1024 + dc * 512 + 512],
                            start=(i0 == 0), stop=(i0 == FC - 1)),
                            inc=self.sT if i == FC - 1 else None)
                    self.wait("vector", last)
                    self.op("vector", lambda v, tc=tc, dc=dc, pbk=pbk: v.tensor_copy(
                        dst[:, tc * 1024 + dc * 512: tc * 1024 + dc * 512 + 512],
                        pbk[:, 0:512]), inc=self.sV)
                    self.drain("vector")
                if l2norm:
                    sl = slice(tc * 1024, (tc + 1) * 1024)
                    self.fence("scalar", "vector")
                    ev = self.op("scalar", lambda s, a=sl: s.activation(
                        out=ROWF[:, 0:1024], in_=dst[:, a], func=AF.Square), inc=self.sS)
                    self.wait("vector", ev)
                    self.op("vector", lambda v: v.tensor_reduce(
                        out=self.STATS[:, 64:80],
                        in_=ROWF[:, 0:1024].rearrange("p (h e) -> p h e", e=64),
                        axis=AX, op=OP.add))
                    self.drain("vector")
                    self.fence("scalar", "vector")
                    ev = self.op("scalar", lambda s: s.activation(
                        out=self.STATS[:, 80:96], in_=self.STATS[:, 64:80],
                        func=AF.Sqrt), inc=self.sS)
                    self.wait("vector", ev)
                    self.op("vector", lambda v: v.tensor_scalar(
                        out=self.STATS[:, 80:96], in0=self.STATS[:, 80:96],
                        scalar1=1e-12, scalar2=None, op0=OP.max))
                    self.drain("vector")
                    self.op("vector", lambda v: v.reciprocal(
                        out=self.STATS[:, 80:96], in_=self.STATS[:, 80:96]))
                    self.drain("vector")
                    self.op("vector", lambda v, a=sl: v.tensor_tensor(
                        out=dst[:, a].rearrange("p (h e) -> p h e", e=64),
                        in0=dst[:, a].rearrange("p (h e) -> p h e", e=64),
                        in1=self.STATS[:, 80:96].unsqueeze(2).to_broadcast([128, 16, 64]),
                        op=OP.mult), inc=self.sV)
                    self.drain("vector")

        self.dma("sync", self.STATS[0:1, 0:16], self.dram["L0nrm"][:, :])
        self.dma_wait("scalar", "sync")
        self.op("scalar", lambda s: s.activation(
            out=self.STATS[0:1, 16:32], in_=self.STATS[0:1, 0:16], func=AF.Sigmoid))
        self.drain("scalar")
        ev = self.op("scalar", lambda s: s.activation(
            out=self.STATS[0:1, 16:32], in_=self.STATS[0:1, 16:32], func=AF.Exp,
            scale=-math.log(float(self.T))), inc=self.sS)
        self.wait("vector", ev)
        self.wait("tensor", ev)
        self.replicate_cols(self.STATS[0:1, 16:32], 16, 0)

        project_tm(A, "L0WqT", l2norm=True)
        self.barrier(("tensor", "vector", "scalar", "sync"))
        for tc in range(NT128):
            for dc in range(FC):
                pbt = self.pt0 if (tc * FC + dc) % 2 == 0 else self.pt1
                self.fence("tensor", "vector")
                ev = self.op("tensor", lambda t, tc=tc, dc=dc, pbt=pbt: t.transpose(
                    pbt[:, 0:128], A[:, tc * 1024 + dc * 128: tc * 1024 + dc * 128 + 128],
                    self.ident), inc=self.sT)
                self.wait("vector", ev)
                self.op("vector", lambda v, tc=tc, dc=dc, pbt=pbt: v.tensor_copy(
                    B[:, dc * T + tc * 128: dc * T + tc * 128 + 128], pbt[:, 0:128]),
                    inc=self.sV)
        self.barrier(("tensor", "vector", "sync"))
        project_tm(A, "L0WkT", l2norm=True)
        self.barrier(("tensor", "vector", "scalar", "sync"))

        ev = self.load_cast_w("L0WvT", WSLOT, WSTG)
        self.wait("tensor", ev)
        pkv = self.pb[2]
        kv_read_ev = None
        for tc in range(NT128):
            vt = VTS[:, 4096 + (tc % 2) * 2048: 4096 + (tc % 2) * 2048 + 1024]
            for dc in range(2):
                pbk = self.pb[3 + dc]
                self.fence("tensor", "vector")
                last = None
                for i in range(FC):
                    last = self.op("tensor", lambda t, k=i, i0=i, tc=tc, dc=dc, pbk=pbk: t.matmul(
                        pbk[:, 0:512],
                        self.XN[:, k * T + tc * 128:k * T + tc * 128 + 128],
                        WSLOT[:, k * 1024 + dc * 512:k * 1024 + dc * 512 + 512],
                        start=(i0 == 0), stop=(i0 == FC - 1)),
                        inc=self.sT if i == FC - 1 else None)
                self.wait("vector", last)
                if kv_read_ev is not None:
                    self.wait("vector", kv_read_ev)
                self.op("vector", lambda v, vt=vt, dc=dc, pbk=pbk: v.tensor_copy(
                    vt[:, dc * 512:dc * 512 + 512], pbk[:, 0:512]), inc=self.sV)
                self.drain("vector")
            evv = self.op("vector", lambda v: v.nop(), inc=self.sV)
            self.wait("tensor", evv)
            last = None
            for h in range(16):
                pbase = 64 * (h % 2)
                cbase = 64 * (h // 2)
                last = self.op("tensor", lambda t, h=h, tc=tc, vt=vt, pbase=pbase, cbase=cbase: t.matmul(
                    pkv[pbase:pbase + 64, cbase:cbase + 64],
                    A[:, tc * 1024 + h * 64: tc * 1024 + h * 64 + 64],
                    vt[:, h * 64:h * 64 + 64],
                    start=(tc == 0 and h < 2), stop=(tc == NT128 - 1 and h >= 14),
                    skip_group_check=True),
                    inc=self.sT if h == 15 else None)
            kv_read_ev = last
        self.wait("vector", kv_read_ev)
        for h in range(16):
            pbase = 64 * (h % 2)
            cbase = 64 * (h // 2)
            self.op("vector", lambda v, h=h, pbase=pbase, cbase=cbase: v.tensor_scalar(
                out=KV[pbase:pbase + 64, cbase:cbase + 64],
                in0=pkv[pbase:pbase + 64, cbase:cbase + 64],
                scalar1=self.GMAX[pbase:pbase + 64, h:h + 1], scalar2=None, op0=OP.mult))
            self.drain("vector")
        self.barrier(("tensor", "vector", "sync"))

        for h in range(16):
            pbase = 64 * (h % 2)
            cb = (h // 2) * T
            for tc in range(NT):
                pbk = self.pb[(h * NT + tc) % 2]
                self.fence("tensor", "vector")
                ev = self.op("tensor", lambda t, h=h, tc=tc, pbase=pbase, cb=cb, pbk=pbk: t.matmul(
                    pbk[0:64, 0:512],
                    KV[pbase:pbase + 64, (h // 2) * 64:(h // 2) * 64 + 64],
                    B[pbase:pbase + 64, cb + tc * 512:cb + tc * 512 + 512],
                    start=True, stop=True), inc=self.sT)
                self.wait("vector", ev)
                self.op("vector", lambda v, h=h, tc=tc, pbase=pbase, cb=cb, pbk=pbk: v.tensor_copy(
                    A[pbase:pbase + 64, cb + tc * 512:cb + tc * 512 + 512], pbk[0:64, 0:512]),
                    inc=self.sV)
        self.barrier(("tensor", "vector", "sync"))

        ev = self.load_cast_w("L0WoT", WSLOT, WSTG)
        self.wait("tensor", ev)
        for cc in range(FC):
            for tc in range(NT):
                pbk = self.pb[(cc * NT + tc) % 2]
                self.fence("tensor", "vector")
                last = None
                for i in range(FC):
                    last = self.op("tensor", lambda t, k=i, i0=i, cc=cc, tc=tc, pbk=pbk: t.matmul(
                        pbk[:, 0:512],
                        WSLOT[:, k * 1024 + cc * 128:k * 1024 + cc * 128 + 128],
                        A[:, k * T + tc * 512:k * T + tc * 512 + 512],
                        start=(i0 == 0), stop=(i0 == FC - 1)),
                        inc=self.sT if i == FC - 1 else None)
                self.wait("vector", last)
                self.op("vector", lambda v, cc=cc, tc=tc, pbk=pbk: v.tensor_tensor(
                    out=self.X[:, cc * T + tc * 512: cc * T + tc * 512 + 512],
                    in0=self.X[:, cc * T + tc * 512: cc * T + tc * 512 + 512],
                    in1=pbk[:, 0:512], op=OP.add), inc=self.sV)
                self.drain("vector")
        self.barrier()

    def qdq_xn(self):
        T, FC = self.T, self.FC
        for c in range(FC):
            self.op("vector", lambda v, c=c: v.tensor_reduce(
                out=self.STATS[:, 32 + c:33 + c], in_=self.XN[:, c * T:(c + 1) * T],
                axis=AX, op=OP.max, apply_absolute_value=True))
            self.drain("vector")
        self.part_tree_max(list(range(32, 32 + FC)), dst_col=0)
        self.collective_max(0, 1, 18)
        self.make_inv_scale(18, 16, 17, 32767.0)
        g = self.GMAX
        tmp = self.ARF[:, 0:512]
        for c in range(FC):
            for tc in range(self.NT):
                sl = slice(c * T + tc * 512, c * T + (tc + 1) * 512)
                self.op("vector", lambda v, a=sl: v.tensor_scalar(
                    out=tmp, in0=self.XN[:, a], scalar1=g[:, 16:17],
                    scalar2=RNE, op0=OP.mult, op1=OP.add))
                self.drain("vector")
                self.op("vector", lambda v, a=sl: v.tensor_scalar(
                    out=self.XN[:, a], in0=tmp, scalar1=RNE, scalar2=g[:, 17:18],
                    op0=OP.subtract, op1=OP.mult))
                self.drain("vector")
        self.barrier(("vector", "tensor", "sync"))

    def lpbq_chunk(self, wname, oc, dst, I):
        WSTG = self.ARF[:, 0:1024] if I == 1024 else self.ARF[:, 0:3072]
        scr = self.ARF[:, 3072:3072 + 1024] if I == 1024 else self.ARF[:, 3072:3072 + 3072]
        QN = self.ARB[:, 0:I]
        nb = I // 32
        self.fence("sync", "vector")
        self.dma("sync", WSTG[:, 0:I], self.dram[wname][oc * 128:(oc + 1) * 128, :])
        self.dma_wait("vector", "sync")
        self.op("vector", lambda v: v.tensor_reduce(
            out=self.STATS[:, 0:nb], in_=WSTG[:, 0:I].rearrange("p (b w) -> p b w", w=32),
            axis=AX, op=OP.max, apply_absolute_value=True))
        self.drain("vector")
        self.op("vector", lambda v: v.tensor_scalar(
            out=self.STATS[:, 0:nb], in0=self.STATS[:, 0:nb],
            scalar1=1.0 / 7.0, scalar2=1e-12, op0=OP.mult, op1=OP.max))
        self.drain("vector")
        self.op("vector", lambda v: v.reciprocal(
            out=self.STATS[:, 96:96 + nb], in_=self.STATS[:, 0:nb]))
        self.drain("vector")
        self.op("vector", lambda v: v.tensor_tensor(
            out=scr[:, 0:I].rearrange("p (b w) -> p b w", w=32),
            in0=WSTG[:, 0:I].rearrange("p (b w) -> p b w", w=32),
            in1=self.STATS[:, 96:96 + nb].unsqueeze(2).to_broadcast([128, nb, 32]),
            op=OP.mult))
        self.drain("vector")
        self.op("vector", lambda v: v.tensor_scalar(
            out=scr[:, 0:I], in0=scr[:, 0:I], scalar1=RNE, scalar2=RNE,
            op0=OP.add, op1=OP.subtract))
        self.drain("vector")
        self.op("vector", lambda v: v.tensor_tensor(
            out=QN[:, 0:I].rearrange("p (b w) -> p b w", w=32),
            in0=scr[:, 0:I].rearrange("p (b w) -> p b w", w=32),
            in1=self.STATS[:, 0:nb].unsqueeze(2).to_broadcast([128, nb, 32]),
            op=OP.mult))
        self.drain("vector")
        ev = self.op("vector", lambda v: v.nop(), inc=self.sV)
        self.wait("tensor", ev)
        evv = None
        for ic in range(I // 128):
            pbt = self.pt0 if ic % 2 == 0 else self.pt1
            if evv is not None:
                self.wait("tensor", evv)
            ev = self.op("tensor", lambda t, ic=ic, pbt=pbt: t.transpose(
                pbt[:, 0:128], QN[:, ic * 128:ic * 128 + 128], self.ident), inc=self.sT)
            self.wait("vector", ev)
            evv = self.op("vector", lambda v, ic=ic, pbt=pbt: v.tensor_copy(
                dst[:, ic * 128:ic * 128 + 128], pbt[:, 0:128]), inc=self.sV)
        return evv

    def qdq_ffn(self, wg, wu, wd, spills):
        T, NT, FC = self.T, self.NT, self.FC
        sp_gate, sp_up, sp_g2, sp_o = spills
        WS = self.ARB[:, 4096:4096 + 4096]
        SPT = self.ARB[:, 8192:8192 + 2048]
        SPT2 = self.ARB[:, 10240:10240 + 2048]
        TIN = self.ARB[:, 12288:12288 + 2048]
        TIN2 = self.ARB[:, 14336:14336 + 2048]
        g = self.GMAX
        sA = self.ARF[:, 2048 + 0:2048 + 512]
        sB = self.ARF[:, 2560:3072]
        sC = self.ARF[:, 2048 + 2048:2048 + 2560] if False else None

        self.op("vector", lambda v: v.memset(self.STATS[:, 100:102], 0.0))
        self.drain("vector")
        for oc in range(24):
            for j, (wname, spill, spt) in enumerate(((wg, sp_gate, SPT), (wu, sp_up, SPT2))):
                wslc = WS[:, j * 2048:(j + 1) * 2048]
                evq = self.lpbq_chunk(wname, oc, wslc, 1024)
                self.wait("tensor", evq)
                self.dma_wait("vector", "gpsimd")
                for tc in range(NT):
                    pbk = self.pb[tc % 2]
                    self.fence("tensor", "vector")
                    self.fence("tensor", "gpsimd")
                    last = None
                    for i in range(FC):
                        last = self.op("tensor", lambda t, k=i, i0=i, tc=tc, pbk=pbk, wslc=wslc: t.matmul(
                            pbk[:, 0:512],
                            wslc[:, k * 128:k * 128 + 128],
                            self.XN[:, k * T + tc * 512:k * T + tc * 512 + 512],
                            start=(i0 == 0), stop=(i0 == FC - 1)),
                            inc=self.sT if i == FC - 1 else None)
                    self.wait("vector", last)
                    st = spt[:, (tc % 4) * 512:(tc % 4) * 512 + 512]
                    self.op("vector", lambda v, st=st, pbk=pbk: v.tensor_copy(st, pbk[:, 0:512]))
                    self.drain("vector")
                    self.op("vector", lambda v, st=st: v.tensor_reduce(
                        out=self.STATS[:, 102:103], in_=st, axis=AX, op=OP.max,
                        apply_absolute_value=True))
                    self.drain("vector")
                    self.op("vector", lambda v, j=j: v.tensor_tensor(
                        out=self.STATS[:, 100 + j:101 + j], in0=self.STATS[:, 100 + j:101 + j],
                        in1=self.STATS[:, 102:103], op=OP.max))
                    self.drain("vector")
                    evv = self.op("vector", lambda v: v.nop(), inc=self.sV)
                    self.wait("gpsimd", evv)
                    self.dma("gpsimd", spill[oc * 128:(oc + 1) * 128, tc * 512:(tc + 1) * 512], st)
        self.barrier(("vector", "tensor", "gpsimd", "sync"))
        self.part_tree_max([100], dst_col=0)
        self.part_tree_max([101], dst_col=1)
        self.collective_max(0, 2, 24)
        self.make_inv_scale(24, 4, 5, 32767.0)
        self.make_inv_scale(25, 6, 7, 32767.0)
        self.barrier(("vector", "tensor", "gpsimd", "sync"))

        self.op("vector", lambda v: v.memset(self.STATS[:, 100:101], 0.0))
        self.drain("vector")
        totB = 24 * NT
        ldB, spB = {}, {}
        PF = 2

        def b_slot(idx):
            return TIN[:, (idx % 4) * 512:(idx % 4) * 512 + 512]

        def b_issue(idx):
            oc, tc = divmod(idx, NT)
            if idx >= 4:
                self.wait("sync", spB[idx - 4])
            ldB[idx] = self.dma_ev(
                "sync", b_slot(idx),
                sp_gate[oc * 128:(oc + 1) * 128, tc * 512:(tc + 1) * 512])
        for p in range(min(PF, totB)):
            b_issue(p)
        for idx in range(totB):
            oc, tc = divmod(idx, NT)
            ti = b_slot(idx)
            if idx + PF < totB:
                b_issue(idx + PF)
            self.wait("vector", ldB[idx])
            self.op("vector", lambda v, ti=ti: v.tensor_scalar(
                out=sA, in0=ti, scalar1=g[:, 4:5], scalar2=RNE, op0=OP.mult, op1=OP.add))
            self.drain("vector")
            self.op("vector", lambda v: v.tensor_scalar(
                out=sA, in0=sA, scalar1=RNE, scalar2=g[:, 5:6], op0=OP.subtract, op1=OP.mult))
            self.drain("vector")
            ev = self.op("vector", lambda v: v.nop(), inc=self.sV)
            self.wait("scalar", ev)
            ev = self.op("scalar", lambda s: s.activation(
                out=sB, in_=sA, func=AF.Sigmoid), inc=self.sS)
            self.wait("vector", ev)
            self.op("vector", lambda v: v.tensor_scalar(
                out=sB, in0=sB, scalar1=65536.0, scalar2=RNE, op0=OP.mult, op1=OP.add))
            self.drain("vector")
            self.op("vector", lambda v: v.tensor_scalar(
                out=sB, in0=sB, scalar1=RNE, scalar2=65535.0, op0=OP.subtract, op1=OP.min))
            self.drain("vector")
            self.op("vector", lambda v: v.tensor_scalar(
                out=sB, in0=sB, scalar1=2.0 ** -16, scalar2=None, op0=OP.mult))
            self.drain("vector")
            self.op("vector", lambda v, ti=ti: v.tensor_tensor(
                out=ti, in0=sA, in1=sB, op=OP.mult))
            self.drain("vector")
            self.op("vector", lambda v, ti=ti: v.tensor_reduce(
                out=self.STATS[:, 102:103], in_=ti, axis=AX, op=OP.max,
                apply_absolute_value=True))
            self.drain("vector")
            self.op("vector", lambda v: v.tensor_tensor(
                out=self.STATS[:, 100:101], in0=self.STATS[:, 100:101],
                in1=self.STATS[:, 102:103], op=OP.max))
            self.drain("vector")
            evv = self.op("vector", lambda v: v.nop(), inc=self.sV)
            self.wait("gpsimd", evv)
            spB[idx] = self.dma_ev(
                "gpsimd", sp_g2[oc * 128:(oc + 1) * 128, tc * 512:(tc + 1) * 512], ti)
        self.barrier(("vector", "tensor", "gpsimd", "scalar", "sync"))
        self.part_tree_max([100], dst_col=0)
        self.collective_max(0, 1, 26)
        self.make_inv_scale(26, 8, 9, 32767.0)
        self.barrier(("vector", "tensor", "gpsimd", "sync"))

        self.op("vector", lambda v: v.memset(self.STATS[:, 100:101], 0.0))
        self.drain("vector")
        totC = 24 * NT
        ldC, spC = {}, {}

        def c_issue(idx):
            oc, tc = divmod(idx, NT)
            ti = TIN[:, (idx % 4) * 512:(idx % 4) * 512 + 512]
            ti2 = TIN2[:, (idx % 4) * 512:(idx % 4) * 512 + 512]
            if idx >= 4:
                self.wait("sync", spC[idx - 4])
            self.dma("sync", ti, sp_g2[oc * 128:(oc + 1) * 128, tc * 512:(tc + 1) * 512])
            ldC[idx] = self.dma_ev(
                "sync", ti2, sp_up[oc * 128:(oc + 1) * 128, tc * 512:(tc + 1) * 512])
        for p in range(min(PF, totC)):
            c_issue(p)
        for idx in range(totC):
            oc, tc = divmod(idx, NT)
            ti = TIN[:, (idx % 4) * 512:(idx % 4) * 512 + 512]
            ti2 = TIN2[:, (idx % 4) * 512:(idx % 4) * 512 + 512]
            if idx + PF < totC:
                c_issue(idx + PF)
            self.wait("vector", ldC[idx])
            self.op("vector", lambda v, ti=ti: v.tensor_scalar(
                out=sA, in0=ti, scalar1=g[:, 8:9], scalar2=RNE, op0=OP.mult, op1=OP.add))
            self.drain("vector")
            self.op("vector", lambda v: v.tensor_scalar(
                out=sA, in0=sA, scalar1=RNE, scalar2=g[:, 9:10], op0=OP.subtract, op1=OP.mult))
            self.drain("vector")
            self.op("vector", lambda v, ti2=ti2: v.tensor_scalar(
                out=sB, in0=ti2, scalar1=g[:, 6:7], scalar2=RNE, op0=OP.mult, op1=OP.add))
            self.drain("vector")
            self.op("vector", lambda v: v.tensor_scalar(
                out=sB, in0=sB, scalar1=RNE, scalar2=g[:, 7:8], op0=OP.subtract, op1=OP.mult))
            self.drain("vector")
            self.op("vector", lambda v, ti=ti: v.tensor_tensor(
                out=ti, in0=sA, in1=sB, op=OP.mult))
            self.drain("vector")
            self.op("vector", lambda v, ti=ti: v.tensor_reduce(
                out=self.STATS[:, 102:103], in_=ti, axis=AX, op=OP.max,
                apply_absolute_value=True))
            self.drain("vector")
            self.op("vector", lambda v: v.tensor_tensor(
                out=self.STATS[:, 100:101], in0=self.STATS[:, 100:101],
                in1=self.STATS[:, 102:103], op=OP.max))
            self.drain("vector")
            evv = self.op("vector", lambda v: v.nop(), inc=self.sV)
            self.wait("gpsimd", evv)
            spC[idx] = self.dma_ev(
                "gpsimd", sp_o[oc * 128:(oc + 1) * 128, tc * 512:(tc + 1) * 512], ti)
        self.barrier(("vector", "tensor", "gpsimd", "sync"))
        self.part_tree_max([100], dst_col=0)
        self.collective_max(0, 1, 27)
        self.make_inv_scale(27, 10, 11, 32767.0)
        self.barrier(("vector", "tensor", "gpsimd", "sync"))

        WDQ = self.ARB[:, 4096 + 12288:4096 + 12288 + 24576]
        for oc2 in range(8):
            evv = self.lpbq_chunk(wd, oc2, None, 3072) if False else None
            self._lpbq_into_wdq(wd, oc2, WDQ)
        self.barrier(("vector", "tensor", "sync"))
        mmD = {}
        for tc in range(NT):
            self.fence("tensor", "vector")
            ldD = {}

            def d_issue(ic, tc=tc):
                gidx = tc * 24 + ic
                ti = TIN[:, (ic % 4) * 512:(ic % 4) * 512 + 512]
                if gidx >= 4:
                    self.wait("sync", mmD[gidx - 4])
                ldD[ic] = self.dma_ev(
                    "sync", ti, sp_o[ic * 128:(ic + 1) * 128, tc * 512:(tc + 1) * 512])
            for p in range(min(PF, 24)):
                d_issue(p)
            last_mm = None
            for ic in range(24):
                gidx = tc * 24 + ic
                ti = TIN[:, (ic % 4) * 512:(ic % 4) * 512 + 512]
                if ic + PF < 24:
                    d_issue(ic + PF)
                self.wait("vector", ldD[ic])
                self.op("vector", lambda v, ti=ti: v.tensor_scalar(
                    out=sA, in0=ti, scalar1=g[:, 10:11], scalar2=RNE, op0=OP.mult, op1=OP.add))
                self.drain("vector")
                ev = self.op("vector", lambda v, ti=ti: v.tensor_scalar(
                    out=ti, in0=sA, scalar1=RNE, scalar2=g[:, 11:12],
                    op0=OP.subtract, op1=OP.mult), inc=self.sV)
                self.drain("vector")
                self.wait("tensor", ev)
                for cc in range(8):
                    last_mm = self.op("tensor", lambda t, ic=ic, cc=cc, ti=ti: t.matmul(
                        self.pb[cc][:, 0:512],
                        WDQ[:, ic * 1024 + cc * 128:ic * 1024 + cc * 128 + 128],
                        ti, start=(ic == 0), stop=(ic == 23)),
                        inc=self.sT if cc == 7 else None)
                mmD[gidx] = last_mm
            for cc in range(8):
                self.wait("vector", last_mm)
                self.op("vector", lambda v, cc=cc, tc=tc: v.tensor_tensor(
                    out=self.X[:, cc * T + tc * 512: cc * T + tc * 512 + 512],
                    in0=self.X[:, cc * T + tc * 512: cc * T + tc * 512 + 512],
                    in1=self.pb[cc][:, 0:512], op=OP.add), inc=self.sV)
                self.drain("vector")
        self.barrier()

    def _lpbq_into_wdq(self, wd, oc2, WDQ):
        I = 3072
        WSTG = self.ARF[:, 0:3072]
        scr = self.ARF[:, 3072:3072 + 3072]
        QN = self.ARB[:, 0:I]
        nb = I // 32
        self.fence("sync", "vector")
        self.dma("sync", WSTG[:, 0:I], self.dram[wd][oc2 * 128:(oc2 + 1) * 128, :])
        self.dma_wait("vector", "sync")
        self.op("vector", lambda v: v.tensor_reduce(
            out=self.STATS[:, 0:nb], in_=WSTG[:, 0:I].rearrange("p (b w) -> p b w", w=32),
            axis=AX, op=OP.max, apply_absolute_value=True))
        self.drain("vector")
        self.op("vector", lambda v: v.tensor_scalar(
            out=self.STATS[:, 0:nb], in0=self.STATS[:, 0:nb],
            scalar1=1.0 / 7.0, scalar2=1e-12, op0=OP.mult, op1=OP.max))
        self.drain("vector")
        self.op("vector", lambda v: v.reciprocal(
            out=self.STATS[:, 96:96 + nb], in_=self.STATS[:, 0:nb]))
        self.drain("vector")
        self.op("vector", lambda v: v.tensor_tensor(
            out=scr[:, 0:I].rearrange("p (b w) -> p b w", w=32),
            in0=WSTG[:, 0:I].rearrange("p (b w) -> p b w", w=32),
            in1=self.STATS[:, 96:96 + nb].unsqueeze(2).to_broadcast([128, nb, 32]),
            op=OP.mult))
        self.drain("vector")
        self.op("vector", lambda v: v.tensor_scalar(
            out=scr[:, 0:I], in0=scr[:, 0:I], scalar1=RNE, scalar2=RNE,
            op0=OP.add, op1=OP.subtract))
        self.drain("vector")
        self.op("vector", lambda v: v.tensor_tensor(
            out=QN[:, 0:I].rearrange("p (b w) -> p b w", w=32),
            in0=scr[:, 0:I].rearrange("p (b w) -> p b w", w=32),
            in1=self.STATS[:, 0:nb].unsqueeze(2).to_broadcast([128, nb, 32]),
            op=OP.mult))
        self.drain("vector")
        ev = self.op("vector", lambda v: v.nop(), inc=self.sV)
        self.wait("tensor", ev)
        evv = None
        for ic in range(24):
            pbt = self.pt0 if ic % 2 == 0 else self.pt1
            if evv is not None:
                self.wait("tensor", evv)
            ev = self.op("tensor", lambda t, ic=ic, pbt=pbt: t.transpose(
                pbt[:, 0:128], QN[:, ic * 128:ic * 128 + 128], self.ident), inc=self.sT)
            self.wait("vector", ev)
            evv = self.op("vector", lambda v, ic=ic, pbt=pbt, oc2=oc2: v.tensor_copy(
                WDQ[:, ic * 1024 + oc2 * 128:ic * 1024 + oc2 * 128 + 128], pbt[:, 0:128]),
                inc=self.sV)
        self.wait("tensor", evv)

    def proj_fm(self, wname, width, dst, wslot, wstg, scale=None, rhs_region=None):
        rhs = rhs_region if rhs_region is not None else self.XN
        T, NT, FC = self.T, self.NT, self.FC
        ev = self.load_cast_w(wname, wslot, wstg, width=width, scale=scale)
        self.wait("tensor", ev)
        for dc in range(width // 128):
            for tc in range(NT):
                pbk = self.pb[(dc * NT + tc) % 2]
                self.fence("tensor", "vector")
                last = None
                for i in range(FC):
                    last = self.op("tensor", lambda t, k=i, i0=i, dc=dc, tc=tc, pbk=pbk: t.matmul(
                        pbk[:, 0:512],
                        wslot[:, k * width + dc * 128:k * width + dc * 128 + 128],
                        rhs[:, k * T + tc * 512:k * T + tc * 512 + 512],
                        start=(i0 == 0), stop=(i0 == FC - 1)),
                        inc=self.sT if i == FC - 1 else None)
                self.wait("vector", last)
                self.op("vector", lambda v, dc=dc, tc=tc, pbk=pbk: v.tensor_copy(
                    dst[:, dc * T + tc * 512: dc * T + tc * 512 + 512], pbk[:, 0:512]),
                    inc=self.sV)
                self.drain("vector")

    def l1_attn(self):
        T, NT, NT128, FC = self.T, self.NT, self.NT128, self.FC
        Q = self.ARB[:, 0:8 * T]
        K = self.ARB[:, 8 * T:12 * T]
        CTAB = self.ARB[:, 12 * T:13 * T]
        STAB = self.ARB[:, 13 * T:14 * T]
        WSLOT = self.ARB[:, 14 * T:14 * T + 8192]
        SHUF = self.ARB[:, 14 * T + 8192:15 * T + 8192]
        PT = WSLOT[:, 0:2048]
        KSTG = WSLOT[:, 2048:2048 + T]
        QSTG = WSLOT[:, 2048 + T:2048 + 2 * T]
        OUTTM = SHUF[:, 0:64]
        WSTG = self.ARF[:, 0:1024]
        BO = self.ARF[:, 5200:5208]
        VTS = self.ARF.bitcast(BF16)
        VA = VTS[:, 2048:2048 + self.NT128 * 520]

        self.dma("sync", CTAB, self.dram["l1rope"][0, :, :])
        self.dma("sync", STAB, self.dram["l1rope"][1, :, :])
        self.dma("sync", BO, self.dram["L1bo"][:, :])
        self.dma_wait("vector", "sync")
        self.dma_wait("tensor", "sync")

        mask = []
        for i in range(16):
            mask += [2 * i + 1, 2 * i]

        def rope_apply(region, nchunks):
            for c in range(nchunks):
                tile = region[:, c * T:(c + 1) * T]
                self.op("vector", lambda v, tile=tile: v.stream_shuffle(
                    out=SHUF, in_=tile, mask=mask))
                self.drain("vector")
                self.op("vector", lambda v, tile=tile: v.tensor_tensor(
                    out=tile, in0=tile, in1=CTAB, op=OP.mult))
                self.drain("vector")
                self.op("vector", lambda v: v.tensor_tensor(
                    out=SHUF, in0=SHUF, in1=STAB, op=OP.mult))
                self.drain("vector")
                self.op("vector", lambda v, tile=tile: v.tensor_tensor(
                    out=tile, in0=tile, in1=SHUF, op=OP.add))
                self.drain("vector")

        self.proj_fm("L1Wq", 1024, Q, WSLOT, WSTG, scale=64.0 ** -0.5)
        self.dump("l1_q", Q, 8 * T)
        rope_apply(Q, FC)
        self.dump("l1_qrope", Q, 8 * T)
        self.proj_fm("L1Wk", 512, K, WSLOT, WSTG)
        rope_apply(K, 4)
        self.dump("l1_krope", K, 4 * T)
        self.barrier(("tensor", "vector", "sync"))

        ev = self.load_cast_w("L1Wv", WSLOT, WSTG, width=512)
        self.wait("tensor", ev)
        for tc in range(NT128):
            vt = VA[:, tc * 520:(tc + 1) * 520]
            pbk = self.pb[tc % 2]
            self.fence("tensor", "vector")
            last = None
            for i in range(FC):
                last = self.op("tensor", lambda t, k=i, i0=i, tc=tc, pbk=pbk: t.matmul(
                    pbk[:, 0:512],
                    self.XN[:, k * T + tc * 128:k * T + tc * 128 + 128],
                    WSLOT[:, k * 512:k * 512 + 512],
                    start=(i0 == 0), stop=(i0 == FC - 1)),
                    inc=self.sT if i == FC - 1 else None)
            self.wait("vector", last)
            self.op("vector", lambda v, vt=vt, pbk=pbk: v.tensor_copy(
                vt.rearrange("p (h w) -> p h w", w=65)[:, :, 0:64],
                pbk[:, 0:512].rearrange("p (h w) -> p h w", w=64)))
            self.drain("vector")
            self.op("vector", lambda v, vt=vt: v.memset(
                vt.rearrange("p (h w) -> p h w", w=65)[:, :, 64:65], 1.0), inc=self.sV)
            self.drain("vector")
        self.barrier(("tensor", "vector", "scalar", "sync"))

        for h in range(16):
            kvh = h // 2
            kbase = 64 * (kvh % 2)
            kchunk = kvh // 2
            qbase = 64 * (h % 2)
            qchunk = h // 2
            self.fence("vector", "tensor")
            if h % 2 == 0:
                self.op("vector", lambda v, kbase=kbase, kchunk=kchunk: v.tensor_copy(
                    KSTG[0:64, :], K[kbase:kbase + 64, kchunk * T:(kchunk + 1) * T]))
                self.drain("vector")
            ev = self.op("vector", lambda v, qbase=qbase, qchunk=qchunk: v.tensor_copy(
                QSTG[0:64, :], Q[qbase:qbase + 64, qchunk * T:(qchunk + 1) * T]),
                inc=self.sV)
            self.drain("vector")
            self.wait("tensor", ev)
            for qb in range(NT):
                ppv = self.pb[2]
                self.fence("tensor", "vector")
                for kc in range(NT128):
                    pbk = self.pb[3 + kc % 2]
                    pt = PT[:, (kc % 4) * 512:(kc % 4) * 512 + 512]
                    self.op("tensor", lambda t, kc=kc, pbk=pbk, qb=qb: t.matmul(
                        pbk[:, 0:512],
                        KSTG[0:64, kc * 128:kc * 128 + 128],
                        QSTG[0:64, qb * 512:qb * 512 + 512],
                        start=True, stop=True), inc=self.sT)
                    self.wait("scalar", (self.sT, self.sT.n))
                    evx = self.op("scalar", lambda s, pt=pt, pbk=pbk: s.activation(
                        out=pt, in_=pbk[:, 0:512], func=AF.Exp), inc=self.sS)
                    self.wait("tensor", evx)
                    for qs in range(4):
                        self.op("tensor", lambda t, kc=kc, qs=qs, pt=pt, ppv=ppv, tcv=kc,
                                kvh=kvh: t.matmul(
                            ppv[:, 65 * qs:65 * qs + 65],
                            pt[:, qs * 128:qs * 128 + 128],
                            VA[:, tcv * 520 + 65 * kvh: tcv * 520 + 65 * kvh + 65],
                            start=(kc == 0 and qs == 0), stop=(kc == NT128 - 1 and qs == 3),
                            skip_group_check=True),
                            inc=self.sT if (kc == NT128 - 1 and qs == 3) else None)
                self.wait("vector", (self.sT, self.sT.n))
                for qs in range(4):
                    self.op("vector", lambda v, qs=qs, ppv=ppv: v.reciprocal(
                        out=self.STATS[:, 120:121], in_=ppv[:, 65 * qs + 64:65 * qs + 65]))
                    self.drain("vector")
                    ev = self.op("vector", lambda v, qs=qs, ppv=ppv: v.tensor_scalar(
                        out=OUTTM, in0=ppv[:, 65 * qs:65 * qs + 64],
                        scalar1=self.STATS[:, 120:121], scalar2=None, op0=OP.mult),
                        inc=self.sV)
                    self.drain("vector")
                    self.wait("tensor", ev)
                    pbt = self.pt0
                    evt = self.op("tensor", lambda t, pbt=pbt: t.transpose(
                        pbt[0:64, 0:128], OUTTM, self.ident), inc=self.sT)
                    self.wait("vector", evt)
                    evv = self.op("vector", lambda v, qb=qb, qs=qs, pbt=pbt, qbase=qbase,
                                  qchunk=qchunk: v.tensor_copy(
                        Q[qbase:qbase + 64,
                          qchunk * T + qb * 512 + qs * 128: qchunk * T + qb * 512 + qs * 128 + 128],
                        pbt[0:64, 0:128]), inc=self.sV)
                    self.wait("tensor", evv)
        self.barrier(("tensor", "vector", "scalar", "sync"))
        self.dump("l1_attnT", Q, 8 * T)

        ev = self.load_cast_w("L1Wo", WSLOT, WSTG)
        self.wait("tensor", ev)
        for cc in range(FC):
            for tc in range(NT):
                pbk = self.pb[(cc * NT + tc) % 2]
                self.fence("tensor", "vector")
                last = None
                for i in range(FC):
                    last = self.op("tensor", lambda t, k=i, i0=i, cc=cc, tc=tc, pbk=pbk: t.matmul(
                        pbk[:, 0:512],
                        WSLOT[:, k * 1024 + cc * 128:k * 1024 + cc * 128 + 128],
                        Q[:, k * T + tc * 512:k * T + tc * 512 + 512],
                        start=(i0 == 0), stop=(i0 == FC - 1)),
                        inc=self.sT if i == FC - 1 else None)
                self.wait("vector", last)
                self.op("vector", lambda v, cc=cc, tc=tc, pbk=pbk: v.scalar_tensor_tensor(
                    out=self.X[:, cc * T + tc * 512: cc * T + tc * 512 + 512],
                    in0=pbk[:, 0:512], scalar=BO[:, cc:cc + 1],
                    in1=self.X[:, cc * T + tc * 512: cc * T + tc * 512 + 512],
                    op0=OP.add, op1=OP.add), inc=self.sV)
                self.drain("vector")
        self.barrier()
        self.dump_x("l1_xout")

    def fp4_qdq(self, tile, width):
        nb = width // 16
        r = self.ARF[:, 1024:1024 + width]
        a = self.ARF[:, 1024 + width:1024 + 2 * width]
        t1 = self.ARF[:, 1024 + 2 * width:1024 + 3 * width]
        t2 = self.ARF[:, 1024 + 3 * width:1024 + 4 * width]
        TS, STT, TT = "vector", "vector", "vector"
        v3 = lambda ap: ap.rearrange("p (b w) -> p b w", w=16)
        self.op("vector", lambda v: v.tensor_reduce(
            out=self.STATS[:, 0:nb], in_=v3(tile), axis=AX, op=OP.max,
            apply_absolute_value=True))
        self.drain("vector")
        self.op("vector", lambda v: v.tensor_scalar(
            out=self.STATS[:, 0:nb], in0=self.STATS[:, 0:nb],
            scalar1=1.0 / 6.0, scalar2=1e-12, op0=OP.mult, op1=OP.max))
        self.drain("vector")
        self.op("vector", lambda v: v.reciprocal(
            out=self.STATS[:, 128:128 + nb], in_=self.STATS[:, 0:nb]))
        self.drain("vector")
        self.op("vector", lambda v: v.tensor_scalar(
            out=self.STATS[:, 0:nb], in0=self.STATS[:, 0:nb],
            scalar1=0.5, scalar2=None, op0=OP.mult))
        self.drain("vector")
        self.op("vector", lambda v: v.tensor_tensor(
            out=v3(r), in0=v3(tile),
            in1=self.STATS[:, 128:128 + nb].unsqueeze(2).to_broadcast([128, nb, 16]),
            op=OP.mult))
        self.drain("vector")
        self.op("vector", lambda v: v.tensor_scalar(
            out=a, in0=r, scalar1=2.0, scalar2=RNE, op0=OP.mult, op1=OP.add))
        self.drain("vector")
        self.op("vector", lambda v: v.tensor_scalar(
            out=a, in0=a, scalar1=RNE, scalar2=4.0, op0=OP.subtract, op1=OP.min))
        self.drain("vector")
        self.op("vector", lambda v: v.tensor_scalar(
            out=a, in0=a, scalar1=-4.0, scalar2=None, op0=OP.max))
        self.drain("vector")
        self.op("vector", lambda v: v.tensor_scalar(
            out=t1, in0=r, scalar1=2.5, scalar2=None, op0=OP.is_gt))
        self.drain("vector")
        for thr in (3.5, 5.0, 5.0):
            self.op("vector", lambda v, thr=thr: v.scalar_tensor_tensor(
                out=t1, in0=r, scalar=thr, in1=t1, op0=OP.is_gt, op1=OP.add))
            self.drain("vector")
        self.op("vector", lambda v: v.tensor_scalar(
            out=t2, in0=r, scalar1=-2.5, scalar2=None, op0=OP.is_lt))
        self.drain("vector")
        for thr in (-3.5, -5.0, -5.0):
            self.op("vector", lambda v, thr=thr: v.scalar_tensor_tensor(
                out=t2, in0=r, scalar=thr, in1=t2, op0=OP.is_lt, op1=OP.add))
            self.drain("vector")
        self.op("vector", lambda v: v.scalar_tensor_tensor(
            out=a, in0=t1, scalar=2.0, in1=a, op0=OP.mult, op1=OP.add))
        self.drain("vector")
        self.op("vector", lambda v: v.scalar_tensor_tensor(
            out=a, in0=t2, scalar=-2.0, in1=a, op0=OP.mult, op1=OP.add))
        self.drain("vector")
        self.op("vector", lambda v: v.tensor_tensor(
            out=v3(tile), in0=v3(a),
            in1=self.STATS[:, 0:nb].unsqueeze(2).to_broadcast([128, nb, 16]),
            op=OP.mult))
        self.drain("vector")

    def l1_ffn(self):
        T, NT, NT128, FC = self.T, self.NT, self.NT128, self.FC
        XQ4T = self.ARB[:, 0:8 * T]
        XTM = self.ARB[:, 8 * T:16 * T]
        WSL = self.ARB[:, 16 * T:16 * T + 2048]
        HSTRIP = self.ARB[:, 16 * T + 2048:16 * T + 2048 + 2048]
        WN = self.ARB[:, 16 * T + 4096:16 * T + 4096 + 1024]
        TIN = self.ARB[:, 16 * T + 5120:16 * T + 5120 + 2048]
        WSTG = self.ARF[:, 0:1024]
        WDQ = self.ARB[:, 16 * T + 8192 if False else 0:0]

        for tc in range(NT128):
            for fc in range(FC):
                pbt = self.pt0 if (tc * FC + fc) % 2 == 0 else self.pt1
                self.fence("tensor", "vector")
                ev = self.op("tensor", lambda t, tc=tc, fc=fc, pbt=pbt: t.transpose(
                    pbt[:, 0:128], self.XN[:, fc * T + tc * 128:fc * T + tc * 128 + 128],
                    self.ident), inc=self.sT)
                self.wait("vector", ev)
                self.op("vector", lambda v, tc=tc, fc=fc, pbt=pbt: v.tensor_copy(
                    XTM[:, tc * 1024 + fc * 128:tc * 1024 + fc * 128 + 128], pbt[:, 0:128]),
                    inc=self.sV)
        self.fence("vector", "tensor")
        for tc in range(NT128):
            self.fp4_qdq(XTM[:, tc * 1024:(tc + 1) * 1024], 1024)
        self.fence("tensor", "vector")
        for tc in range(NT128):
            for fc in range(FC):
                pbt = self.pt0 if (tc * FC + fc) % 2 == 0 else self.pt1
                self.fence("tensor", "vector")
                ev = self.op("tensor", lambda t, tc=tc, fc=fc, pbt=pbt: t.transpose(
                    pbt[:, 0:128], XTM[:, tc * 1024 + fc * 128:tc * 1024 + fc * 128 + 128],
                    self.ident), inc=self.sT)
                self.wait("vector", ev)
                self.op("vector", lambda v, tc=tc, fc=fc, pbt=pbt: v.tensor_copy(
                    XQ4T[:, fc * T + tc * 128:fc * T + tc * 128 + 128], pbt[:, 0:128]),
                    inc=self.sV)
        self.barrier(("tensor", "vector", "sync"))

        IC = 2816 // 128
        for ic in range(IC):
            for j, wname in enumerate(("L1wg", "L1wu")):
                self.fence("sync", "vector")
                self.dma("sync", WSTG[:, :], self.dram[wname][ic * 128:(ic + 1) * 128, :])
                self.dma_wait("vector", "sync")
                self.op("vector", lambda v: v.tensor_copy(WN[:, 0:1024], WSTG[:, :]))
                self.drain("vector")
                self.fp4_qdq(WN[:, 0:1024], 1024)
                self.fence("tensor", "vector")
                for fc in range(FC):
                    pbt = self.pt0 if fc % 2 == 0 else self.pt1
                    ev = self.op("tensor", lambda t, fc=fc, pbt=pbt: t.transpose(
                        pbt[:, 0:128], WN[:, fc * 128:fc * 128 + 128], self.ident),
                        inc=self.sT)
                    self.wait("vector", ev)
                    evv = self.op("vector", lambda v, fc=fc, j=j, pbt=pbt: v.tensor_copy(
                        WSL[:, j * 1024 + fc * 128:j * 1024 + fc * 128 + 128], pbt[:, 0:128]),
                        inc=self.sV)
                    self.wait("tensor", evv)
            self.fence("tensor", "vector")
            for tc in range(NT128):
                pg, py = self.pb[0], self.pb[1]
                self.fence("tensor", "vector")
                self.fence("tensor", "scalar")
                lg = ly = None
                for i in range(FC):
                    lg = self.op("tensor", lambda t, k=i, i0=i, tc=tc, pg=pg: t.matmul(
                        pg[:, 0:128],
                        XQ4T[:, k * T + tc * 128:k * T + tc * 128 + 128],
                        WSL[:, 0 * 1024 + k * 128:0 * 1024 + k * 128 + 128],
                        start=(i0 == 0), stop=(i0 == FC - 1)),
                        inc=self.sT if i == FC - 1 else None)
                for i in range(FC):
                    ly = self.op("tensor", lambda t, k=i, i0=i, tc=tc, py=py: t.matmul(
                        py[:, 0:128],
                        XQ4T[:, k * T + tc * 128:k * T + tc * 128 + 128],
                        WSL[:, 1 * 1024 + k * 128:1 * 1024 + k * 128 + 128],
                        start=(i0 == 0), stop=(i0 == FC - 1)),
                        inc=self.sT if i == FC - 1 else None)
                self.wait("scalar", lg)
                evs = self.op("scalar", lambda s, tc=tc, pg=pg: s.activation(
                    out=HSTRIP[:, tc * 128:tc * 128 + 128], in_=pg[:, 0:128],
                    func=AF.Silu), inc=self.sS)
                self.wait("vector", ly)
                self.wait("vector", evs)
                self.op("vector", lambda v, tc=tc, py=py: v.tensor_tensor(
                    out=HSTRIP[:, tc * 128:tc * 128 + 128],
                    in0=HSTRIP[:, tc * 128:tc * 128 + 128], in1=py[:, 0:128],
                    op=OP.mult), inc=self.sV)
                self.drain("vector")
                self.wait("tensor", (self.sV, self.sV.n))
            for half in range(2):
                self.fp4_qdq(HSTRIP[:, half * 1024:(half + 1) * 1024], 1024)
            self.fence("tensor", "vector")
            self.dma_wait("vector", "gpsimd")
            for tc in range(NT128):
                pbt = self.pt0 if tc % 2 == 0 else self.pt1
                ev = self.op("tensor", lambda t, tc=tc, pbt=pbt: t.transpose(
                    pbt[:, 0:128], HSTRIP[:, tc * 128:tc * 128 + 128], self.ident),
                    inc=self.sT)
                self.wait("vector", ev)
                evv = self.op("vector", lambda v, tc=tc, ic=ic, pbt=pbt: v.tensor_copy(
                    TIN[:, (tc % 8) * 128:(tc % 8) * 128 + 128] if False else
                    TIN[:, (tc % 16) * 128:(tc % 16) * 128 + 128], pbt[:, 0:128]),
                    inc=self.sV)
                self.wait("tensor", evv)
                self.wait("gpsimd", evv)
                self.dma("gpsimd",
                         self.sp_gate[ic * 128:(ic + 1) * 128, tc * 128:(tc + 1) * 128],
                         TIN[:, (tc % 16) * 128:(tc % 16) * 128 + 128])
        self.barrier(("tensor", "vector", "scalar", "gpsimd", "sync"))

        WDQL = self.ARB[:, 0:IC * 1024]
        WN3 = self.ARB[:, IC * 1024:IC * 1024 + 2816]
        TIN2 = self.ARB[:, IC * 1024 + 2816:IC * 1024 + 2816 + 2048]
        WSTG3 = self.ARF[:, 0:1024]
        for oc in range(8):
            for seg, (c0, w) in enumerate(((0, 1024), (1024, 1024), (2048, 768))):
                self.fence("sync", "vector")
                self.dma("sync", WSTG3[:, 0:w], self.dram["L1wd"][oc * 128:(oc + 1) * 128, c0:c0 + w])
                self.dma_wait("vector", "sync")
                self.op("vector", lambda v, c0=c0, w=w: v.tensor_copy(
                    WN3[:, c0:c0 + w], WSTG3[:, 0:w]))
                self.drain("vector")
                self.fp4_qdq(WN3[:, c0:c0 + w], w)
            self.fence("tensor", "vector")
            for ic in range(IC):
                pbt = self.pt0 if ic % 2 == 0 else self.pt1
                ev = self.op("tensor", lambda t, ic=ic, pbt=pbt: t.transpose(
                    pbt[:, 0:128], WN3[:, ic * 128:ic * 128 + 128], self.ident),
                    inc=self.sT)
                self.wait("vector", ev)
                evv = self.op("vector", lambda v, ic=ic, oc=oc, pbt=pbt: v.tensor_copy(
                    WDQL[:, ic * 1024 + oc * 128:ic * 1024 + oc * 128 + 128], pbt[:, 0:128]),
                    inc=self.sV)
                self.wait("tensor", evv)
        self.barrier(("tensor", "vector", "sync"))
        for tc in range(NT):
            self.fence("tensor", "vector")
            lasts = {}
            for ic in range(IC):
                ti = TIN2[:, (ic % 2) * 1024:(ic % 2) * 1024 + 1024]
                self.fence("sync", "tensor")
                self.dma("sync", ti[:, 0:512],
                         self.sp_gate[ic * 128:(ic + 1) * 128, tc * 512:(tc + 1) * 512])
                self.dma_wait("tensor", "sync")
                for cc in range(self.FC):
                    lasts[cc] = self.op("tensor", lambda t, ic=ic, cc=cc, ti=ti: t.matmul(
                        self.pb[cc][:, 0:512],
                        WDQL[:, ic * 1024 + cc * 128:ic * 1024 + cc * 128 + 128],
                        ti[:, 0:512], start=(ic == 0), stop=(ic == IC - 1)),
                        inc=self.sT if ic == IC - 1 else None)
            for cc in range(self.FC):
                self.wait("vector", lasts[cc])
                self.op("vector", lambda v, cc=cc, tc=tc: v.tensor_tensor(
                    out=self.X[:, cc * T + tc * 512: cc * T + tc * 512 + 512],
                    in0=self.X[:, cc * T + tc * 512: cc * T + tc * 512 + 512],
                    in1=self.pb[cc][:, 0:512], op=OP.add), inc=self.sV)
                self.drain("vector")
        self.barrier()
    def l2_ffn(self):
        T, NT, FC = self.T, self.NT, self.FC
        CW = self.ARB[:, 0:8192]
        SW = self.ARB[:, 8192:16384]
        W1 = self.ARF[:, 0:1024]
        W2 = self.ARF[:, 1024:2048]
        S1 = self.ARF[:, 2048:2560]
        S2 = self.ARF[:, 2560:3072]
        for j, (wa, wb, dst, bias) in enumerate((("L2th", "L2thn", CW, self.CCOL[:, 1:2]),
                                                 ("L2ph", "L2phn", SW, None))):
            for c in range(FC):
                self.fence("sync", "vector")
                self.fence("sync", "scalar")
                self.dma("sync", W1, self.dram[wa][c * 128:(c + 1) * 128, :])
                self.dma("sync", W2, self.dram[wb][c * 128:(c + 1) * 128, :])
                self.dma_wait("vector", "sync")
                ev = self.op("vector", lambda v: v.tensor_tensor(
                    out=W1, in0=W1, in1=W2, op=OP.add), inc=self.sV)
                self.wait("scalar", ev)
                ev = self.op("scalar", lambda s, c=c, dst=dst, bias=bias: s.activation(
                    out=dst[:, c * 1024:(c + 1) * 1024], in_=W1, func=AF.Sin,
                    **({"bias": bias} if bias is not None else {})), inc=self.sS)
                self.wait("vector", ev)
                self.wait("sync", ev)
        self.barrier(("tensor", "vector", "scalar", "sync"))
        for oc in range(FC):
            for tc in range(NT):
                pr, pi = self.pb[(tc % 2) * 2], self.pb[(tc % 2) * 2 + 1]
                self.fence("tensor", "vector")
                self.fence("tensor", "scalar")
                for (pbk, wsl) in ((pr, CW), (pi, SW)):
                    last = None
                    for i in range(FC):
                        last = self.op("tensor", lambda t, k=i, i0=i, oc=oc, tc=tc, pbk=pbk, wsl=wsl: t.matmul(
                            pbk[:, 0:512],
                            wsl[:, k * 1024 + oc * 128:k * 1024 + oc * 128 + 128],
                            self.XN[:, k * T + tc * 512:k * T + tc * 512 + 512],
                            start=(i0 == 0), stop=(i0 == FC - 1)),
                            inc=self.sT if i == FC - 1 else None)
                self.wait("scalar", last)
                self.op("scalar", lambda s, pr=pr: s.activation(out=S1, in_=pr[:, 0:512], func=AF.Square))
                self.drain("scalar")
                ev = self.op("scalar", lambda s, pi=pi: s.activation(out=S2, in_=pi[:, 0:512], func=AF.Square),
                             inc=self.sS)
                self.wait("vector", ev)
                ev = self.op("vector", lambda v: v.tensor_tensor(out=S1, in0=S1, in1=S2, op=OP.add),
                             inc=self.sV)
                self.drain("vector")
                self.wait("scalar", ev)
                ev = self.op("scalar", lambda s: s.activation(out=S2, in_=S1, func=AF.Sqrt), inc=self.sS)
                self.wait("vector", ev)
                self.op("vector", lambda v, oc=oc, tc=tc: v.tensor_tensor(
                    out=self.X[:, oc * T + tc * 512:oc * T + tc * 512 + 512],
                    in0=self.X[:, oc * T + tc * 512:oc * T + tc * 512 + 512],
                    in1=S2, op=OP.add), inc=self.sV)
                self.drain("vector")
        self.barrier()

    def l2_attn(self):
        T, NT, NT128, FC = self.T, self.NT, self.NT128, self.FC
        Q = self.ARB[:, 0:8 * T]
        K = self.ARB[:, 8 * T:16 * T]
        WIN = self.ARB[:, 16 * T:16 * T + 8192]
        WINF = WIN.bitcast(F32)
        WSTG = self.ARF[:, 0:1024]
        VTS = self.ARF.bitcast(BF16)
        PB = self.XN
        KTAR = float(max(1, int(T * 0.1)))

        g100 = self.STATS[:, 140:148]
        ph025 = self.STATS[:, 148:156]
        kcol = self.STATS[:, 156:168]
        th100 = self.STATS[:, 168:169]
        pbias = self.STATS[:, 169:177]
        self.dma("sync", g100, self.dram["L2gam"][:, :])
        self.dma("sync", ph025, self.dram["L2phs"][:, :])
        self.dma("sync", pbias, self.dram["L2pb"][:, :])
        self.dma("sync", self.STATS[0:1, 200:212], self.dram["L2ker"][:, :])
        self.dma("sync", self.STATS[0:1, 212:213], self.dram["L2thf"][:, :])
        self.dma_wait("vector", "sync")
        self.dma_wait("tensor", "sync")
        self.replicate_cols(self.STATS[0:1, 200:212], 12, 0)
        self.drain("vector")
        self.op("vector", lambda v: v.tensor_copy(kcol, self.GMAX[:, 0:12]))
        self.drain("vector")
        self.replicate_cols(self.STATS[0:1, 212:213], 1, 12)
        self.drain("vector")
        self.op("vector", lambda v: v.tensor_scalar(
            out=th100, in0=self.GMAX[:, 12:13], scalar1=0.01, scalar2=None, op0=OP.mult))
        self.drain("vector")
        self.op("vector", lambda v: v.tensor_scalar(
            out=g100, in0=g100, scalar1=0.01, scalar2=None, op0=OP.mult))
        self.drain("vector")
        self.op("vector", lambda v: v.tensor_scalar(
            out=ph025, in0=ph025, scalar1=1.0 / (2 * math.pi), scalar2=0.25,
            op0=OP.mult, op1=OP.add), inc=self.sV)
        self.drain("vector")

        WSL = WIN[:, 0:2048]
        CHK = WIN[:, 2048:2048 + T]
        GD = WIN[:, 2048 + T:2048 + 2 * T]

        def qkv_slice(j, dst_tile_fn, scale=None):
            for k in range(FC):
                self.fence("sync", "vector")
                self.dma("sync", WSTG[:, 0:128],
                         self.dram["L2qkvT"][k * 128:(k + 1) * 128, j * 128:(j + 1) * 128])
                self.dma_wait("vector", "sync")
                self.op("vector", lambda v, k=k: v.tensor_scalar(
                    out=WSL[:, k * 128:(k + 1) * 128], in0=WSTG[:, 0:128],
                    scalar1=(1.0 if scale is None else scale), scalar2=None, op0=OP.mult),
                    inc=self.sV)
                self.drain("vector")
            self.fence("tensor", "vector")
            for tc in range(NT):
                pbk = self.pb[tc % 2]
                self.fence("tensor", "vector")
                last = None
                for i in range(FC):
                    last = self.op("tensor", lambda t, k=i, i0=i, tc=tc, pbk=pbk: t.matmul(
                        pbk[:, 0:512],
                        WSL[:, k * 128:k * 128 + 128],
                        self.XN[:, k * T + tc * 512:k * T + tc * 512 + 512],
                        start=(i0 == 0), stop=(i0 == FC - 1)),
                        inc=self.sT if i == FC - 1 else None)
                self.wait("vector", last)
                dst_tile_fn(tc, pbk)

        def dendritic(tile, kc0):
            self.dma_wait("vector", "gpsimd")
            self.op("vector", lambda v, tile=tile: v.tensor_scalar(
                out=GD, in0=tile, scalar1=kcol[:, kc0 + 3:kc0 + 4], scalar2=None, op0=OP.mult))
            self.drain("vector")
            for sh, ki in ((1, 2), (2, 1), (3, 0)):
                self.op("vector", lambda v, sh=sh, ki=ki, tile=tile: v.scalar_tensor_tensor(
                    out=GD[:, sh:T], in0=tile[:, 0:T - sh], scalar=kcol[:, kc0 + ki:kc0 + ki + 1],
                    in1=GD[:, sh:T], op0=OP.mult, op1=OP.add))
                self.drain("vector")
            self.op("vector", lambda v, tile=tile: v.tensor_copy(tile, GD))
            self.drain("vector")

        VB = WIN[:, 2048 + 2 * T:2048 + 2 * T + 512] if False else GD
        for dc in range(FC):
            def v_evac(tc, pbk, dc=dc):
                self.op("vector", lambda v, tc=tc, pbk=pbk: v.tensor_copy(
                    CHK[:, tc * 512:tc * 512 + 512], pbk[:, 0:512]), inc=self.sV)
                self.drain("vector")
            qkv_slice(16 + dc, v_evac)
            dendritic(CHK, 8)
            self.fence("tensor", "vector")
            for tc in range(NT128):
                pbt = self.pt0 if tc % 2 == 0 else self.pt1
                ev = self.op("tensor", lambda t, tc=tc, pbt=pbt: t.transpose(
                    pbt[:, 0:128], CHK[:, tc * 128:tc * 128 + 128], self.ident), inc=self.sT)
                self.wait("vector", ev)
                self.dma_wait("vector", "gpsimd")
                evv = self.op("vector", lambda v, tc=tc, pbt=pbt: v.tensor_copy(
                    GD[:, (tc % 8) * 128:(tc % 8) * 128 + 128], pbt[:, 0:128]),
                    inc=self.sV)
                self.wait("tensor", evv)
                self.wait("gpsimd", evv)
                self.dma("gpsimd",
                         self.sp_g2[tc * 128:(tc + 1) * 128, dc * 128:(dc + 1) * 128],
                         GD[:, (tc % 8) * 128:(tc % 8) * 128 + 128])
        for dc in range(FC):
            def q_evac(tc, pbk, dc=dc):
                self.op("vector", lambda v, tc=tc, dc=dc, pbk=pbk: v.tensor_copy(
                    Q[:, dc * T + tc * 512:dc * T + tc * 512 + 512], pbk[:, 0:512]), inc=self.sV)
                self.drain("vector")
            qkv_slice(dc, q_evac, scale=64.0 ** -0.5)
        for dc in range(FC):
            def k_evac(tc, pbk, dc=dc):
                self.op("vector", lambda v, tc=tc, dc=dc, pbk=pbk: v.tensor_copy(
                    K[:, dc * T + tc * 512:dc * T + tc * 512 + 512], pbk[:, 0:512]), inc=self.sV)
                self.drain("vector")
            qkv_slice(8 + dc, k_evac)
        self.barrier()
        for dc in range(FC):
            dendritic(Q[:, dc * T:(dc + 1) * T], 0)
            dendritic(K[:, dc * T:(dc + 1) * T], 4)
        self.barrier(("tensor", "vector", "sync"))

        TRF = WINF[:, 0:2048][:, 0:T]
        SCR = WINF[:, 2048:4096][:, 0:T]
        self.dma("sync", TRF, self.dram["trampf"][:, 0:T])
        self.dma_wait("vector", "sync")
        self.dma_wait("scalar", "sync")
        CTP = [self.pb[2 + i] for i in range(4)]
        self.fence("vector", "tensor")
        RNEC = 12582912.0
        FLT = self.XN.bitcast(F32)[:, 0:T]
        self.op("vector", lambda v: v.tensor_scalar(
            out=SCR, in0=TRF, scalar1=th100[:, 0:1], scalar2=0.25,
            op0=OP.mult, op1=OP.add))
        self.drain("vector")
        self.op("vector", lambda v: v.tensor_scalar(
            out=FLT, in0=SCR, scalar1=-0.5, scalar2=None, op0=OP.add))
        self.drain("vector")
        self.op("vector", lambda v: v.tensor_scalar(
            out=FLT, in0=FLT, scalar1=RNEC, scalar2=None, op0=OP.add))
        self.drain("vector")
        self.op("vector", lambda v: v.tensor_scalar(
            out=FLT, in0=FLT, scalar1=-RNEC, scalar2=None, op0=OP.add))
        self.drain("vector")
        self.op("vector", lambda v: v.tensor_tensor(
            out=SCR, in0=SCR, in1=FLT, op=OP.subtract))
        self.drain("vector")
        ev = self.op("vector", lambda v: v.nop(), inc=self.sV)
        self.wait("scalar", ev)
        ev = self.op("scalar", lambda s: s.activation(
            out=SCR, in_=SCR, func=AF.Sin, scale=2 * math.pi, bias=self.CCOL[:, 3:4]),
            inc=self.sS)
        self.wait("vector", ev)
        for i in range(T // 512):
            self.op("vector", lambda v, i=i: v.tensor_scalar(
                out=CTP[i][:, 0:512], in0=SCR[:, i * 512:(i + 1) * 512],
                scalar1=-0.2, scalar2=0.5, op0=OP.mult, op1=OP.add))
            self.drain("vector")
        for dc in range(FC):
            self.op("vector", lambda v, dc=dc: v.tensor_scalar(
                out=SCR, in0=TRF, scalar1=g100[:, dc:dc + 1], scalar2=ph025[:, dc:dc + 1],
                op0=OP.mult, op1=OP.add))
            self.drain("vector")
            self.op("vector", lambda v: v.tensor_scalar(
                out=FLT, in0=SCR, scalar1=-0.5, scalar2=None, op0=OP.add))
            self.drain("vector")
            self.op("vector", lambda v: v.tensor_scalar(
                out=FLT, in0=FLT, scalar1=RNEC, scalar2=None, op0=OP.add))
            self.drain("vector")
            self.op("vector", lambda v: v.tensor_scalar(
                out=FLT, in0=FLT, scalar1=-RNEC, scalar2=None, op0=OP.add))
            self.drain("vector")
            self.op("vector", lambda v: v.tensor_tensor(
                out=SCR, in0=SCR, in1=FLT, op=OP.subtract))
            self.drain("vector")
            ev = self.op("vector", lambda v: v.nop(), inc=self.sV)
            self.wait("scalar", ev)
            ev = self.op("scalar", lambda s: s.activation(
                out=SCR, in_=SCR, func=AF.Sin, scale=2 * math.pi, bias=self.CCOL[:, 3:4]),
                inc=self.sS)
            self.wait("vector", ev)
            for i in range(T // 512):
                self.op("vector", lambda v, i=i: v.scalar_tensor_tensor(
                    out=SCR[:, i * 512:(i + 1) * 512], in0=SCR[:, i * 512:(i + 1) * 512],
                    scalar=-0.3, in1=CTP[i][:, 0:512], op0=OP.mult, op1=OP.add))
                self.drain("vector")
            for reg in (Q, K):
                self.op("vector", lambda v, dc=dc, reg=reg: v.tensor_tensor(
                    out=reg[:, dc * T:(dc + 1) * T], in0=reg[:, dc * T:(dc + 1) * T],
                    in1=SCR, op=OP.mult))
                self.drain("vector")
        self.barrier(("tensor", "vector", "scalar", "sync"))

        CTAB = WIN[:, 0:T]
        STAB = WIN[:, T:2 * T]
        SHUF = WIN[:, 2 * T:3 * T]
        self.dma("sync", CTAB, self.dram["l2rope"][0, :, 0:T])
        self.dma("sync", STAB, self.dram["l2rope"][1, :, 0:T])
        self.dma_wait("vector", "sync")
        mask = []
        for i in range(16):
            mask += [2 * i + 1, 2 * i]
        for reg in (Q, K):
            for c in range(FC):
                tile = reg[:, c * T:(c + 1) * T]
                self.op("vector", lambda v, tile=tile: v.stream_shuffle(out=SHUF, in_=tile, mask=mask))
                self.drain("vector")
                self.op("vector", lambda v, tile=tile: v.tensor_tensor(
                    out=tile, in0=tile, in1=CTAB, op=OP.mult))
                self.drain("vector")
                self.op("vector", lambda v: v.tensor_tensor(out=SHUF, in0=SHUF, in1=STAB, op=OP.mult))
                self.drain("vector")
                self.op("vector", lambda v, tile=tile: v.tensor_tensor(
                    out=tile, in0=tile, in1=SHUF, op=OP.add))
                self.drain("vector")
        self.barrier(("tensor", "vector", "sync"))
        for c in range(FC):
            self.dma("gpsimd", self.sp_gate[c * 128:(c + 1) * 128, 0:T], Q[:, c * T:(c + 1) * T])
            self.dma("gpsimd", self.sp_up[c * 128:(c + 1) * 128, 0:T], K[:, c * T:(c + 1) * T])
        self.dma_wait("sync", "gpsimd")
        VL = self.ARB[:, 0:8 * T]
        for tc in range(NT128):
            self.dma("sync", VL[:, tc * 1024:(tc + 1) * 1024],
                     self.sp_g2[tc * 128:(tc + 1) * 128, 0:1024])
        self.dma_wait("tensor", "sync")
        self.dma_wait("vector", "sync")

        QSTG = WIN[:, 0:T]
        KSTG = WIN[:, T:2 * T]
        JNK = WIN[:, 2 * T:3 * T]
        PTT = WIN[:, 3 * T:3 * T + 512]
        OUTTM = WIN[:, 3 * T + 512:3 * T + 512 + 64]
        NQT = NT128
        GSZ = min(8, NQT)
        lo = self.STATS[:, 0:16]
        hi = self.STATS[:, 16:32]
        mid = self.STATS[:, 32:48]
        cnt = self.STATS[:, 48:64]
        dirm = self.STATS[:, 64:80]
        rsum = self.STATS[:, 80:96]
        NITER = 10
        for h in range(16):
            self.fence("sync", "tensor")
            self.dma("sync", QSTG[0:64, :], self.sp_gate[h * 64:h * 64 + 64, 0:T])
            self.dma("sync", KSTG[0:64, :], self.sp_up[h * 64:h * 64 + 64, 0:T])
            self.dma_wait("tensor", "sync")
            self.dma_wait("vector", "sync")
            self.dma_wait("scalar", "sync")
            for grp in range(NQT // GSZ):
                qts = list(range(grp * GSZ, grp * GSZ + GSZ))
                if h == 0:
                    for qi, qt in enumerate(qts):
                        kvalid = 128 * (qt + 1)
                        if kvalid < T:
                            self.op("vector", lambda v, qi=qi, qt=qt, kvalid=kvalid: v.memset(
                                PB[:, qi * T + kvalid:(qi + 1) * T], 0.0))
                            self.drain("vector")
                for qi, qt in enumerate(qts):
                    kvalid = 128 * (qt + 1)
                    for ks in range(0, kvalid, 512):
                        kw = min(512, kvalid - ks)
                        pbk = self.pb[(ks // 512) % 2]
                        self.fence("tensor", "scalar")
                        ev = self.op("tensor", lambda t, qt=qt, ks=ks, kw=kw, pbk=pbk: t.matmul(
                            pbk[:, 0:kw],
                            QSTG[0:64, qt * 128:qt * 128 + 128],
                            KSTG[0:64, ks:ks + kw],
                            start=True, stop=True), inc=self.sT)
                        self.wait("vector", ev)
                        self.wait("scalar", ev)
                        if ks + kw == kvalid:
                            self.op("vector", lambda v, kw=kw, pbk=pbk: v.tensor_tensor(
                                out=pbk[:, kw - 128:kw], in0=pbk[:, kw - 128:kw],
                                in1=self.TRIADD[:, :], op=OP.add), inc=self.sV)
                            self.drain("vector")
                            self.wait("scalar", (self.sV, self.sV.n))
                        evx = self.op("scalar", lambda s, qi=qi, ks=ks, kw=kw, pbk=pbk: s.activation(
                            out=PB[:, qi * T + ks:qi * T + ks + kw], in_=pbk[:, 0:kw],
                            func=AF.Exp), inc=self.sS)
                        self.wait("tensor", evx)
                self.fence("vector", "scalar")
                self.op("vector", lambda v: v.memset(lo[:, 0:GSZ], 1e-18))
                self.drain("vector")
                self.op("vector", lambda v: v.memset(hi[:, 0:GSZ], 1e18))
                self.drain("vector")
                for it in range(NITER):
                    self.op("vector", lambda v: v.tensor_tensor(
                        out=mid[:, 0:GSZ], in0=lo[:, 0:GSZ], in1=hi[:, 0:GSZ], op=OP.mult))
                    self.drain("vector")
                    ev = self.op("vector", lambda v: v.nop(), inc=self.sV)
                    self.wait("scalar", ev)
                    ev = self.op("scalar", lambda s: s.activation(
                        out=mid[:, 0:GSZ], in_=mid[:, 0:GSZ], func=AF.Sqrt), inc=self.sS)
                    self.wait("vector", ev)
                    for qi, qt in enumerate(qts):
                        kvalid = 128 * (qt + 1)
                        self.op("vector", lambda v, qi=qi, kvalid=kvalid: v.tensor_scalar(
                            out=JNK[:, 0:kvalid], in0=PB[:, qi * T:qi * T + kvalid],
                            scalar1=mid[:, qi:qi + 1], scalar2=0.0,
                            op0=OP.is_ge, op1=OP.add, accum_out=cnt[:, qi:qi + 1]))
                        self.drain("vector")
                    self.op("vector", lambda v: v.tensor_scalar(
                        out=dirm[:, 0:GSZ], in0=cnt[:, 0:GSZ], scalar1=KTAR, scalar2=None,
                        op0=OP.is_ge))
                    self.drain("vector")
                    tsel = self.STATS[:, 100:116]
                    self.op("vector", lambda v: v.tensor_tensor(
                        out=tsel[:, 0:GSZ], in0=mid[:, 0:GSZ], in1=lo[:, 0:GSZ],
                        op=OP.subtract))
                    self.drain("vector")
                    self.op("vector", lambda v: v.tensor_tensor(
                        out=tsel[:, 0:GSZ], in0=tsel[:, 0:GSZ], in1=dirm[:, 0:GSZ],
                        op=OP.mult))
                    self.drain("vector")
                    self.op("vector", lambda v: v.tensor_tensor(
                        out=lo[:, 0:GSZ], in0=lo[:, 0:GSZ], in1=tsel[:, 0:GSZ],
                        op=OP.add))
                    self.drain("vector")
                    self.op("vector", lambda v: v.tensor_tensor(
                        out=tsel[:, 0:GSZ], in0=hi[:, 0:GSZ], in1=mid[:, 0:GSZ],
                        op=OP.subtract))
                    self.drain("vector")
                    self.op("vector", lambda v: v.tensor_tensor(
                        out=tsel[:, 0:GSZ], in0=tsel[:, 0:GSZ], in1=dirm[:, 0:GSZ],
                        op=OP.mult))
                    self.drain("vector")
                    self.op("vector", lambda v: v.tensor_tensor(
                        out=hi[:, 0:GSZ], in0=mid[:, 0:GSZ], in1=tsel[:, 0:GSZ],
                        op=OP.add))
                    self.drain("vector")
                for qi, qt in enumerate(qts):
                    kvalid = 128 * (qt + 1)
                    ev = self.op("vector", lambda v, qi=qi, kvalid=kvalid: v.scalar_tensor_tensor(
                        out=PB[:, qi * T:qi * T + kvalid], in0=PB[:, qi * T:qi * T + kvalid],
                        scalar=lo[:, qi:qi + 1], in1=PB[:, qi * T:qi * T + kvalid],
                        op0=OP.is_ge, op1=OP.mult, accum_out=rsum[:, qi:qi + 1]), inc=self.sV)
                    self.drain("vector")
                    self.wait("tensor", ev)
                    ppv = self.pb[2]
                    self.fence("tensor", "vector")
                    for kc in range(qt + 1):
                        pbt = self.pt0 if kc % 2 == 0 else self.pt1
                        ev = self.op("tensor", lambda t, qi=qi, kc=kc, pbt=pbt: t.transpose(
                            pbt[:, 0:128], PB[:, qi * T + kc * 128:qi * T + kc * 128 + 128],
                            self.ident), inc=self.sT)
                        self.wait("vector", ev)
                        evv = self.op("vector", lambda v, kc=kc, pbt=pbt: v.tensor_copy(
                            PTT[:, (kc % 4) * 128:(kc % 4) * 128 + 128], pbt[:, 0:128]),
                            inc=self.sV)
                        self.wait("tensor", evv)
                        self.op("tensor", lambda t, kc=kc, qt=qt, h=h, ppv=ppv: t.matmul(
                            ppv[:, 0:64],
                            PTT[:, (kc % 4) * 128:(kc % 4) * 128 + 128],
                            VL[:, kc * 1024 + h * 64:kc * 1024 + h * 64 + 64],
                            start=(kc == 0), stop=(kc == qt)),
                            inc=self.sT if kc == qt else None)
                    self.wait("vector", (self.sT, self.sT.n))
                    self.op("vector", lambda v, qi=qi: v.reciprocal(
                        out=self.STATS[:, 96:97], in_=rsum[:, qi:qi + 1]))
                    self.drain("vector")
                    ev = self.op("vector", lambda v, ppv=ppv: v.tensor_scalar(
                        out=OUTTM, in0=ppv[:, 0:64], scalar1=self.STATS[:, 96:97],
                        scalar2=None, op0=OP.mult), inc=self.sV)
                    self.drain("vector")
                    self.wait("tensor", ev)
                    evt = self.op("tensor", lambda t: t.transpose(
                        self.pt0[0:64, 0:128], OUTTM, self.ident), inc=self.sT)
                    self.wait("vector", evt)
                    evv = self.op("vector", lambda v: v.tensor_copy(
                        JNK[0:64, 0:128], self.pt0[0:64, 0:128]), inc=self.sV)
                    self.wait("gpsimd", evv)
                    self.dma("gpsimd",
                             self.sp_attn[h * 64:h * 64 + 64, qt * 128:qt * 128 + 128],
                             JNK[0:64, 0:128])
                    self.dma_wait("vector", "gpsimd")
        self.barrier(("tensor", "vector", "scalar", "gpsimd", "sync"))

        WSLOT = WIN
        ev = self.load_cast_w("L2projT", WSLOT, WSTG)
        self.wait("tensor", ev)
        TIN4 = self.ARB[:, 0:2048]
        for tc in range(NT):
            self.fence("tensor", "vector")
            lasts = {}
            for dc in range(FC):
                ti = TIN4[:, (dc % 2) * 1024:(dc % 2) * 1024 + 512]
                self.fence("sync", "tensor")
                self.dma("sync", ti, self.sp_attn[dc * 128:(dc + 1) * 128, tc * 512:(tc + 1) * 512])
                self.dma_wait("tensor", "sync")
                for cc in range(FC):
                    lasts[cc] = self.op("tensor", lambda t, dc=dc, cc=cc, ti=ti: t.matmul(
                        self.pb[cc][:, 0:512],
                        WSLOT[:, dc * 1024 + cc * 128:dc * 1024 + cc * 128 + 128],
                        ti, start=(dc == 0), stop=(dc == FC - 1)),
                        inc=self.sT if dc == FC - 1 else None)
            for cc in range(FC):
                self.wait("vector", lasts[cc])
                self.op("vector", lambda v, cc=cc, tc=tc: v.scalar_tensor_tensor(
                    out=self.X[:, cc * T + tc * 512: cc * T + tc * 512 + 512],
                    in0=self.pb[cc][:, 0:512], scalar=pbias[:, cc:cc + 1],
                    in1=self.X[:, cc * T + tc * 512: cc * T + tc * 512 + 512],
                    op0=OP.add, op1=OP.add), inc=self.sV)
                self.drain("vector")
        self.barrier()

    def l3_attn(self):
        T, NT, NT128, FC = self.T, self.NT, self.NT128, self.FC
        A = self.ARB[:, 0:8 * T]
        B = self.ARB[:, 8 * T:16 * T]
        WSLOT = self.ARB[:, 16 * T:16 * T + 8192]
        WSTG = self.ARF[:, 0:1024]
        MIXS = self.ARF.bitcast(BF16)[:, 2048:2048 + T]
        TH = T // 2
        sp_sr = self.sp_g2
        sp_rw = self.sp_o

        VCOL = self.STATS[:, 140:180]
        tmk, tmv, tmr = VCOL[:, 0:8], VCOL[:, 8:16], VCOL[:, 16:24]
        ewc, euc = VCOL[:, 24:32], VCOL[:, 32:40]
        self.dma("sync", tmk, self.dram["L3tmk"][:, :])
        self.dma("sync", tmv, self.dram["L3tmv"][:, :])
        self.dma("sync", tmr, self.dram["L3tmr"][:, :])
        self.dma("sync", ewc, self.dram["L3td"][:, :])
        self.dma("sync", euc, self.dram["L3tf"][:, :])
        self.dma_wait("scalar", "sync")
        self.dma_wait("vector", "sync")
        self.op("scalar", lambda s: s.activation(out=ewc, in_=ewc, func=AF.Exp))
        self.drain("scalar")
        self.op("scalar", lambda s: s.activation(out=ewc, in_=ewc, func=AF.Exp, scale=-1.0))
        self.drain("scalar")
        ev = self.op("scalar", lambda s: s.activation(out=euc, in_=euc, func=AF.Exp), inc=self.sS)
        self.wait("vector", ev)
        self.wait("tensor", ev)

        def mix(dstregion, tmcol):
            for c in range(FC):
                xn_c = self.XN[:, c * T:(c + 1) * T]
                dst_c = dstregion[:, c * T:(c + 1) * T]
                self.op("vector", lambda v, xn_c=xn_c: v.tensor_tensor(
                    out=MIXS[:, 1:T], in0=xn_c[:, 1:T], in1=xn_c[:, 0:T - 1],
                    op=OP.subtract))
                self.drain("vector")
                self.op("vector", lambda v, xn_c=xn_c, dst_c=dst_c, tmcol=tmcol, c=c: v.scalar_tensor_tensor(
                    out=dst_c[:, 1:T], in0=MIXS[:, 1:T], scalar=tmcol[:, c:c + 1],
                    in1=xn_c[:, 0:T - 1], op0=OP.mult, op1=OP.add))
                self.drain("vector")
                self.op("vector", lambda v, xn_c=xn_c, dst_c=dst_c, tmcol=tmcol, c=c: v.tensor_scalar(
                    out=dst_c[:, 0:1], in0=xn_c[:, 0:1], scalar1=tmcol[:, c:c + 1],
                    scalar2=None, op0=OP.mult))
                self.drain("vector")

        def project_from_A(wname, evac):
            ev = self.load_cast_w(wname, WSLOT, WSTG)
            self.wait("tensor", ev)
            for dc in range(FC):
                for tc in range(NT):
                    pbk = self.pb[(dc * NT + tc) % 2]
                    self.fence("tensor", "vector")
                    self.fence("tensor", "scalar")
                    last = None
                    for i in range(FC):
                        last = self.op("tensor", lambda t, k=i, i0=i, dc=dc, tc=tc, pbk=pbk: t.matmul(
                            pbk[:, 0:512],
                            WSLOT[:, k * 1024 + dc * 128:k * 1024 + dc * 128 + 128],
                            A[:, k * T + tc * 512:k * T + tc * 512 + 512],
                            start=(i0 == 0), stop=(i0 == FC - 1)),
                            inc=self.sT if i == FC - 1 else None)
                    evac(dc, tc, pbk, last)

        mix(A, tmr)
        self.barrier(("tensor", "vector", "sync"))

        def evac_sr(dc, tc, pbk, last):
            self.wait("scalar", last)
            st = MIXS[:, 0:512]
            self.dma_wait("scalar", "gpsimd")
            ev = self.op("scalar", lambda s, st=st, pbk=pbk: s.activation(
                out=st, in_=pbk[:, 0:512], func=AF.Sigmoid), inc=self.sS)
            self.wait("tensor", ev)
            self.wait("gpsimd", ev)
            self.dma("gpsimd", sp_sr[dc * 128:(dc + 1) * 128, tc * 512:(tc + 1) * 512], st)
        project_from_A("L3WrT", evac_sr)
        self.barrier(("tensor", "vector", "scalar", "gpsimd", "sync"))

        mix(A, tmk)
        self.barrier(("tensor", "vector", "sync"))

        def evac_ek(dc, tc, pbk, last):
            self.wait("scalar", last)
            ev = self.op("scalar", lambda s, dc=dc, tc=tc, pbk=pbk: s.activation(
                out=B[:, dc * T + tc * 512:dc * T + tc * 512 + 512],
                in_=pbk[:, 0:512], func=AF.Exp), inc=self.sS)
            self.wait("tensor", ev)
        project_from_A("L3WkT", evac_ek)
        self.barrier(("tensor", "vector", "scalar", "sync"))
        self.dump("l3_ek", B, 8 * T)
        self.dump("l3_xk", A, 8 * T)

        mix(A, tmv)
        self.barrier(("tensor", "vector", "sync"))
        ev = self.load_cast_w("L3WvT", WSLOT, WSTG)
        self.wait("tensor", ev)
        ekf = self.ARF[:, 0:TH]
        vf = self.ARF[:, TH:2 * TH]
        As = self.ARF[:, 2 * TH:3 * TH]
        Bs = self.ARF[:, 3 * TH:4 * TH]
        ewt = self.ARF[:, 4 * TH:5 * TH]
        STA = self.STATS[:, 104:120]
        SRT = self.ARF.bitcast(BF16)[:, 10 * TH:11 * TH]
        RWT = self.ARF.bitcast(BF16)[:, 11 * TH:12 * TH]
        self.op("vector", lambda v: v.memset(STA, 0.0))
        self.drain("vector")
        for dc in range(FC):
            self.op("vector", lambda v: v.memset(ewt, 1.0))
            self.drain("vector")
            self.op("vector", lambda v, dc=dc: v.tensor_scalar(
                out=ewt, in0=ewt, scalar1=ewc[:, dc:dc + 1], scalar2=None, op0=OP.mult),
                inc=self.sV)
            self.drain("vector")
            SW = min(512, TH)
            for th in range(2):
                self.fence("tensor", "vector")
                for sub in range(TH // SW):
                    pbk = self.pb[2 + sub % 2]
                    last = None
                    for i in range(FC):
                        last = self.op("tensor", lambda t, k=i, i0=i, dc=dc, th=th, sub=sub, pbk=pbk: t.matmul(
                            pbk[:, 0:SW],
                            WSLOT[:, k * 1024 + dc * 128:k * 1024 + dc * 128 + 128],
                            A[:, k * T + th * TH + sub * SW:k * T + th * TH + sub * SW + SW],
                            start=(i0 == 0), stop=(i0 == FC - 1)),
                            inc=self.sT if i == FC - 1 else None)
                    self.wait("vector", last)
                    self.op("vector", lambda v, sub=sub, pbk=pbk: v.tensor_copy(
                        vf[:, sub * SW:sub * SW + SW], pbk[:, 0:SW]))
                    self.drain("vector")
                self.op("vector", lambda v, dc=dc, th=th: v.tensor_copy(
                    ekf, B[:, dc * T + th * TH:dc * T + th * TH + TH]))
                self.drain("vector")
                self.op("vector", lambda v: v.tensor_tensor(
                    out=vf, in0=ekf, in1=vf, op=OP.mult))
                self.drain("vector")
                self.op("vector", lambda v, dc=dc: v.tensor_tensor_scan(
                    out=As, data0=ewt, data1=vf, initial=STA[:, 2 * dc:2 * dc + 1],
                    op0=OP.mult, op1=OP.add))
                self.drain("vector")
                self.op("vector", lambda v, dc=dc: v.tensor_tensor_scan(
                    out=Bs, data0=ewt, data1=ekf, initial=STA[:, 2 * dc + 1:2 * dc + 2],
                    op0=OP.mult, op1=OP.add))
                self.drain("vector")
                self.op("vector", lambda v, dc=dc: v.scalar_tensor_tensor(
                    out=vf[:, 0:1], in0=vf[:, 0:1], scalar=euc[:, dc:dc + 1],
                    in1=STA[:, 2 * dc:2 * dc + 1], op0=OP.mult, op1=OP.add))
                self.drain("vector")
                self.op("vector", lambda v, dc=dc: v.scalar_tensor_tensor(
                    out=vf[:, 1:TH], in0=vf[:, 1:TH], scalar=euc[:, dc:dc + 1],
                    in1=As[:, 0:TH - 1], op0=OP.mult, op1=OP.add))
                self.drain("vector")
                self.op("vector", lambda v, dc=dc: v.scalar_tensor_tensor(
                    out=ekf[:, 0:1], in0=ekf[:, 0:1], scalar=euc[:, dc:dc + 1],
                    in1=STA[:, 2 * dc + 1:2 * dc + 2], op0=OP.mult, op1=OP.add))
                self.drain("vector")
                self.op("vector", lambda v, dc=dc: v.scalar_tensor_tensor(
                    out=ekf[:, 1:TH], in0=ekf[:, 1:TH], scalar=euc[:, dc:dc + 1],
                    in1=Bs[:, 0:TH - 1], op0=OP.mult, op1=OP.add))
                self.drain("vector")
                self.op("vector", lambda v, dc=dc: v.tensor_copy(
                    STA[:, 2 * dc:2 * dc + 1], As[:, TH - 1:TH]))
                self.drain("vector")
                self.op("vector", lambda v, dc=dc: v.tensor_copy(
                    STA[:, 2 * dc + 1:2 * dc + 2], Bs[:, TH - 1:TH]))
                self.drain("vector")
                self.op("vector", lambda v: v.reciprocal(out=ekf, in_=ekf))
                self.drain("vector")
                self.op("vector", lambda v: v.tensor_tensor(
                    out=vf, in0=vf, in1=ekf, op=OP.mult))
                self.drain("vector")
                self.dma_wait("sync", "gpsimd")
                self.fence("sync", "vector")
                self.dma("sync", SRT, sp_sr[dc * 128:(dc + 1) * 128, th * TH:(th + 1) * TH])
                self.dma_wait("vector", "sync")
                self.op("vector", lambda v: v.tensor_tensor(
                    out=RWT, in0=vf, in1=SRT, op=OP.mult), inc=self.sV)
                self.drain("vector")
                evv = self.op("vector", lambda v: v.nop(), inc=self.sV)
                self.wait("gpsimd", evv)
                self.dma("gpsimd", sp_rw[dc * 128:(dc + 1) * 128, th * TH:(th + 1) * TH], RWT)
                self.dma_wait("vector", "gpsimd")
        self.barrier(("tensor", "vector", "scalar", "gpsimd", "sync"))

        ev = self.load_cast_w("L3WoT", WSLOT, WSTG)
        self.wait("tensor", ev)
        TIN3 = self.ARB[:, 0:2048]
        for tc in range(NT):
            self.fence("tensor", "vector")
            lasts = {}
            for dc in range(FC):
                ti = TIN3[:, (dc % 2) * 1024:(dc % 2) * 1024 + 512]
                self.fence("sync", "tensor")
                self.dma("sync", ti, sp_rw[dc * 128:(dc + 1) * 128, tc * 512:(tc + 1) * 512])
                self.dma_wait("tensor", "sync")
                for cc in range(FC):
                    lasts[cc] = self.op("tensor", lambda t, dc=dc, cc=cc, ti=ti: t.matmul(
                        self.pb[cc][:, 0:512],
                        WSLOT[:, dc * 1024 + cc * 128:dc * 1024 + cc * 128 + 128],
                        ti, start=(dc == 0), stop=(dc == FC - 1)),
                        inc=self.sT if dc == FC - 1 else None)
            for cc in range(FC):
                self.wait("vector", lasts[cc])
                self.op("vector", lambda v, cc=cc, tc=tc: v.tensor_tensor(
                    out=self.X[:, cc * T + tc * 512: cc * T + tc * 512 + 512],
                    in0=self.X[:, cc * T + tc * 512: cc * T + tc * 512 + 512],
                    in1=self.pb[cc][:, 0:512], op=OP.add), inc=self.sV)
                self.drain("vector")
        self.barrier()

    def emit_output(self):
        T = self.T
        self.barrier()
        for c in range(self.FC):
            self.dma("sync", self.yT[c * 128:(c + 1) * 128, :], self.X[:, c * T:(c + 1) * T])
        self.dma_wait("sync", "sync")
        self.dma_wait("gpsimd", "gpsimd")


def build_nc(T=2048, layers=LAYERS):
    import contextlib
    nc = bass.Bass("TRN2", target_bir_lowering=False)
    with contextlib.ExitStack() as stack:
        mk = MK(nc, T, layers, stack)
        mk.declare()
        mk.init_loads()
        if layers >= 1:
            mk.layernorm()
            mk.l0_attn()
            mk.qdq_xn()
            mk.qdq_ffn("L0wg", "L0wu", "L0wd",
                       (mk.sp_gate, mk.sp_up, mk.sp_g2, mk.sp_o))
        if layers >= 2:
            mk.layernorm()
            if "l1_attn" not in SKIP:
                mk.l1_attn()
            if "l1_ffn" not in SKIP:
                mk.l1_ffn()
        if layers >= 3:
            mk.layernorm()
            if "l2_ffn" not in SKIP:
                mk.l2_ffn()
            if "l2_attn" not in SKIP:
                mk.l2_attn()
        if layers >= 4:
            mk.layernorm()
            if "l3_attn" not in SKIP:
                mk.l3_attn()
            if "l3_ffn" not in SKIP:
                mk.qdq_xn()
                mk.qdq_ffn("L3wg", "L3wu", "L3wd",
                           (mk.sp_gate, mk.sp_up, mk.sp_g2, mk.sp_o))
        mk.emit_output()
        with nc.Block() as block:
            mk.replay(block)
    return nc



def make_consts(T):
    import ml_dtypes
    bf = ml_dtypes.bfloat16
    consts = {}
    consts["ident"] = np.eye(128, dtype=np.float32).astype(bf)
    consts["identf"] = np.eye(128, dtype=np.float32)
    consts["ones_bf"] = np.ones((128, 128), np.float32).astype(bf)
    consts["ones_f"] = np.ones((1, 128), np.float32)
    consts["triadd"] = np.where(np.tril(np.ones((128, 128), bool)), 0.0, -1e30).astype(np.float32)
    consts["trampf"] = np.broadcast_to(np.arange(T, dtype=np.float32), (128, T)).copy()

    t = np.arange(T, dtype=np.float64)

    def rope_tables(rot, head, base=10000.0):
        c = np.zeros((128, T)); s = np.zeros((128, T))
        inv = 1.0 / base ** (np.arange(0, rot, 2, dtype=np.float64) / rot)
        for r in range(128):
            l = r % head
            if l < rot:
                f = t * inv[l // 2]
                sgn = -1.0 if (l % 2 == 0) else 1.0
                c[r], s[r] = np.cos(f), sgn * np.sin(f)
            else:
                c[r], s[r] = 1.0, 0.0
        return np.stack([c, s]).astype(np.float32).astype(bf)

    consts["l1rope"] = rope_tables(32, 64)
    consts["l2rope"] = rope_tables(64, 64)
    return consts


def make_inmaps(inputs, T):
    ct = np.ascontiguousarray
    f32 = lambda a: np.asarray(a, dtype=np.float32)
    shared = {}
    shared["L0WqT"] = ct(f32(inputs["L0_attn_Wq"]).T)
    shared["L0WkT"] = ct(f32(inputs["L0_attn_Wk"]).T)
    shared["L0WvT"] = ct(f32(inputs["L0_attn_Wv"]).T)
    shared["L0WoT"] = ct(f32(inputs["L0_attn_Wo"]).T)
    shared["L0nrm"] = ct(f32(inputs["L0_attn_norm_const"]).reshape(1, 16))
    shared["L0wg"] = f32(inputs["L0_ffn_w_gate"])
    shared["L0wu"] = f32(inputs["L0_ffn_w_up"])
    shared["L0wd"] = f32(inputs["L0_ffn_w_down"])
    shared["L1Wq"] = f32(inputs["L1_attn_Wq"])
    shared["L1Wk"] = f32(inputs["L1_attn_Wk"])
    shared["L1Wv"] = f32(inputs["L1_attn_Wv"])
    shared["L1Wo"] = f32(inputs["L1_attn_Wo"])
    shared["L1bo"] = ct(f32(inputs["L1_attn_bo"]).reshape(8, 128).T)
    shared["L1wg"] = f32(inputs["L1_ffn_w_gate"])
    shared["L1wu"] = f32(inputs["L1_ffn_w_up"])
    shared["L1wd"] = f32(inputs["L1_ffn_w_down"])
    shared["L2qkvT"] = ct(f32(inputs["L2_attn_qkv_w"]).T)
    shared["L2projT"] = ct(f32(inputs["L2_attn_proj_w"]).T)
    shared["L2pb"] = ct(f32(inputs["L2_attn_proj_b"]).reshape(8, 128).T)
    shared["L2ker"] = np.concatenate([
        f32(inputs["L2_attn_kernel_q"]).ravel(),
        f32(inputs["L2_attn_kernel_k"]).ravel(),
        f32(inputs["L2_attn_kernel_v"]).ravel()]).reshape(1, 12)
    shared["L2gam"] = ct(f32(inputs["L2_attn_gamma_freq"]).reshape(8, 128).T)
    shared["L2phs"] = ct(f32(inputs["L2_attn_phase"]).reshape(8, 128).T)
    shared["L2thf"] = f32(inputs["L2_attn_theta_freq"]).reshape(1, 1)
    shared["L2th"] = f32(inputs["L2_ffn_theta"])
    shared["L2ph"] = f32(inputs["L2_ffn_phi"])
    shared["L2thn"] = f32(inputs["L2_ffn_theta_noise"])
    shared["L2phn"] = f32(inputs["L2_ffn_phi_noise"])
    shared["L3td"] = ct(f32(inputs["L3_attn_time_decay"]).reshape(8, 128).T)
    shared["L3tf"] = ct(f32(inputs["L3_attn_time_first"]).reshape(8, 128).T)
    shared["L3tmk"] = ct(f32(inputs["L3_attn_time_mix_k"]).reshape(8, 128).T)
    shared["L3tmv"] = ct(f32(inputs["L3_attn_time_mix_v"]).reshape(8, 128).T)
    shared["L3tmr"] = ct(f32(inputs["L3_attn_time_mix_r"]).reshape(8, 128).T)
    shared["L3WkT"] = ct(f32(inputs["L3_attn_Wk"]).T)
    shared["L3WvT"] = ct(f32(inputs["L3_attn_Wv"]).T)
    shared["L3WrT"] = ct(f32(inputs["L3_attn_Wr"]).T)
    shared["L3WoT"] = ct(f32(inputs["L3_attn_Wo"]).T)
    shared["L3wg"] = f32(inputs["L3_ffn_w_gate"])
    shared["L3wu"] = f32(inputs["L3_ffn_w_up"])
    shared["L3wd"] = f32(inputs["L3_ffn_w_down"])
    shared.update(make_consts(T))
    shared = {k: np.ascontiguousarray(v) for k, v in shared.items()}

    x = np.asarray(inputs["L0_attn_x"])
    in_maps = []
    for c in range(NCORES):
        m = dict(shared)
        m["xT"] = np.ascontiguousarray(f32(x[c]).T)
        in_maps.append(m)
    return in_maps


def kernel(**inputs):
    T = int(np.asarray(inputs["L0_attn_x"]).shape[1])
    in_maps = make_inmaps(inputs, T)
    nc = build_nc(T=T, layers=LAYERS)
    res = run_bass_kernel_spmd(nc, in_maps, core_ids=list(range(NCORES)))
    outs = [np.ascontiguousarray(np.asarray(r["yT"]).T) for r in res.results]
    return np.stack(outs).astype(np.float32)
